# version 05
# baseline (speedup 1.0000x reference)
import numpy as np
import ml_dtypes
from contextlib import ExitStack

import concourse.bass as bass
import concourse.mybir as mybir
from concourse.bass_utils import run_bass_kernel_spmd

F32 = mybir.dt.float32
BF16 = mybir.dt.bfloat16
AF = mybir.ActivationFunctionType
ALU = mybir.AluOpType
AX = mybir.AxisListType
BF = ml_dtypes.bfloat16

S = 2048
D = 2048
NT = S // 128
NK = D // 128
NB = S // 512
SC0 = 1.0 / float(np.sqrt(2048.0))
SC1 = 1.0 / float(np.sqrt(128.0))
L1SHIFT = -4.0
NEG = -1.0e30

ENGINES = ("sync", "vector", "scalar", "tensor", "gpsimd")
DMA_ENGS = ("sync", "scalar", "gpsimd")

SCRATCH = {
    "xn":     ([S, D], F32, "p1"),
    "zfm":    ([NK, 128, S], BF16, "p1"),
    "xnfm":   ([NK, 128, S], BF16, "p1"),
    "masks":  ([NT, 128, 8], F32, "p1"),
    "score":  ([1, S], F32, "p1"),
    "qfm":    ([NK, 128, S], BF16, "p2"),
    "kfm":    ([NK, 128, S], BF16, "p2"),
    "vfm":    ([NK, 128, S], BF16, "p2"),
    "vo":     ([NT, 128, D], BF16, "p3"),
    "x1":     ([S, D], F32, "p4"),
    "xn2fm":  ([NK, 128, S], BF16, "p6"),
    "kfull":  ([16, 128, S], BF16, "p7"),
    "qfull":  ([16, 128, S], BF16, "p7"),
    "v1tm":   ([NT, 128, D], BF16, "p7"),
    "outfm":  ([16, 128, S], BF16, "p8"),
    "p2t":    ([S, D], F32, "p9"),
    "w0b":    ([8192, D], BF16, "p7"),
    "w1b":    ([D, 8192], BF16, "p7"),
    "dfm":    ([64, 128, S], BF16, "p10"),
}
MUTATES = {"p5": ["x1"]}

INPUTS = {
    "x0": ([S, D], F32),
    "L0_attn_qkv": ([D, 3 * D], F32),
    "L0_attn_o_proj": ([D, D], F32),
    "cgamma": ([128, D], F32),
    "cbeta": ([128, D], F32),
    "L0_ffn_router_w": ([D, 8], F32),
    "L0_ffn_gate_w": ([8, D, D], F32),
    "L0_ffn_up_w": ([8, D, D], F32),
    "L0_ffn_down_w": ([8, D, D], F32),
    "L0_ffn_shared_gate_w": ([D, D], F32),
    "L0_ffn_shared_up_w": ([D, D], F32),
    "L0_ffn_shared_down_w": ([D, D], F32),
    "L1_attn_W_dkv": ([D, 512], F32),
    "L1_attn_b_dkv": ([1, 512], F32),
    "L1_attn_W_dq": ([D, 512], F32),
    "L1_attn_b_dq": ([1, 512], F32),
    "L1_attn_W_uk": ([512, 1536], F32),
    "L1_attn_b_uk": ([1, 1536], F32),
    "L1_attn_W_uv": ([512, D], F32),
    "L1_attn_b_uv": ([1, D], F32),
    "L1_attn_W_uq": ([512, 1536], F32),
    "L1_attn_b_uq": ([1, 1536], F32),
    "L1_attn_W_qr": ([512, 512], F32),
    "L1_attn_b_qr": ([1, 512], F32),
    "L1_attn_W_kr": ([D, 512], F32),
    "L1_attn_b_kr": ([1, 512], F32),
    "L1_attn_W_o": ([D, D], F32),
    "L1_attn_b_o": ([1, D], F32),
    "L1_ffn_W0": ([8192, D], F32),
    "L1_ffn_bias0": ([1, 8192], F32),
    "L1_ffn_W1": ([D, 8192], F32),
    "L1_ffn_bias1": ([1, D], F32),
    "cident": ([128, 128], BF16),
    "cones": ([1, 512], BF16),
    "cones32": ([1, 512], F32),
    "conescol": ([128, 1], BF16),
    "cconsts": ([128, 8], F32),
    "cmask": ([128, 4 * 512], BF16),
    "ccos": ([S, 512], F32),
    "csin": ([S, 512], F32),
}
C_SC0, C_SC1, C_L1SHIFT, C_EPS, C_NHALF, C_ONE = 0, 1, 2, 3, 4, 5

PHASE_INPUTS = {
    "p1": ["x0", "L0_ffn_router_w", "cgamma", "cbeta"],
    "p2": ["L0_attn_qkv"],
    "p3": ["L0_attn_o_proj"],
    "p4": ["x0"],
    "p5": ["L0_ffn_gate_w", "L0_ffn_up_w", "L0_ffn_down_w",
           "L0_ffn_shared_gate_w", "L0_ffn_shared_up_w", "L0_ffn_shared_down_w"],
    "p6": [],
    "p7": ["L1_attn_W_dkv", "L1_attn_b_dkv", "L1_attn_W_dq", "L1_attn_b_dq",
           "L1_attn_W_uk", "L1_attn_b_uk", "L1_attn_W_uv", "L1_attn_b_uv",
           "L1_attn_W_uq", "L1_attn_b_uq", "L1_attn_W_qr", "L1_attn_b_qr",
           "L1_attn_W_kr", "L1_attn_b_kr", "ccos", "csin",
           "L1_ffn_W0", "L1_ffn_W1"],
    "p8": [],
    "p9": ["L1_attn_W_o", "L1_attn_b_o"],
    "p10": ["L1_ffn_bias0", "L1_ffn_bias1"],
}
CONST_INPUTS = ["cident", "cones", "cones32", "conescol", "cconsts", "cmask"]
PHASE_SCRATCH_READS = {
    "p1": [],
    "p2": ["zfm"],
    "p3": ["vfm"],
    "p4": ["qfm", "kfm", "vo", "xn"],
    "p5": ["xnfm", "masks", "score"],
    "p6": ["x1"],
    "p7": ["xn2fm"],
    "p8": ["kfull", "qfull", "v1tm"],
    "p9": ["outfm", "x1"],
    "p10": ["xn2fm", "p2t", "w0b", "w1b"],
}
ALL_PHASES = ["p1", "p2", "p3", "p4", "p5", "p6", "p7", "p8", "p9", "p10"]
PHASE_ARENAS = {"p2": ["A"], "p4": ["A", "B"], "p5": ["A", "B"],
                "p7": ["A", "B"], "p8": ["A", "B"], "p9": ["B"], "p10": ["A", "B"]}
PHASE_SCRATCH_WRITES = {p: [n for n, v in SCRATCH.items() if v[2] == p]
                        for p in ALL_PHASES}


class Sem:
    __slots__ = ("h", "n", "name")


class Buf:
    __slots__ = ("ap", "wr", "rd", "name")

    def __init__(self, ap, name=""):
        self.ap = ap
        self.wr = []
        self.rd = []
        self.name = name

    def __getitem__(self, sl):
        return self.ap[sl]


class Ctx:
    def __init__(self, nc, stack):
        self.nc = nc
        self.stack = stack
        self.ops = {e: [] for e in ENGINES}
        self.waited = {e: {} for e in ENGINES}
        self.semlist = []
        self.nops = 0
        self.phase_guard = []
        self.hbufs = {}

    def sem(self, name):
        s = Sem()
        s.h = self.stack.enter_context(self.nc.semaphore(name))
        s.n = 0
        s.name = name
        self.semlist.append(s)
        return s

    def _resolve(self, eng, waits):
        ws = []
        for (s, v) in waits:
            if v <= 0:
                continue
            prev = self.waited[eng].get(id(s), 0)
            if v > prev:
                ws.append((s.h, v))
                self.waited[eng][id(s)] = v
        return ws

    def op(self, eng, f, ins=(), outs=(), partial=False, extra=(), sem=None,
           inc_by=None):
        if sem is None:
            sem = self.esem[eng]
            inc_by = 1 if inc_by is None else inc_by
        waits = list(extra)
        for b in ins:
            waits += b.wr
        for b in outs:
            waits += b.rd
            if not partial:
                waits += b.wr
        ws = self._resolve(eng, waits)
        sem.n += inc_by
        tgt = (sem, sem.n)
        for b in ins:
            b.rd.append(tgt)
            if len(b.rd) > 400:
                del b.rd[0:len(b.rd) - 400]
        for b in outs:
            if partial:
                b.wr.append(tgt)
                if len(b.wr) > 400:
                    del b.wr[0:len(b.wr) - 400]
            else:
                b.wr = [tgt]
                b.rd = []
        ops = self.ops[eng]
        self.nops += 1

        def run(e, f=f, ws=tuple(ws), h=sem.h, inc_by=inc_by):
            for (hh, v) in ws:
                e.wait_ge(hh, v)
            f(e).then_inc(h, inc_by)
        ops.append(run)
        return tgt

    def _hb(self, ap):
        t = getattr(ap, "tensor", None)
        nm = getattr(t, "name", None)
        return self.hbufs.get(nm)

    def dma(self, eng, out, in_, ins=(), outs=(), partial=False, extra=(), **kw):
        hi = self._hb(in_)
        ho = self._hb(out)
        if hi is not None:
            ins = tuple(ins) + (hi,)
        if ho is not None:
            outs = tuple(outs) + (ho,)
            partial = True
        return self.op(eng, lambda e: e.dma_start(out=out, in_=in_, **kw),
                       ins=ins, outs=outs, partial=partial, extra=extra,
                       sem=self.qsem[eng], inc_by=16)

    def dmat(self, eng, out, in_, ins=(), outs=(), partial=False, extra=()):
        hi = self._hb(in_)
        if hi is not None:
            ins = tuple(ins) + (hi,)
        return self.op(eng, lambda e: e.dma_start_transpose(out, in_),
                       ins=ins, outs=outs, partial=partial, extra=extra,
                       sem=self.qsem[eng], inc_by=16)

    def waitall(self, eng, targets):
        ws = self._resolve(eng, targets)
        if not ws:
            return
        self.nops += 1

        def run(e, ws=tuple(ws)):
            for (hh, v) in ws:
                e.wait_ge(hh, v)
        self.ops[eng].append(run)

    def barrier(self):
        targets = [(s, s.n) for s in self.semlist if s.n > 0]
        for e in ENGINES:
            self.waitall(e, targets)

    def vbuf(self, ap, name=""):
        b = Buf(ap, name)
        b.wr = list(self.phase_guard)
        return b

    def buf(self, name, shape, dtype, stack=None):
        st = stack if stack is not None else self.stack
        h = st.enter_context(self.nc.sbuf_tensor(name, shape, dtype))
        b = Buf(h[:], name)
        b.wr = list(self.phase_guard)
        return b

    def pbuf(self, name, shape, dtype=F32, stack=None):
        st = stack if stack is not None else self.stack
        h = st.enter_context(self.nc.psum_tensor(name, shape, dtype))
        b = Buf(h[:], name)
        b.wr = list(self.phase_guard)
        return b


def mm(k, out_psum, lhsT, rhs, start, stop, ins, partial_out=False, out_ap=None):
    oap = out_psum.ap if out_ap is None else out_ap
    return k.op("tensor", lambda t: t.matmul(oap, lhsT, rhs, start=start, stop=stop),
                ins=ins, outs=(out_psum,),
                partial=(not start) or partial_out)


class Rot:
    def __init__(self, bufs):
        self.bufs = bufs
        self.i = 0

    def next(self):
        b = self.bufs[self.i % len(self.bufs)]
        self.i += 1
        return b


def phase1(k, io):
    nc = k.nc
    cc = k.r0["cconsts"]
    eps = cc.ap[:, C_EPS:C_EPS + 1]
    nhalf = cc.ap[:, C_NHALF:C_NHALF + 1]
    with ExitStack() as ps:
        xb = Rot([k.buf(f"p1_x{j}", [128, D], F32, ps) for j in range(2)])
        xnb_ = Rot([k.buf(f"p1_xn{j}", [128, D], F32, ps) for j in range(2)])
        z16 = k.buf("p1_z16", [128, D], BF16, ps)
        sq = z16
        zg = z16
        hi16 = k.buf("p1_hi16", [128, D], BF16, ps)
        lo16 = k.buf("p1_lo16", [128, D], BF16, ps)
        zst = k.buf("p1_zst", [128, D], BF16, ps)
        hist = k.buf("p1_hist", [128, D], BF16, ps)
        lost = k.buf("p1_lost", [128, D], BF16, ps)
        gb = k.buf("p1_gb", [128, D], BF16, ps)
        bb = k.buf("p1_bb", [128, D], BF16, ps)
        rwhi = k.buf("p1_rwhi", [128, 8 * NK], BF16, ps)
        rwlo = k.buf("p1_rwlo", [128, 8 * NK], BF16, ps)
        logit = k.buf("p1_logit", [128, 8], F32, ps)
        m8 = k.buf("p1_m8", [128, 8], F32, ps)
        scol = k.buf("p1_scol", [128, 16], F32, ps)
        srow = k.buf("p1_srow", [16, 128], F32, ps)
        pstr = Rot([k.pbuf(f"p1_tr{j}", [128, 128], BF16, ps) for j in range(4)])
        ps_r = k.pbuf("p1_psr", [128, 8], F32, ps)
        ps_sr = k.pbuf("p1_pssr", [16, 128], F32, ps)
        st = k.statbuf

        def stat(c):
            return st.ap[:, c:c + 1]

        stats = {c: k.vbuf(st.ap[:, c:c + 1], f"st{c}") for c in range(12)}

        with ExitStack() as ips:
            g32 = xb.bufs[0]
            rw32 = k.buf("p1_rw32", [128, 8 * NK], F32, ips)
            k.dma("sync", g32.ap, io["cgamma"], outs=(g32,))
            k.op("vector", lambda v: v.tensor_copy(gb.ap, g32.ap), ins=(g32,),
                 outs=(gb,))
            k.dma("sync", g32.ap, io["cbeta"], outs=(g32,))
            k.op("vector", lambda v: v.tensor_copy(bb.ap, g32.ap), ins=(g32,),
                 outs=(bb,))
            k.dma("sync", rw32.ap.rearrange("p (a c) -> p a c", a=NK),
                  io["L0_ffn_router_w"].rearrange("(a b) c -> b a c", b=128),
                  outs=(rw32,))
            k.op("vector", lambda v: v.tensor_copy(rwhi.ap, rw32.ap), ins=(rw32,),
                 outs=(rwhi,))
            k.op("vector", lambda v: v.tensor_sub(rwlo.ap, rw32.ap, rwhi.ap),
                 ins=(rw32, rwhi), outs=(rwlo,))
            k.barrier()

        def ln_stats(buf, c_nm, c_rstd, c_b2):
            k.op("vector", lambda v: v.reduce_sum(stat(c_nm), buf.ap, AX.X,
                                                  negate=True),
                 ins=(buf,), outs=(stats[c_nm],))
            k.op("vector", lambda v: v.tensor_scalar_mul(stat(c_nm), stat(c_nm),
                                                         1.0 / D),
                 ins=(stats[c_nm],), outs=(stats[c_nm],))
            k.op("scalar", lambda s: s.activation(sq.ap, buf.ap, AF.Square,
                                                  bias=stat(c_nm),
                                                  accum_out=stat(c_rstd)),
                 ins=(buf, stats[c_nm]), outs=(sq, stats[c_rstd]))
            k.op("scalar", lambda s: s.activation(stat(c_rstd), stat(c_rstd), AF.Ln,
                                                  scale=1.0 / D, bias=eps),
                 ins=(stats[c_rstd],), outs=(stats[c_rstd],))
            k.op("scalar", lambda s: s.activation(stat(c_rstd), stat(c_rstd), AF.Exp,
                                                  scale=nhalf),
                 ins=(stats[c_rstd],), outs=(stats[c_rstd],))
            k.op("vector", lambda v: v.tensor_tensor(stat(c_b2), stat(c_nm),
                                                     stat(c_rstd), ALU.mult),
                 ins=(stats[c_nm], stats[c_rstd]), outs=(stats[c_b2],))

        def transpose_to(src, stage):
            for kk in range(NK):
                pb = pstr.next()
                k.op("tensor", lambda t, pb=pb, kk=kk: t.transpose(
                    pb.ap, src[0:128, kk * 128:(kk + 1) * 128], k.r0["cident"].ap),
                    ins=(src, k.r0["cident"]), outs=(pb,))
                eng = "vector" if (kk % 2 == 0) else "scalar"
                if eng == "vector":
                    k.op(eng, lambda v, pb=pb, kk=kk: v.tensor_copy(
                        stage[0:128, kk * 128:(kk + 1) * 128], pb.ap),
                        ins=(pb,), outs=(stage,), partial=True)
                else:
                    k.op(eng, lambda s, pb=pb, kk=kk: s.copy(
                        stage[0:128, kk * 128:(kk + 1) * 128], pb.ap),
                        ins=(pb,), outs=(stage,), partial=True)

        for i in range(NT):
            x = xb.next()
            xn = xnb_.next()
            k.dma("sync", x.ap, io["x0"][i * 128:(i + 1) * 128, :], outs=(x,))
            ln_stats(x, 0, 1, 2)
            k.op("vector", lambda v, x=x, xn=xn: v.tensor_scalar(
                xn.ap, x.ap, stat(0), stat(1), ALU.add, ALU.mult),
                ins=(x, stats[0], stats[1]), outs=(xn,))
            k.dma("sync", io["xn"][i * 128:(i + 1) * 128, :], xn.ap, ins=(xn,))
            k.op("vector", lambda v, xn=xn: v.tensor_copy(hi16.ap, xn.ap),
                 ins=(xn,), outs=(hi16,))
            k.op("vector", lambda v, xn=xn: v.tensor_sub(lo16.ap, xn.ap, hi16.ap),
                 ins=(xn, hi16), outs=(lo16,))
            ln_stats(xn, 4, 5, 6)
            k.op("vector", lambda v, xn=xn: v.tensor_scalar(
                z16.ap, xn.ap, stat(4), stat(5), ALU.add, ALU.mult),
                ins=(xn, stats[4], stats[5]), outs=(z16,))
            k.op("vector", lambda v: v.tensor_mul(zg.ap, z16.ap, gb.ap),
                 ins=(z16, gb), outs=(zg,), partial=True)
            k.op("vector", lambda v: v.tensor_add(zg.ap, zg.ap, bb.ap),
                 ins=(zg, bb), outs=(zg,), partial=True)
            transpose_to(zg, zst)
            transpose_to(hi16, hist)
            transpose_to(lo16, lost)
            k.dma("sync",
                  io["zfm"][:, :, i * 128:(i + 1) * 128].rearrange("a p c -> p a c"),
                  zst.ap.rearrange("p (a c) -> p a c", a=NK), ins=(zst,))
            k.dma("sync",
                  io["xnfm"][:, :, i * 128:(i + 1) * 128].rearrange("a p c -> p a c"),
                  hist.ap.rearrange("p (a c) -> p a c", a=NK), ins=(hist,))
            for kk in range(NK):
                mm(k, ps_r, hist[0:128, kk * 128:(kk + 1) * 128],
                   rwhi[0:128, kk * 8:(kk + 1) * 8],
                   start=(kk == 0), stop=False, ins=(hist, rwhi))
            for kk in range(NK):
                mm(k, ps_r, hist[0:128, kk * 128:(kk + 1) * 128],
                   rwlo[0:128, kk * 8:(kk + 1) * 8], start=False, stop=False,
                   ins=(hist, rwlo))
            for kk in range(NK):
                mm(k, ps_r, lost[0:128, kk * 128:(kk + 1) * 128],
                   rwhi[0:128, kk * 8:(kk + 1) * 8], start=False,
                   stop=(kk == NK - 1), ins=(lost, rwhi))
            k.op("vector", lambda v: v.tensor_copy(logit.ap, ps_r.ap),
                 ins=(ps_r,), outs=(logit,))
            k.op("vector", lambda v: v.reduce_max(stat(8), logit.ap, AX.X),
                 ins=(logit,), outs=(stats[8],))
            for ee in range(8):
                k.op("vector", lambda v, ee=ee: v.tensor_scalar(
                    m8.ap[:, ee:ee + 1], logit.ap[:, ee:ee + 1], stat(8), None,
                    ALU.is_equal), ins=(logit, stats[8]), outs=(m8,), partial=(ee > 0))
            k.dma("sync", io["masks"][i], m8.ap, ins=(m8,))
            k.op("scalar", lambda s, i=i: s.activation(
                scol.ap[:, i:i + 1], stat(8), AF.Sigmoid),
                ins=(stats[8],), outs=(scol,), partial=(i > 0))
        k.op("tensor", lambda t: t.transpose(ps_sr.ap, scol.ap, k.r0["cident32"].ap),
             ins=(scol, k.r0["cident32"]), outs=(ps_sr,))
        k.op("vector", lambda v: v.tensor_copy(srow.ap, ps_sr.ap), ins=(ps_sr,),
             outs=(srow,))
        k.dma("sync", io["score"].rearrange("o (a c) -> (o a) c", a=16), srow.ap,
              ins=(srow,))


def phase2(k, io):
    with ExitStack() as ps:
        wb = Rot([k.buf(f"p2_w{j}", [128, NK * 128], BF16, ps) for j in range(3)])
        ost = Rot([k.buf(f"p2_o{j}", [128, S], BF16, ps) for j in range(2)])
        pso = Rot([k.pbuf(f"p2_ps{j}", [128, S], F32, ps) for j in range(2)])
        zA = k.Abuf
        for t in range(NK):
            k.dma("sync", k.A.ap[:, t * S:(t + 1) * S], io["zfm"][t],
                  outs=(zA,), partial=(t > 0))
        qkv = io["L0_attn_qkv"]
        for name, col in (("qfm", 0), ("kfm", 1), ("vfm", 2)):
            for mc in range(NK):
                w = wb.next()
                for hh in range(2):
                    k.dma("gpsimd",
                          w.ap[:, hh * 8 * 128:(hh + 1) * 8 * 128].rearrange(
                              "p (a c) -> p a c", a=8),
                          qkv[hh * 1024:(hh + 1) * 1024,
                              col * D + mc * 128: col * D + (mc + 1) * 128].rearrange(
                              "(a b) c -> b a c", b=128),
                          outs=(w,), partial=(hh > 0))
                po = pso.next()
                o = ost.next()
                for tb in range(NB):
                    for kk in range(NK):
                        mm(k, po, w[0:128, kk * 128:(kk + 1) * 128],
                           k.A.ap[:, kk * S + tb * 512: kk * S + (tb + 1) * 512],
                           start=(kk == 0), stop=(kk == NK - 1),
                           ins=(w, zA), partial_out=(tb > 0),
                           out_ap=po.ap[:, tb * 512:(tb + 1) * 512])
                for tb in range(NB):
                    k.op("vector", lambda v, o=o, po=po, tb=tb: v.tensor_copy(
                        o.ap[:, tb * 512:(tb + 1) * 512],
                        po.ap[:, tb * 512:(tb + 1) * 512]),
                        ins=(po,), outs=(o,), partial=(tb > 0))
                k.dma("sync", io[name][mc], o.ap, ins=(o,))


def phase3(k, io):
    with ExitStack() as ps:
        ob = k.buf("p3_ob", [128, NK * 512], BF16, ps)
        vb = Rot([k.buf(f"p3_v{j}", [128, NK * 128], BF16, ps) for j in range(2)])
        ost = Rot([k.buf(f"p3_o{j}", [128, 512], BF16, ps) for j in range(2)])
        pso = Rot([k.pbuf(f"p3_ps{j}", [128, 512], F32, ps) for j in range(2)])
        op_ = io["L0_attn_o_proj"]
        for db in range(NB):
            for kk in range(NK):
                k.dma("gpsimd", ob.ap[:, kk * 512:(kk + 1) * 512],
                      op_[kk * 128:(kk + 1) * 128, db * 512:(db + 1) * 512],
                      outs=(ob,), partial=(kk > 0))
            for tc in range(NT):
                v = vb.next()
                k.dma("sync", v.ap.rearrange("p (a c) -> p a c", a=NK),
                      io["vfm"][:, :, tc * 128:(tc + 1) * 128].rearrange(
                          "a b c -> b a c"), outs=(v,))
                po = pso.next()
                o = ost.next()
                for kk in range(NK):
                    mm(k, po, v[0:128, kk * 128:(kk + 1) * 128],
                       ob.ap[:, kk * 512:(kk + 1) * 512],
                       start=(kk == 0), stop=(kk == NK - 1), ins=(v, ob))
                k.op("vector", lambda vv, o=o, po=po: vv.tensor_copy(o.ap, po.ap),
                     ins=(po,), outs=(o,))
                k.dma("sync", io["vo"][tc, :, db * 512:(db + 1) * 512], o.ap,
                      ins=(o,))




def phase4(k, io):
    cc = k.r0["cconsts"]
    with ExitStack() as ps:
        qt = Rot([k.buf(f"p4_q{j}", [128, NK * 128], BF16, ps) for j in range(2)])
        P16 = Rot([k.buf(f"p4_p{j}", [128, S], BF16, ps) for j in range(2)])
        PT = Rot([k.buf(f"p4_pt{j}", [128, NK * 128], BF16, ps) for j in range(2)])
        xbl = Rot([k.buf(f"p4_xb{j}", [128, 512], F32, ps) for j in range(4)])
        xnbl = Rot([k.buf(f"p4_xnb{j}", [128, 512], F32, ps) for j in range(4)])
        obl = Rot([k.buf(f"p4_ob{j}", [128, 512], F32, ps) for j in range(4)])
        ps_sc = k.pbuf("p4_sc", [128, S], F32, ps)
        ps_av = k.pbuf("p4_av", [128, 1024], F32, ps)
        pstr = Rot([k.pbuf(f"p4_tr{j}", [128, 128], BF16, ps) for j in range(2)])
        st = k.statbuf

        def stat(c):
            return st.ap[:, c:c + 1]

        stats = {c: k.vbuf(st.ap[:, c:c + 1], f"st{c}") for c in range(12)}

        for t in range(NT):
            k.dma("sync", k.A.ap[:, t * S:(t + 1) * S], io["vo"][t],
                  outs=(k.A,), partial=(t > 0))
        for t in range(NK):
            k.dma("scalar", k.B.ap[:, t * S:(t + 1) * S], io["kfm"][t],
                  outs=(k.B,), partial=(t > 0))

        state = {}

        def softmax_part(i):
            nblk = i // 4 + 1
            W = 512 * nblk
            q = qt.next()
            k.dma("sync", q.ap.rearrange("p (a c) -> p a c", a=NK),
                  io["qfm"][:, :, i * 128:(i + 1) * 128].rearrange("a b c -> b a c"),
                  outs=(q,))
            for tb in range(nblk):
                for kk in range(NK):
                    mm(k, ps_sc, q[0:128, kk * 128:(kk + 1) * 128],
                       k.B.ap[:, kk * S + tb * 512: kk * S + (tb + 1) * 512],
                       start=(kk == 0), stop=(kk == NK - 1), ins=(q, k.B),
                       partial_out=(tb > 0),
                       out_ap=ps_sc.ap[:, tb * 512:(tb + 1) * 512])
            cn, cb, cs = 2 * (i % 2), 2 * (i % 2) + 1, 8 + (i % 2)
            k.op("vector", lambda v: v.reduce_max(stat(cn), ps_sc.ap[:, 0:W], AX.X,
                                                  negate=True),
                 ins=(ps_sc,), outs=(stats[cn],))
            k.op("vector", lambda v: v.tensor_scalar_mul(stat(cb), stat(cn), SC0),
                 ins=(stats[cn],), outs=(stats[cb],))
            p = P16.next()
            k.op("scalar", lambda s: s.activation(
                p.ap[:, 0:W], ps_sc.ap[:, 0:W], AF.Exp,
                scale=cc.ap[:, C_SC0:C_SC0 + 1], bias=stat(cb)),
                ins=(ps_sc, stats[cb], cc), outs=(p,))
            r = i % 4
            tbd = i // 4
            k.op("vector", lambda v: v.tensor_mul(
                p.ap[:, tbd * 512:(tbd + 1) * 512],
                p.ap[:, tbd * 512:(tbd + 1) * 512],
                k.r0["cmask"].ap[:, r * 512:(r + 1) * 512]),
                ins=(p, k.r0["cmask"]), outs=(p,), partial=True)
            k.op("vector", lambda v: v.reduce_sum(stat(cs), p.ap[:, 0:W], AX.X),
                 ins=(p,), outs=(stats[cs],))
            k.op("vector", lambda v: v.reciprocal(stat(cs), stat(cs)),
                 ins=(stats[cs],), outs=(stats[cs],))
            state[i] = (p, stats[cs])

        def av_part(i):
            p, rr = state.pop(i)
            pt = PT.next()
            for j in range(i + 1):
                pb = pstr.next()
                k.op("tensor", lambda t, pb=pb, j=j: t.transpose(
                    pb.ap, p.ap[:, j * 128:(j + 1) * 128], k.r0["cident"].ap),
                    ins=(p, k.r0["cident"]), outs=(pb,))
                eng = "vector" if (j % 2 == 0) else "scalar"
                if eng == "vector":
                    k.op(eng, lambda v, pb=pb, j=j: v.tensor_copy(
                        pt.ap[:, j * 128:(j + 1) * 128], pb.ap),
                        ins=(pb,), outs=(pt,), partial=True)
                else:
                    k.op(eng, lambda s, pb=pb, j=j: s.copy(
                        pt.ap[:, j * 128:(j + 1) * 128], pb.ap),
                        ins=(pb,), outs=(pt,), partial=True)
            for half in range(2):
                for b in range(2):
                    off = half * 1024 + b * 512
                    for j in range(i + 1):
                        mm(k, ps_av, pt[0:128, j * 128:(j + 1) * 128],
                           k.A.ap[:, j * S + off: j * S + off + 512],
                           start=(j == 0), stop=(j == i), ins=(pt, k.A),
                           partial_out=(b > 0),
                           out_ap=ps_av.ap[:, b * 512:(b + 1) * 512])
                for b in range(2):
                    off = half * 1024 + b * 512
                    xb_ = xbl.next()
                    xnb_ = xnbl.next()
                    ob = obl.next()
                    k.dma("sync", xb_.ap,
                          io["x0"][i * 128:(i + 1) * 128, off:off + 512],
                          outs=(xb_,))
                    k.dma("scalar", xnb_.ap,
                          io["xn"][i * 128:(i + 1) * 128, off:off + 512],
                          outs=(xnb_,))
                    k.op("vector", lambda v, ob=ob, b=b, rr=rr, xb_=xb_:
                         v.scalar_tensor_tensor(
                             ob.ap, ps_av.ap[:, b * 512:(b + 1) * 512], rr.ap,
                             xb_.ap, ALU.mult, ALU.add),
                         ins=(ps_av, rr, xb_), outs=(ob,))
                    k.op("vector", lambda v, ob=ob, xnb_=xnb_: v.tensor_add(
                        ob.ap, ob.ap, xnb_.ap), ins=(ob, xnb_), outs=(ob,))
                    k.dma("sync", io["x1"][i * 128:(i + 1) * 128, off:off + 512],
                          ob.ap, ins=(ob,), outs=(k.x1reg[i][off // 512],))

        for i in range(NT):
            softmax_part(i)
            if i >= 1:
                av_part(i - 1)
        av_part(NT - 1)


def phase5(k, io):
    nc = k.nc
    with ExitStack() as ps:
        wg = Rot([k.buf(f"p5_wg{j}", [128, NK * 128], BF16, ps) for j in range(2)])
        wu = Rot([k.buf(f"p5_wu{j}", [128, NK * 128], BF16, ps) for j in range(2)])
        dt_ = Rot([k.buf(f"p5_dt{j}", [128, NK * 512], BF16, ps) for j in range(2)])
        gs = Rot([k.buf(f"p5_gs{j}", [128, 512], BF16, ps) for j in range(2)])
        tmp = Rot([k.buf(f"p5_tmp{j}", [128, 512], F32, ps) for j in range(3)])
        scoreb = k.buf("p5_scb", [128, S], BF16, ps)
        srowb = k.buf("p5_srow", [1, S], F32, ps)
        msk = k.buf("p5_msk", [128, NT * 8], F32, ps)
        pb = [k.pbuf(f"p5_pb{j}", [128, 512], F32, ps) for j in range(8)]

        if ("x1_in" in io):
            for q4 in range(4):
                cp = k.dma("gpsimd", io["x1"][q4 * 512:(q4 + 1) * 512, :],
                           io["x1_in"][q4 * 512:(q4 + 1) * 512, :])
                for i in range(q4 * 4, q4 * 4 + 4):
                    for b in range(NB):
                        k.x1reg[i][b].wr = [cp]

        for t in range(NK):
            k.dma("sync", k.A.ap[:, t * S:(t + 1) * S], io["xnfm"][t],
                  outs=(k.A,), partial=(t > 0))
        k.dma("sync", msk.ap.rearrange("p (a c) -> p a c", a=NT),
              io["masks"].rearrange("a p c -> p a c"), outs=(msk,))
        k.dma("sync", srowb.ap, io["score"], outs=(srowb,))

        def expert(xbuf, gate_ap, up_ap, down_ap, mask_col):
            for fc in range(NK):
                g = wg.next()
                u = wu.next()
                for hh in range(2):
                    k.dma("gpsimd",
                          g.ap[:, hh * 8 * 128:(hh + 1) * 8 * 128].rearrange(
                              "p (a c) -> p a c", a=8),
                          gate_ap[hh * 1024:(hh + 1) * 1024,
                                  fc * 128:(fc + 1) * 128].rearrange(
                              "(a b) c -> b a c", b=128),
                          outs=(g,), partial=(hh > 0))
                for hh in range(2):
                    k.dma("gpsimd",
                          u.ap[:, hh * 8 * 128:(hh + 1) * 8 * 128].rearrange(
                              "p (a c) -> p a c", a=8),
                          up_ap[hh * 1024:(hh + 1) * 1024,
                                fc * 128:(fc + 1) * 128].rearrange(
                              "(a b) c -> b a c", b=128),
                          outs=(u,), partial=(hh > 0))
                for tb in range(NB):
                    for kk in range(NK):
                        mm(k, pb[tb], g[0:128, kk * 128:(kk + 1) * 128],
                           xbuf.ap[:, kk * S + tb * 512: kk * S + (tb + 1) * 512],
                           start=(kk == 0), stop=(kk == NK - 1), ins=(g, xbuf))
                for tb in range(NB):
                    for kk in range(NK):
                        mm(k, pb[4 + tb], u[0:128, kk * 128:(kk + 1) * 128],
                           xbuf.ap[:, kk * S + tb * 512: kk * S + (tb + 1) * 512],
                           start=(kk == 0), stop=(kk == NK - 1), ins=(u, xbuf))
                for tb in range(NB):
                    gg = gs.next()
                    k.op("scalar", lambda s, gg=gg, tb=tb: s.activation(
                        gg.ap, pb[tb].ap, AF.Silu), ins=(pb[tb],), outs=(gg,))
                    k.op("vector", lambda v, gg=gg, tb=tb, fc=fc: v.tensor_mul(
                        k.B.ap[:, fc * S + tb * 512: fc * S + (tb + 1) * 512],
                        gg.ap, pb[4 + tb].ap),
                        ins=(gg, pb[4 + tb]), outs=(k.B,), partial=True)
            for db in range(NB):
                d = dt_.next()
                for hh in range(2):
                    k.dma("gpsimd",
                          d.ap[:, hh * 8 * 512:(hh + 1) * 8 * 512].rearrange(
                              "p (a c) -> p a c", a=8),
                          down_ap[hh * 1024:(hh + 1) * 1024,
                                  db * 512:(db + 1) * 512].rearrange(
                              "(a b) c -> b a c", b=128),
                          outs=(d,), partial=(hh > 0))
                for tc in range(NT):
                    pd = pb[tc % 2]
                    for fc in range(NK):
                        mm(k, pd, k.B.ap[:, fc * S + tc * 128: fc * S + tc * 128 + 128],
                           d.ap[:, fc * 512:(fc + 1) * 512],
                           start=(fc == 0), stop=(fc == NK - 1), ins=(k.B, d))
                    t_ = tmp.next()
                    if mask_col is None:
                        k.op("scalar", lambda s, t_=t_, pd=pd: s.copy(t_.ap, pd.ap),
                             ins=(pd,), outs=(t_,))
                    else:
                        mc = Buf(msk.ap[:, mask_col(tc):mask_col(tc) + 1], "m")
                        mc.wr = msk.wr
                        k.op("vector", lambda v, t_=t_, pd=pd, mc=mc: v.tensor_scalar(
                            t_.ap, pd.ap, mc.ap, None, ALU.mult),
                            ins=(pd, msk), outs=(t_,))
                    k.dma("gpsimd", io["x1"][tc * 128:(tc + 1) * 128,
                                             db * 512:(db + 1) * 512],
                          t_.ap, ins=(t_,), outs=(k.x1reg[tc][db],),
                          accum_op=ALU.add)

        expert(k.A, io["L0_ffn_shared_gate_w"], io["L0_ffn_shared_up_w"],
               io["L0_ffn_shared_down_w"], None)
        for nb in range(NB):
            mm(k, pb[nb], k.r0["cones32"][0:1, 0:128],
               srowb.ap[0:1, nb * 512:(nb + 1) * 512], start=True, stop=True,
               ins=(k.r0["cones32"], srowb))
            k.op("vector", lambda v, nb=nb: v.tensor_copy(
                scoreb.ap[:, nb * 512:(nb + 1) * 512], pb[nb].ap),
                ins=(pb[nb],), outs=(scoreb,), partial=(nb > 0))
        for kk in range(NK):
            k.op("vector", lambda v, kk=kk: v.tensor_mul(
                k.A.ap[:, kk * S:(kk + 1) * S], k.A.ap[:, kk * S:(kk + 1) * S],
                scoreb.ap), ins=(k.A, scoreb), outs=(k.A,), partial=True)
        for ee in range(8):
            expert(k.A, io["L0_ffn_gate_w"][ee], io["L0_ffn_up_w"][ee],
                   io["L0_ffn_down_w"][ee],
                   mask_col=lambda tc, ee=ee: tc * 8 + ee)




def phase6(k, io):
    cc = k.r0["cconsts"]
    eps = cc.ap[:, C_EPS:C_EPS + 1]
    nhalf = cc.ap[:, C_NHALF:C_NHALF + 1]
    with ExitStack() as ps:
        xb = Rot([k.buf(f"p6_x{j}", [128, D], F32, ps) for j in range(2)])
        sq = k.buf("p6_sq", [128, D], BF16, ps)
        z16 = k.buf("p6_z16", [128, D], BF16, ps)
        zst = k.buf("p6_zst", [128, D], BF16, ps)
        pstr = Rot([k.pbuf(f"p6_tr{j}", [128, 128], BF16, ps) for j in range(4)])
        st = k.statbuf

        def stat(c):
            return st.ap[:, c:c + 1]

        stats = {c: k.vbuf(st.ap[:, c:c + 1], f"st{c}") for c in range(4)}
        for i in range(NT):
            x = xb.next()
            k.dma("sync", x.ap, io["x1"][i * 128:(i + 1) * 128, :], outs=(x,))
            k.op("vector", lambda v, x=x: v.reduce_sum(stat(0), x.ap, AX.X,
                                                       negate=True),
                 ins=(x,), outs=(stats[0],))
            k.op("vector", lambda v: v.tensor_scalar_mul(stat(0), stat(0), 1.0 / D),
                 ins=(stats[0],), outs=(stats[0],))
            k.op("scalar", lambda s, x=x: s.activation(
                sq.ap, x.ap, AF.Square, bias=stat(0), accum_out=stat(1)),
                ins=(x, stats[0]), outs=(sq, stats[1]))
            k.op("scalar", lambda s: s.activation(stat(1), stat(1), AF.Ln,
                                                  scale=1.0 / D, bias=eps),
                 ins=(stats[1],), outs=(stats[1],))
            k.op("scalar", lambda s: s.activation(stat(1), stat(1), AF.Exp,
                                                  scale=nhalf),
                 ins=(stats[1],), outs=(stats[1],))
            k.op("vector", lambda v, x=x: v.tensor_scalar(
                z16.ap, x.ap, stat(0), stat(1), ALU.add, ALU.mult),
                ins=(x, stats[0], stats[1]), outs=(z16,))
            for kk in range(NK):
                pb = pstr.next()
                k.op("tensor", lambda t, pb=pb, kk=kk: t.transpose(
                    pb.ap, z16.ap[:, kk * 128:(kk + 1) * 128], k.r0["cident"].ap),
                    ins=(z16, k.r0["cident"]), outs=(pb,))
                if kk % 2 == 0:
                    k.op("vector", lambda v, pb=pb, kk=kk: v.tensor_copy(
                        zst.ap[:, kk * 128:(kk + 1) * 128], pb.ap),
                        ins=(pb,), outs=(zst,), partial=True)
                else:
                    k.op("scalar", lambda s, pb=pb, kk=kk: s.copy(
                        zst.ap[:, kk * 128:(kk + 1) * 128], pb.ap),
                        ins=(pb,), outs=(zst,), partial=True)
            k.dma("sync",
                  io["xn2fm"][:, :, i * 128:(i + 1) * 128].rearrange("a p c -> p a c"),
                  zst.ap.rearrange("p (a c) -> p a c", a=NK), ins=(zst,))


def phase7(k, io):
    cc = k.r0["cconsts"]
    with ExitStack() as ps:
        for j in range(8):
            k.dma("gpsimd", io["w0b"][j * 1024:(j + 1) * 1024, :],
                  io["L1_ffn_W0"][j * 1024:(j + 1) * 1024, :])
        for j in range(8):
            k.dma("gpsimd", io["w1b"][j * 256:(j + 1) * 256, :],
                  io["L1_ffn_W1"][j * 256:(j + 1) * 256, :])

        for t in range(NK):
            k.dma("sync", k.A.ap[:, t * S:(t + 1) * S], io["xn2fm"][t],
                  outs=(k.A,), partial=(t > 0))
        brows = {}
        for nm, n in (("b_dkv", 512), ("b_dq", 512), ("b_uk", 1536), ("b_uv", D),
                      ("b_uq", 1536), ("b_qr", 512), ("b_kr", 512)):
            b = k.buf("p7_" + nm, [1, n], BF16, ps)
            k.dma("gpsimd", b.ap, io["L1_attn_" + nm], outs=(b,))
            brows[nm] = b
        wcol = Rot([k.buf(f"p7_wc{j}", [128, NK * 128], BF16, ps) for j in range(2)])
        wslab = k.buf("p7_ws", [128, 8192], BF16, ps)
        vstage = Rot([k.buf(f"p7_vs{j}", [128, D], BF16, ps) for j in range(2)])
        rtmp = Rot([k.buf(f"p7_rt{j}", [128, 512], F32, ps) for j in range(2)])
        rrot = k.buf("p7_rrot", [128, 512], F32, ps)
        rope16 = k.buf("p7_rope16", [128, 512], BF16, ps)
        cosb = Rot([k.buf(f"p7_cos{j}", [128, 512], F32, ps) for j in range(2)])
        sinb = Rot([k.buf(f"p7_sin{j}", [128, 512], F32, ps) for j in range(2)])
        khst = Rot([k.buf(f"p7_kh{j}", [96, D], BF16, ps) for j in range(2)])
        ps_big = Rot([k.pbuf("p7_pb0", [128, D], F32, ps)])
        ps_sm = Rot([k.pbuf(f"p7_sm{j}", [128, 512], F32, ps) for j in range(2)])
        pstr = Rot([k.pbuf(f"p7_tr{j}", [128, 128], BF16, ps) for j in range(2)])

        ones_row = k.r0["cones"]
        for nm, wname, bname, boff in (("ckv", "L1_attn_W_dkv", "b_dkv", 0),
                                       ("cq", "L1_attn_W_dq", "b_dq", 4 * S)):
            for mc in range(4):
                w = wcol.next()
                for hh in range(2):
                    k.dma("gpsimd",
                          w.ap[:, hh * 8 * 128:(hh + 1) * 8 * 128].rearrange(
                              "p (a c) -> p a c", a=8),
                          io[wname][hh * 1024:(hh + 1) * 1024,
                                    mc * 128:(mc + 1) * 128].rearrange(
                              "(a b) c -> b a c", b=128),
                          outs=(w,), partial=(hh > 0))
                po = ps_big.next()
                for tb in range(NB):
                    for kk in range(NK):
                        mm(k, po, w[0:128, kk * 128:(kk + 1) * 128],
                           k.A.ap[:, kk * S + tb * 512: kk * S + (tb + 1) * 512],
                           start=(kk == 0), stop=False, ins=(w, k.A),
                           partial_out=(tb > 0),
                           out_ap=po.ap[:, tb * 512:(tb + 1) * 512])
                    mm(k, po, brows[bname][0:1, mc * 128:(mc + 1) * 128],
                       ones_row[0:1, 0:512], start=False, stop=True,
                       ins=(brows[bname], ones_row), partial_out=True,
                       out_ap=po.ap[:, tb * 512:(tb + 1) * 512])
                for tb in range(NB):
                    k.op("vector", lambda v, po=po, tb=tb, boff=boff, mc=mc:
                         v.tensor_copy(
                             k.B.ap[:, boff + mc * S + tb * 512:
                                    boff + mc * S + (tb + 1) * 512],
                             po.ap[:, tb * 512:(tb + 1) * 512]),
                         ins=(po,), outs=(k.B,), partial=True)

        def rope_side(proj, dst_boff, bias_b, kr=False):
            for tc in range(NT):
                po = ps_sm.next()
                proj(po, tc)
                r = rtmp.next()
                k.op("vector", lambda v, r=r, po=po: v.tensor_copy(
                    r.ap, po.ap[:, 0:512]), ins=(po,), outs=(r,))
                cs = cosb.next()
                sn = sinb.next()
                k.dma("sync", cs.ap, io["ccos"][tc * 128:(tc + 1) * 128, :],
                      outs=(cs,))
                k.dma("sync", sn.ap, io["csin"][tc * 128:(tc + 1) * 128, :],
                      outs=(sn,))
                rr_ = r.ap.rearrange("p (g h) -> p g h", h=32)
                ro_ = rrot.ap.rearrange("p (g h) -> p g h", h=32)
                k.op("vector", lambda v, rr_=rr_, ro_=ro_: v.tensor_scalar_mul(
                    ro_[:, :, 0:16], rr_[:, :, 16:32], -1.0),
                    ins=(r,), outs=(rrot,))
                k.op("vector", lambda v, rr_=rr_, ro_=ro_: v.tensor_copy(
                    ro_[:, :, 16:32], rr_[:, :, 0:16]),
                    ins=(r,), outs=(rrot,), partial=True)
                k.op("vector", lambda v, r=r, cs=cs: v.tensor_mul(
                    r.ap, r.ap, cs.ap), ins=(r, cs), outs=(r,))
                k.op("vector", lambda v, sn=sn: v.tensor_mul(
                    rrot.ap, rrot.ap, sn.ap), ins=(rrot, sn), outs=(rrot,))
                k.op("vector", lambda v, r=r: v.tensor_add(
                    rope16.ap, r.ap, rrot.ap), ins=(r, rrot), outs=(rope16,))
                for g in range(4):
                    pb = pstr.next()
                    k.op("tensor", lambda t, pb=pb, g=g: t.transpose(
                        pb.ap, rope16.ap[:, g * 128:(g + 1) * 128],
                        k.r0["cident"].ap),
                        ins=(rope16, k.r0["cident"]), outs=(pb,))
                    k.op("vector", lambda v, pb=pb, g=g, tc=tc: v.tensor_copy(
                        k.B.ap[:, dst_boff + g * S + tc * 128:
                               dst_boff + g * S + (tc + 1) * 128], pb.ap),
                        ins=(pb,), outs=(k.B,), partial=True)

        for kk in range(NK):
            k.dma("gpsimd", wslab.ap[:, kk * 512:(kk + 1) * 512],
                  io["L1_attn_W_kr"][kk * 128:(kk + 1) * 128, :],
                  outs=(wslab,), partial=(kk > 0))

        def proj_kr(po, tc):
            for kk in range(NK):
                mm(k, po, k.A.ap[:, kk * S + tc * 128: kk * S + tc * 128 + 128],
                   wslab.ap[:, kk * 512:(kk + 1) * 512],
                   start=(kk == 0), stop=False, ins=(k.A, wslab),
                   out_ap=po.ap[:, 0:512])
            mm(k, po, k.r0["cones"][0:1, 0:128], brows["b_kr"][0:1, 0:512],
               start=False, stop=True, ins=(k.r0["cones"], brows["b_kr"]),
               partial_out=True, out_ap=po.ap[:, 0:512])

        rope_side(proj_kr, 8 * S, "b_kr")

        for ccc in range(4):
            k.dma("gpsimd", wslab.ap[:, ccc * 512:(ccc + 1) * 512],
                  io["L1_attn_W_qr"][ccc * 128:(ccc + 1) * 128, :],
                  outs=(wslab,), partial=(ccc > 0))

        def proj_qr(po, tc):
            for ccc in range(4):
                mm(k, po, k.B.ap[:, 4 * S + ccc * S + tc * 128:
                                 4 * S + ccc * S + tc * 128 + 128],
                   wslab.ap[:, ccc * 512:(ccc + 1) * 512],
                   start=(ccc == 0), stop=False, ins=(k.B, wslab),
                   out_ap=po.ap[:, 0:512])
            mm(k, po, k.r0["cones"][0:1, 0:128], brows["b_qr"][0:1, 0:512],
               start=False, stop=True, ins=(k.r0["cones"], brows["b_qr"]),
               partial_out=True, out_ap=po.ap[:, 0:512])

        rope_side(proj_qr, 12 * S, "b_qr")

        for wname, bname, coff, dst in (("L1_attn_W_uk", "b_uk", 0, "kfull"),
                                        ("L1_attn_W_uq", "b_uq", 4 * S, "qfull")):
            for ccc in range(4):
                k.dma("gpsimd", wslab.ap[:, ccc * 1536:(ccc + 1) * 1536],
                      io[wname][ccc * 128:(ccc + 1) * 128, :],
                      outs=(wslab,), partial=(ccc > 0))
            for h in range(16):
                po = ps_big.next()
                kh = khst.next()
                for tb in range(NB):
                    for ccc in range(4):
                        mm(k, po, wslab.ap[:, ccc * 1536 + h * 96:
                                           ccc * 1536 + (h + 1) * 96],
                           k.B.ap[:, coff + ccc * S + tb * 512:
                                  coff + ccc * S + (tb + 1) * 512],
                           start=(ccc == 0), stop=False, ins=(wslab, k.B),
                           partial_out=(tb > 0),
                           out_ap=po.ap[0:96, tb * 512:(tb + 1) * 512])
                    mm(k, po, brows[bname][0:1, h * 96:(h + 1) * 96],
                       ones_row[0:1, 0:512], start=False, stop=True,
                       ins=(brows[bname], ones_row), partial_out=True,
                       out_ap=po.ap[0:96, tb * 512:(tb + 1) * 512])
                for tb in range(NB):
                    k.op("vector", lambda v, kh=kh, po=po, tb=tb: v.tensor_copy(
                        kh.ap[:, tb * 512:(tb + 1) * 512],
                        po.ap[0:96, tb * 512:(tb + 1) * 512]),
                        ins=(po,), outs=(kh,), partial=(tb > 0))
                k.dma("sync", io[dst][h, 0:96, :], kh.ap, ins=(kh,))

        for dst, boff in (("kfull", 8 * S), ("qfull", 12 * S)):
            for h in range(16):
                k.dma("sync", io[dst][h, 96:128, :],
                      k.B.ap[32 * (h % 4):32 * (h % 4) + 32,
                             boff + (h // 4) * S: boff + (h // 4 + 1) * S],
                      ins=(k.B,))

        for ccc in range(4):
            k.dma("gpsimd", wslab.ap[:, ccc * D:(ccc + 1) * D],
                  io["L1_attn_W_uv"][ccc * 128:(ccc + 1) * 128, :],
                  outs=(wslab,), partial=(ccc > 0))
        for tc in range(NT):
            po = ps_big.next()
            vs = vstage.next()
            for db in range(NB):
                for ccc in range(4):
                    mm(k, po, k.B.ap[:, ccc * S + tc * 128: ccc * S + tc * 128 + 128],
                       wslab.ap[:, ccc * D + db * 512: ccc * D + (db + 1) * 512],
                       start=(ccc == 0), stop=False, ins=(k.B, wslab),
                       partial_out=(db > 0),
                       out_ap=po.ap[:, db * 512:(db + 1) * 512])
                mm(k, po, k.r0["cones"][0:1, 0:128],
                   brows["b_uv"][0:1, db * 512:(db + 1) * 512],
                   start=False, stop=True, ins=(k.r0["cones"], brows["b_uv"]),
                   partial_out=True, out_ap=po.ap[:, db * 512:(db + 1) * 512])
            for db in range(NB):
                k.op("vector", lambda v, vs=vs, po=po, db=db: v.tensor_copy(
                    vs.ap[:, db * 512:(db + 1) * 512],
                    po.ap[:, db * 512:(db + 1) * 512]),
                    ins=(po,), outs=(vs,), partial=(db > 0))
            k.dma("sync", io["v1tm"][tc], vs.ap, ins=(vs,))


def phase8(k, io):
    cc = k.r0["cconsts"]
    with ExitStack() as ps:
        qh = Rot([k.buf(f"p8_q{j}", [128, 512], BF16, ps) for j in range(2)])
        vh = Rot([k.buf(f"p8_v{j}", [128, NT * 128], BF16, ps) for j in range(2)])
        expst = Rot([k.buf(f"p8_e{j}", [128, NT * 512], BF16, ps) for j in range(2)])
        rbb = k.buf("p8_rbb", [128, 4 * 512], F32, ps)
        denr = k.buf("p8_den", [1, 4 * 512], F32, ps)
        ps_st = Rot([k.pbuf(f"p8_st{j}", [128, 512], F32, ps) for j in range(2)])
        ps_out = k.pbuf("p8_out", [128, D], F32, ps)
        ps_den = k.pbuf("p8_pden", [1, 512], F32, ps)
        ps_rb = k.pbuf("p8_prb", [128, 512], F32, ps)

        for t in range(NT):
            k.dma("sync", k.A.ap[:, t * S:(t + 1) * S], io["v1tm"][t],
                  outs=(k.A,), partial=(t > 0))
        for t in range(16):
            k.dma("scalar", k.B.ap[:, t * S:(t + 1) * S], io["kfull"][t],
                  outs=(k.B,), partial=(t > 0))

        state = {}

        def st_part(h, b):
            q = qh.next()
            k.dma("sync", q.ap, io["qfull"][h, :, b * 512:(b + 1) * 512], outs=(q,))
            es = expst.next()
            for j in range(NT):
                pst = ps_st.next()
                mm(k, pst, k.B.ap[:, h * S + j * 128: h * S + (j + 1) * 128],
                   q.ap, start=True, stop=True, ins=(k.B, q))
                k.op("scalar", lambda s, es=es, j=j, pst=pst: s.activation(
                    es.ap[:, j * 512:(j + 1) * 512], pst.ap, AF.Exp,
                    scale=cc.ap[:, C_SC1:C_SC1 + 1],
                    bias=cc.ap[:, C_L1SHIFT:C_L1SHIFT + 1]),
                    ins=(pst, cc), outs=(es,), partial=(j > 0))
            state[(h, b)] = es

        def av_part(h, b):
            es = state.pop((h, b))
            v = vh.bufs[h % 2]
            for j in range(NT):
                mm(k, ps_out, v[0:128, j * 128:(j + 1) * 128],
                   es.ap[:, j * 512:(j + 1) * 512],
                   start=(j == 0), stop=(j == NT - 1), ins=(v, es),
                   partial_out=(b > 0), out_ap=ps_out.ap[:, b * 512:(b + 1) * 512])
            for j in range(NT):
                mm(k, ps_den, k.r0["conescol"][0:128, 0:1],
                   es.ap[:, j * 512:(j + 1) * 512],
                   start=(j == 0), stop=(j == NT - 1), ins=(k.r0["conescol"], es))
            k.op("vector", lambda v_, b=b: v_.tensor_copy(
                denr.ap[0:1, b * 512:(b + 1) * 512], ps_den.ap),
                ins=(ps_den,), outs=(denr,), partial=(b > 0))
            k.op("vector", lambda v_, b=b: v_.reciprocal(
                denr.ap[0:1, b * 512:(b + 1) * 512],
                denr.ap[0:1, b * 512:(b + 1) * 512]),
                ins=(denr,), outs=(denr,), partial=True)
            mm(k, ps_rb, k.r0["cones32"][0:1, 0:128],
               denr.ap[0:1, b * 512:(b + 1) * 512], start=True, stop=True,
               ins=(k.r0["cones32"], denr))
            k.op("vector", lambda v_, b=b: v_.tensor_copy(
                rbb.ap[:, b * 512:(b + 1) * 512], ps_rb.ap),
                ins=(ps_rb,), outs=(rbb,), partial=(b > 0))

        for h in range(16):
            v = vh.bufs[h % 2]
            vbuf = v
            k.dma("sync", vbuf.ap.rearrange("p (a c) -> p a c", a=NT),
                  io["v1tm"][:, :, h * 128:(h + 1) * 128].rearrange("a b c -> b a c"),
                  outs=(vbuf,))
            for b in range(NB):
                st_part(h, b)
                if b >= 1:
                    av_part(h, b - 1)
            av_part(h, NB - 1)
            for b in range(NB):
                k.op("vector", lambda v_, h=h, b=b: v_.tensor_mul(
                    k.B.ap[:, h * S + b * 512: h * S + (b + 1) * 512],
                    ps_out.ap[:, b * 512:(b + 1) * 512],
                    rbb.ap[:, b * 512:(b + 1) * 512]),
                    ins=(ps_out, rbb), outs=(k.B,), partial=True)
            k.dma("sync", io["outfm"][h], k.B.ap[:, h * S:(h + 1) * S], ins=(k.B,))


def phase9(k, io):
    with ExitStack() as ps:
        oc = Rot([k.buf(f"p9_oc{j}", [128, 16 * 128], BF16, ps) for j in range(2)])
        bo = k.buf("p9_bo", [1, D], BF16, ps)
        x1b = Rot([k.buf(f"p9_x{j}", [128, 512], F32, ps) for j in range(4)])
        ost = Rot([k.buf(f"p9_o{j}", [128, 512], F32, ps) for j in range(4)])
        pso = Rot([k.pbuf(f"p9_ps{j}", [128, D], F32, ps) for j in range(2)])
        k.dma("gpsimd", bo.ap, io["L1_attn_b_o"], outs=(bo,))
        for t in range(16):
            k.dma("gpsimd", k.B.ap[:, t * S:(t + 1) * S],
                  io["L1_attn_W_o"][t * 128:(t + 1) * 128, :],
                  outs=(k.B,), partial=(t > 0))
        for qc in range(NT):
            o = oc.next()
            k.dma("sync", o.ap.rearrange("p (a c) -> p a c", a=16),
                  io["outfm"][:, :, qc * 128:(qc + 1) * 128].rearrange(
                      "a b c -> b a c"), outs=(o,))
            po = pso.next()
            for db in range(NB):
                for h in range(16):
                    mm(k, po, o[0:128, h * 128:(h + 1) * 128],
                       k.B.ap[:, h * S + db * 512: h * S + (db + 1) * 512],
                       start=(h == 0), stop=False, ins=(o, k.B),
                       partial_out=(db > 0),
                       out_ap=po.ap[:, db * 512:(db + 1) * 512])
                mm(k, po, k.r0["cones"][0:1, 0:128],
                   bo.ap[0:1, db * 512:(db + 1) * 512], start=False, stop=True,
                   ins=(k.r0["cones"], bo), partial_out=True,
                   out_ap=po.ap[:, db * 512:(db + 1) * 512])
            for db in range(NB):
                xb = x1b.next()
                os_ = ost.next()
                k.dma("scalar", xb.ap,
                      io["x1"][qc * 128:(qc + 1) * 128, db * 512:(db + 1) * 512],
                      outs=(xb,))
                k.op("vector", lambda v, os_=os_, po=po, db=db, xb=xb: v.tensor_add(
                    os_.ap, po.ap[:, db * 512:(db + 1) * 512], xb.ap),
                    ins=(po, xb), outs=(os_,))
                k.dma("sync", io["p2t"][qc * 128:(qc + 1) * 128,
                                        db * 512:(db + 1) * 512], os_.ap,
                      ins=(os_,))


def phase10(k, io):
    with ExitStack() as ps:
        w0t = Rot([k.vbuf(k.B.ap[:, 0:8192], "w0t0"), k.vbuf(k.B.ap[:, 8192:16384], "w0t1")])
        b0 = k.buf("p10_b0", [1, 8192], BF16, ps)
        b1 = k.buf("p10_b1", [1, D], BF16, ps)
        dst_ = Rot([k.buf(f"p10_d{j}", [128, D], BF16, ps) for j in range(2)])
        dbuf = Rot([k.vbuf(k.B.ap[:, 16384:24576], "dbuf0"), k.vbuf(k.B.ap[:, 24576:32768], "dbuf1")])
        p2b = Rot([k.buf(f"p10_p2{j}", [128, 512], F32, ps) for j in range(4)])
        ost = Rot([k.buf(f"p10_o{j}", [128, 512], F32, ps) for j in range(4)])
        ps_big = Rot([k.pbuf(f"p10_pb{j}", [128, D], F32, ps) for j in range(2)])
        k.dma("gpsimd", b0.ap, io["L1_ffn_bias0"], outs=(b0,))
        k.dma("gpsimd", b1.ap, io["L1_ffn_bias1"], outs=(b1,))
        for t in range(NK):
            k.dma("sync", k.A.ap[:, t * S:(t + 1) * S], io["xn2fm"][t],
                  outs=(k.A,), partial=(t > 0))
        for fb in range(16):
            w = w0t.next()
            for kc in range(NK):
                k.dmat("scalar", w.ap[:, kc * 512:(kc + 1) * 512],
                       io["w0b"][fb * 512:(fb + 1) * 512, kc * 128:(kc + 1) * 128],
                       outs=(w,), partial=(kc > 0))
            for f4 in range(4):
                fcg = fb * 4 + f4
                po = ps_big.next()
                d = dst_.next()
                for tb in range(NB):
                    for kc in range(NK):
                        mm(k, po, w[0:128, kc * 512 + f4 * 128: kc * 512 + f4 * 128 + 128],
                           k.A.ap[:, kc * S + tb * 512: kc * S + (tb + 1) * 512],
                           start=(kc == 0), stop=False, ins=(w, k.A),
                           partial_out=(tb > 0),
                           out_ap=po.ap[:, tb * 512:(tb + 1) * 512])
                    mm(k, po, b0.ap[0:1, fcg * 128:(fcg + 1) * 128],
                       k.r0["cones"][0:1, 0:512], start=False, stop=True,
                       ins=(b0, k.r0["cones"]), partial_out=True,
                       out_ap=po.ap[:, tb * 512:(tb + 1) * 512])
                for tb in range(NB):
                    k.op("scalar", lambda s, d=d, po=po, tb=tb: s.activation(
                        d.ap[:, tb * 512:(tb + 1) * 512],
                        po.ap[:, tb * 512:(tb + 1) * 512], AF.Gelu),
                        ins=(po,), outs=(d,), partial=(tb > 0))
                k.dma("sync", io["dfm"][fcg], d.ap, ins=(d,))
        for db in range(NB):
            for fc in range(64):
                k.dmat("scalar", k.A.ap[:, fc * 512:(fc + 1) * 512],
                       io["w1b"][db * 512:(db + 1) * 512, fc * 128:(fc + 1) * 128],
                       outs=(k.A,), partial=(fc > 0))
            for tc in range(NT):
                db_ = dbuf.next()
                k.dma("sync", db_.ap.rearrange("p (a c) -> p a c", a=64),
                      io["dfm"][:, :, tc * 128:(tc + 1) * 128].rearrange(
                          "a b c -> b a c"), outs=(db_,))
                po = ps_big.next()
                for fc in range(64):
                    mm(k, po, db_[0:128, fc * 128:(fc + 1) * 128],
                       k.A.ap[:, fc * 512:(fc + 1) * 512],
                       start=(fc == 0), stop=False, ins=(db_, k.A),
                       out_ap=po.ap[:, 0:512])
                mm(k, po, k.r0["cones"][0:1, 0:128],
                   b1.ap[0:1, db * 512:(db + 1) * 512], start=False, stop=True,
                   ins=(k.r0["cones"], b1), partial_out=True,
                   out_ap=po.ap[:, 0:512])
                p2_ = p2b.next()
                os_ = ost.next()
                k.dma("scalar", p2_.ap,
                      io["p2t"][tc * 128:(tc + 1) * 128, db * 512:(db + 1) * 512],
                      outs=(p2_,))
                k.op("vector", lambda v, os_=os_, po=po, p2_=p2_: v.tensor_add(
                    os_.ap, po.ap[:, 0:512], p2_.ap), ins=(po, p2_), outs=(os_,))
                k.dma("sync", io["out"][tc * 128:(tc + 1) * 128,
                                        db * 512:(db + 1) * 512], os_.ap,
                      ins=(os_,))


PHASES = {"p1": phase1, "p2": phase2, "p3": phase3, "p4": phase4,
          "p5": phase5, "p6": phase6, "p7": phase7, "p8": phase8,
          "p9": phase9, "p10": phase10}


def build_group(phases):
    nc = bass.Bass("TRN2", target_bir_lowering=False)
    stack = ExitStack()
    k = Ctx(nc, stack)
    io = {}

    group_inputs = set(CONST_INPUTS)
    for p in phases:
        group_inputs |= set(PHASE_INPUTS[p])
    produced = set()
    consumed = set()
    for p in phases:
        produced |= set(PHASE_SCRATCH_WRITES[p])
        consumed |= set(PHASE_SCRATCH_READS[p])
        for m_ in MUTATES.get(p, []):
            consumed.add(m_)

    in_names, out_names = [], []
    for nm in sorted(group_inputs):
        shape, dt = INPUTS[nm]
        io[nm] = nc.dram_tensor(nm, shape, dt, kind="ExternalInput").ap()
        in_names.append(nm)
    last_idx = ALL_PHASES.index(phases[-1])
    for nm, (shape, dt, prod) in SCRATCH.items():
        if prod in phases:
            later = any(nm in PHASE_SCRATCH_READS[q] or nm in MUTATES.get(q, [])
                        for q in ALL_PHASES[last_idx + 1:])
            kind = "ExternalOutput" if later else "Internal"
            io[nm] = nc.dram_tensor(nm, shape, dt, kind=kind).ap()
            if kind == "ExternalOutput":
                out_names.append(nm)
        elif nm in consumed:
            if any(nm in MUTATES.get(p, []) for p in phases):
                io[nm + "_in"] = nc.dram_tensor(nm + "_in", shape, dt,
                                                kind="ExternalInput").ap()
                in_names.append(nm + "_in")
                later = any(nm in PHASE_SCRATCH_READS[q] or nm in MUTATES.get(q, [])
                            for q in ALL_PHASES[last_idx + 1:])
                kind = "ExternalOutput" if later else "Internal"
                io[nm] = nc.dram_tensor(nm, shape, dt, kind=kind).ap()
                if kind == "ExternalOutput":
                    out_names.append(nm)
                k.mut_init = nm
            else:
                io[nm] = nc.dram_tensor(nm, shape, dt, kind="ExternalInput").ap()
                in_names.append(nm)
    if "p10" in phases:
        io["out"] = nc.dram_tensor("out", [S, D], F32, kind="ExternalOutput").ap()
        out_names.append("out")

    for nm in list(io.keys()):
        base = nm[:-3] if nm.endswith("_in") else nm
        if base in SCRATCH or nm == "out":
            k.hbufs[nm] = Buf(None, "h_" + nm)
    with stack:
        need_arena = {a for p in phases for a in PHASE_ARENAS.get(p, ())}
        if "A" in need_arena:
            k.A = Buf(nc.alloc_sbuf_tensor("arenaA", [128, NK * S], BF16)[:], "A")
            k.Abuf = k.A
        if "B" in need_arena:
            k.B = Buf(nc.alloc_sbuf_tensor("arenaB", [128, NK * S], BF16)[:], "B")
            k.Bbuf = k.B
        k.r0 = {}
        for nm, shape, dt in (("cident", [128, 128], BF16),
                              ("cident32", [128, 128], F32),
                              ("cones", [1, 512], BF16),
                              ("cones32", [1, 512], F32),
                              ("conescol", [128, 1], BF16),
                              ("cconsts", [128, 8], F32),
                              ("cmask", [128, 4 * 512], BF16)):
            k.r0[nm] = Buf(nc.alloc_sbuf_tensor("sb_" + nm, shape, dt)[:], nm)
        k.statbuf = Buf(nc.alloc_sbuf_tensor("stat", [128, 32], F32)[:], "stat")
        k.x1reg = [[Buf(None, f"x1_{i}_{b}") for b in range(NB)] for i in range(NT)]

        k.esem = {"vector": k.sem("s_dve"), "scalar": k.sem("s_act"),
                  "tensor": k.sem("s_pe"), "gpsimd": k.sem("s_gpc"),
                  "sync": k.sem("s_sy")}
        k.qsem = {"sync": k.sem("q_sync"), "scalar": k.sem("q_scalar"),
                  "gpsimd": k.sem("q_gp")}

        for nm in CONST_INPUTS:
            k.dma("sync", k.r0[nm].ap, io[nm], outs=(k.r0[nm],))
        k.op("vector", lambda v: v.tensor_copy(k.r0["cident32"].ap,
                                               k.r0["cident"].ap),
             ins=(k.r0["cident"],), outs=(k.r0["cident32"],))

        for p in phases:
            k.phase_guard = [(s, s.n) for s in k.semlist if s.n > 0]
            PHASES[p](k, io)

        k.waitall("sync", [(s, s.n) for s in k.semlist])

        with nc.Block() as block:
            @block.sync
            def _(eng):
                for f in k.ops["sync"]:
                    f(eng)

            @block.vector
            def _(eng):
                for f in k.ops["vector"]:
                    f(eng)

            @block.scalar
            def _(eng):
                for f in k.ops["scalar"]:
                    f(eng)

            @block.tensor
            def _(eng):
                for f in k.ops["tensor"]:
                    f(eng)

            @block.gpsimd
            def _(eng):
                for f in k.ops["gpsimd"]:
                    f(eng)

    return nc, in_names, out_names


GROUPS = [list(ALL_PHASES)]


def make_constants():
    c = {}
    c["cident"] = np.eye(128, dtype=BF)
    c["cones"] = np.ones((1, 512), dtype=BF)
    c["cones32"] = np.ones((1, 512), dtype=np.float32)
    c["conescol"] = np.ones((128, 1), dtype=BF)
    cc = np.zeros((128, 8), np.float32)
    cc[:, C_SC0] = SC0
    cc[:, C_SC1] = SC1
    cc[:, C_L1SHIFT] = L1SHIFT
    cc[:, C_EPS] = 1e-5
    cc[:, C_NHALF] = -0.5
    cc[:, C_ONE] = 1.0
    c["cconsts"] = cc
    m = np.ones((128, 4, 512), np.float32)
    for r in range(4):
        for q in range(128):
            m[q, r, 128 * r + q + 1:] = 0.0
    c["cmask"] = np.ascontiguousarray(m.reshape(128, 4 * 512)).astype(BF)
    inv = 1.0 / (10000.0 ** (np.arange(0, 32, 2, dtype=np.float32) / 32.0))
    fr = np.arange(S, dtype=np.float32)[:, None] * inv[None, :]
    emb = np.concatenate([fr, fr], axis=1)
    c["ccos"] = np.ascontiguousarray(np.tile(np.cos(emb), (1, 16))).astype(np.float32)
    c["csin"] = np.ascontiguousarray(np.tile(np.sin(emb), (1, 16))).astype(np.float32)
    return c


def prep_inputs(core, inputs):
    d = {}
    d["x0"] = np.ascontiguousarray(np.asarray(inputs["L0_attn_x"])[core])
    for nm in ("L0_attn_qkv", "L0_attn_o_proj", "L0_ffn_router_w",
               "L0_ffn_gate_w", "L0_ffn_up_w", "L0_ffn_down_w",
               "L0_ffn_shared_gate_w", "L0_ffn_shared_up_w",
               "L0_ffn_shared_down_w", "L1_attn_W_dkv", "L1_attn_W_dq",
               "L1_attn_W_uk", "L1_attn_W_uv", "L1_attn_W_uq", "L1_attn_W_qr",
               "L1_attn_W_kr", "L1_attn_W_o", "L1_ffn_W0", "L1_ffn_W1"):
        d[nm] = np.ascontiguousarray(np.asarray(inputs[nm]))
    for nm in ("L1_attn_b_dkv", "L1_attn_b_dq", "L1_attn_b_uk", "L1_attn_b_uv",
               "L1_attn_b_uq", "L1_attn_b_qr", "L1_attn_b_kr", "L1_attn_b_o",
               "L1_ffn_bias0", "L1_ffn_bias1"):
        d[nm] = np.ascontiguousarray(np.asarray(inputs[nm])).reshape(1, -1)
    d["cgamma"] = np.ascontiguousarray(
        np.broadcast_to(np.asarray(inputs["L0_attn_gamma"])[None, :], (128, D)))
    d["cbeta"] = np.ascontiguousarray(
        np.broadcast_to(np.asarray(inputs["L0_attn_beta"])[None, :], (128, D)))
    d.update(make_constants())
    return d


def run_groups(groups, per_core, n_cores=8):
    for phases in groups:
        nc, in_names, out_names = build_group(phases)
        in_maps = []
        for c in range(n_cores):
            m_ = {}
            for nm in in_names:
                if nm.endswith("_in") and nm[:-3] in SCRATCH:
                    m_[nm] = per_core[c][nm[:-3]]
                else:
                    m_[nm] = per_core[c][nm]
            in_maps.append(m_)
        res = run_bass_kernel_spmd(nc, in_maps, core_ids=list(range(n_cores)))
        for c in range(n_cores):
            for nm in out_names:
                per_core[c][nm] = np.asarray(res.results[c][nm])
    return per_core


def kernel(**inputs):
    per_core = [prep_inputs(c, inputs) for c in range(8)]
    per_core = run_groups(GROUPS, per_core)
    return np.stack([per_core[c]["out"] for c in range(8)], axis=0)


# version 07
# speedup vs baseline: 1.0083x; 1.0083x over previous
import numpy as np
import ml_dtypes
from contextlib import ExitStack

import concourse.bass as bass
import concourse.mybir as mybir
from concourse.bass_utils import run_bass_kernel_spmd

F32 = mybir.dt.float32
BF16 = mybir.dt.bfloat16
AF = mybir.ActivationFunctionType
ALU = mybir.AluOpType
AX = mybir.AxisListType
BF = ml_dtypes.bfloat16

S = 2048
D = 2048
NT = S // 128
NK = D // 128
NB = S // 512
SC0 = 1.0 / float(np.sqrt(2048.0))
SC1 = 1.0 / float(np.sqrt(128.0))
L1SHIFT = -4.0
NEG = -1.0e30

ENGINES = ("sync", "vector", "scalar", "tensor", "gpsimd")
DMA_ENGS = ("sync", "scalar", "gpsimd")

SCRATCH = {
    "xn":     ([S, D], F32, "p1"),
    "zfm":    ([NK, 128, S], BF16, "p1"),
    "xnfm":   ([NK, 128, S], BF16, "p1"),
    "masks":  ([NT, 128, 8], F32, "p1"),
    "score":  ([1, S], F32, "p1"),
    "qfm":    ([NK, 128, S], BF16, "p2"),
    "kfm":    ([NK, 128, S], BF16, "p2"),
    "vfm":    ([NK, 128, S], BF16, "p2"),
    "vo":     ([NT, 128, D], BF16, "p3"),
    "x1":     ([S, D], F32, "p4"),
    "xn2fm":  ([NK, 128, S], BF16, "p6"),
    "kfull":  ([16, 128, S], BF16, "p7"),
    "qfull":  ([16, 128, S], BF16, "p7"),
    "v1tm":   ([NT, 128, D], BF16, "p7"),
    "outfm":  ([16, 128, S], BF16, "p8"),
    "p2t":    ([S, D], F32, "p9"),
    "w0b":    ([8192, D], BF16, "p7"),
    "w1b":    ([D, 8192], BF16, "p7"),
    "dfm":    ([64, 128, S], BF16, "p10"),
    "qb_":    ([D, D], BF16, "p2"),
    "kb_":    ([D, D], BF16, "p2"),
    "vb_":    ([D, D], BF16, "p2"),
    "gwb":    ([8, D, D], BF16, "p5"),
    "uwb":    ([8, D, D], BF16, "p5"),
    "dwb":    ([8, D, D], BF16, "p5"),
    "sgb":    ([D, D], BF16, "p5"),
    "sub":    ([D, D], BF16, "p5"),
    "sdb":    ([D, D], BF16, "p5"),
    "dkvb":   ([D, 512], BF16, "p7"),
    "dqb":    ([D, 512], BF16, "p7"),
    "opb":    ([D, D], BF16, "p3"),
    "krb":    ([D, 512], BF16, "p7"),
    "ukb":    ([512, 1536], BF16, "p7"),
    "uvb":    ([512, D], BF16, "p7"),
    "uqb":    ([512, 1536], BF16, "p7"),
    "qrb":    ([512, 512], BF16, "p7"),
    "wob":    ([D, D], BF16, "p9"),
}
MUTATES = {"p5": ["x1"]}

INPUTS = {
    "x0": ([S, D], F32),
    "L0_attn_qkv": ([D, 3 * D], F32),
    "L0_attn_o_proj": ([D, D], F32),
    "cgamma": ([128, D], F32),
    "cbeta": ([128, D], F32),
    "L0_ffn_router_w": ([D, 8], F32),
    "L0_ffn_gate_w": ([8, D, D], F32),
    "L0_ffn_up_w": ([8, D, D], F32),
    "L0_ffn_down_w": ([8, D, D], F32),
    "L0_ffn_shared_gate_w": ([D, D], F32),
    "L0_ffn_shared_up_w": ([D, D], F32),
    "L0_ffn_shared_down_w": ([D, D], F32),
    "L1_attn_W_dkv": ([D, 512], F32),
    "L1_attn_b_dkv": ([1, 512], F32),
    "L1_attn_W_dq": ([D, 512], F32),
    "L1_attn_b_dq": ([1, 512], F32),
    "L1_attn_W_uk": ([512, 1536], F32),
    "L1_attn_b_uk": ([1, 1536], F32),
    "L1_attn_W_uv": ([512, D], F32),
    "L1_attn_b_uv": ([1, D], F32),
    "L1_attn_W_uq": ([512, 1536], F32),
    "L1_attn_b_uq": ([1, 1536], F32),
    "L1_attn_W_qr": ([512, 512], F32),
    "L1_attn_b_qr": ([1, 512], F32),
    "L1_attn_W_kr": ([D, 512], F32),
    "L1_attn_b_kr": ([1, 512], F32),
    "L1_attn_W_o": ([D, D], F32),
    "L1_attn_b_o": ([1, D], F32),
    "L1_ffn_W0": ([8192, D], F32),
    "L1_ffn_bias0": ([1, 8192], F32),
    "L1_ffn_W1": ([D, 8192], F32),
    "L1_ffn_bias1": ([1, D], F32),
    "cident": ([128, 128], BF16),
    "cones": ([1, 512], BF16),
    "cones32": ([1, 512], F32),
    "conescol": ([128, 1], BF16),
    "cconsts": ([128, 8], F32),
    "cmask": ([128, 4 * 512], BF16),
    "ccos": ([S, 512], F32),
    "csin": ([S, 512], F32),
}
C_SC0, C_SC1, C_L1SHIFT, C_EPS, C_NHALF, C_ONE = 0, 1, 2, 3, 4, 5

PHASE_INPUTS = {
    "p1": ["x0", "L0_ffn_router_w", "cgamma", "cbeta"],
    "p2": ["L0_attn_qkv"],
    "p3": ["L0_attn_o_proj"],
    "p4": ["x0"],
    "p5": ["L0_ffn_gate_w", "L0_ffn_up_w", "L0_ffn_down_w",
           "L0_ffn_shared_gate_w", "L0_ffn_shared_up_w", "L0_ffn_shared_down_w"],
    "p6": [],
    "p7": ["L1_attn_W_dkv", "L1_attn_b_dkv", "L1_attn_W_dq", "L1_attn_b_dq",
           "L1_attn_W_uk", "L1_attn_b_uk", "L1_attn_W_uv", "L1_attn_b_uv",
           "L1_attn_W_uq", "L1_attn_b_uq", "L1_attn_W_qr", "L1_attn_b_qr",
           "L1_attn_W_kr", "L1_attn_b_kr", "ccos", "csin",
           "L1_ffn_W0", "L1_ffn_W1"],
    "p8": [],
    "p9": ["L1_attn_W_o", "L1_attn_b_o"],
    "p10": ["L1_ffn_bias0", "L1_ffn_bias1"],
}
CONST_INPUTS = ["cident", "cones", "cones32", "conescol", "cconsts", "cmask"]
PHASE_SCRATCH_READS = {
    "p1": [],
    "p2": ["zfm"],
    "p3": ["vfm"],
    "p4": ["qfm", "kfm", "vo", "xn"],
    "p5": ["xnfm", "masks", "score"],
    "p6": ["x1"],
    "p7": ["xn2fm"],
    "p8": ["kfull", "qfull", "v1tm"],
    "p9": ["outfm", "x1"],
    "p10": ["xn2fm", "p2t", "w0b", "w1b"],
}
ALL_PHASES = ["p1", "p2", "p3", "p4", "p5", "p6", "p7", "p8", "p9", "p10"]
PHASE_ARENAS = {"p2": ["A"], "p4": ["A", "B"], "p5": ["A", "B"],
                "p7": ["A", "B"], "p8": ["A", "B"], "p9": ["B"], "p10": ["A", "B"]}
PHASE_SCRATCH_WRITES = {p: [n for n, v in SCRATCH.items() if v[2] == p]
                        for p in ALL_PHASES}


class Sem:
    __slots__ = ("h", "n", "name")


class Buf:
    __slots__ = ("ap", "wr", "rd", "name")

    def __init__(self, ap, name=""):
        self.ap = ap
        self.wr = []
        self.rd = []
        self.name = name

    def __getitem__(self, sl):
        return self.ap[sl]


class Ctx:
    def __init__(self, nc, stack):
        self.nc = nc
        self.stack = stack
        self.ops = {e: [] for e in ENGINES}
        self.waited = {e: {} for e in ENGINES}
        self.semlist = []
        self.nops = 0
        self.phase_guard = []
        self.hbufs = {}

    def sem(self, name):
        s = Sem()
        s.h = self.stack.enter_context(self.nc.semaphore(name))
        s.n = 0
        s.name = name
        self.semlist.append(s)
        return s

    def _resolve(self, eng, waits):
        ws = []
        for (s, v) in waits:
            if v <= 0:
                continue
            prev = self.waited[eng].get(id(s), 0)
            if v > prev:
                ws.append((s.h, v))
                self.waited[eng][id(s)] = v
        return ws

    def op(self, eng, f, ins=(), outs=(), partial=False, extra=(), sem=None,
           inc_by=None):
        if sem is None:
            sem = self.esem[eng]
            inc_by = 1 if inc_by is None else inc_by
        waits = list(extra)
        for b in ins:
            waits += b.wr
        for b in outs:
            waits += b.rd
            if not partial:
                waits += b.wr
        ws = self._resolve(eng, waits)
        sem.n += inc_by
        tgt = (sem, sem.n)
        for b in ins:
            b.rd.append(tgt)
            if len(b.rd) > 400:
                del b.rd[0:len(b.rd) - 400]
        for b in outs:
            if partial:
                b.wr.append(tgt)
                if len(b.wr) > 400:
                    del b.wr[0:len(b.wr) - 400]
            else:
                b.wr = [tgt]
                b.rd = []
        ops = self.ops[eng]
        self.nops += 1

        def run(e, f=f, ws=tuple(ws), h=sem.h, inc_by=inc_by):
            for (hh, v) in ws:
                e.wait_ge(hh, v)
            f(e).then_inc(h, inc_by)
        ops.append(run)
        return tgt

    def _hb(self, ap):
        t = getattr(ap, "tensor", None)
        nm = getattr(t, "name", None)
        return self.hbufs.get(nm)

    def dma(self, eng, out, in_, ins=(), outs=(), partial=False, extra=(), **kw):
        hi = self._hb(in_)
        ho = self._hb(out)
        if hi is not None:
            ins = tuple(ins) + (hi,)
        if ho is not None:
            outs = tuple(outs) + (ho,)
            partial = True
        return self.op(eng, lambda e: e.dma_start(out=out, in_=in_, **kw),
                       ins=ins, outs=outs, partial=partial, extra=extra,
                       sem=self.qsem[eng], inc_by=16)

    def dmat(self, eng, out, in_, ins=(), outs=(), partial=False, extra=()):
        hi = self._hb(in_)
        if hi is not None:
            ins = tuple(ins) + (hi,)
        return self.op(eng, lambda e: e.dma_start_transpose(out, in_),
                       ins=ins, outs=outs, partial=partial, extra=extra,
                       sem=self.qsem[eng], inc_by=16)

    def waitall(self, eng, targets):
        ws = self._resolve(eng, targets)
        if not ws:
            return
        self.nops += 1

        def run(e, ws=tuple(ws)):
            for (hh, v) in ws:
                e.wait_ge(hh, v)
        self.ops[eng].append(run)

    def barrier(self):
        targets = [(s, s.n) for s in self.semlist if s.n > 0]
        for e in ENGINES:
            self.waitall(e, targets)

    def vbuf(self, ap, name=""):
        b = Buf(ap, name)
        b.wr = list(self.phase_guard)
        return b

    def buf(self, name, shape, dtype, stack=None):
        st = stack if stack is not None else self.stack
        h = st.enter_context(self.nc.sbuf_tensor(name, shape, dtype))
        b = Buf(h[:], name)
        b.wr = list(self.phase_guard)
        return b

    def pbuf(self, name, shape, dtype=F32, stack=None):
        st = stack if stack is not None else self.stack
        h = st.enter_context(self.nc.psum_tensor(name, shape, dtype))
        b = Buf(h[:], name)
        b.wr = list(self.phase_guard)
        return b


def mm(k, out_psum, lhsT, rhs, start, stop, ins, partial_out=False, out_ap=None):
    oap = out_psum.ap if out_ap is None else out_ap
    return k.op("tensor", lambda t: t.matmul(oap, lhsT, rhs, start=start, stop=stop),
                ins=ins, outs=(out_psum,),
                partial=(not start) or partial_out)


class Rot:
    def __init__(self, bufs):
        self.bufs = bufs
        self.i = 0

    def next(self):
        b = self.bufs[self.i % len(self.bufs)]
        self.i += 1
        return b


def phase1(k, io):
    nc = k.nc
    cc = k.r0["cconsts"]
    eps = cc.ap[:, C_EPS:C_EPS + 1]
    nhalf = cc.ap[:, C_NHALF:C_NHALF + 1]
    with ExitStack() as ps:
        xb = Rot([k.buf(f"p1_x{j}", [128, D], F32, ps) for j in range(2)])
        xnb_ = Rot([k.buf(f"p1_xn{j}", [128, D], F32, ps) for j in range(2)])
        z16 = k.buf("p1_z16", [128, D], BF16, ps)
        sq = z16
        zg = z16
        hi16 = k.buf("p1_hi16", [128, D], BF16, ps)
        lo16 = k.buf("p1_lo16", [128, D], BF16, ps)
        zst = k.buf("p1_zst", [128, D], BF16, ps)
        hist = k.buf("p1_hist", [128, D], BF16, ps)
        lost = k.buf("p1_lost", [128, D], BF16, ps)
        gb = k.buf("p1_gb", [128, D], BF16, ps)
        bb = k.buf("p1_bb", [128, D], BF16, ps)
        rwhi = k.buf("p1_rwhi", [128, 8 * NK], BF16, ps)
        rwlo = k.buf("p1_rwlo", [128, 8 * NK], BF16, ps)
        logit = k.buf("p1_logit", [128, 8], F32, ps)
        m8 = k.buf("p1_m8", [128, 8], F32, ps)
        scol = k.buf("p1_scol", [128, 16], F32, ps)
        srow = k.buf("p1_srow", [16, 128], F32, ps)
        pstr = Rot([k.pbuf(f"p1_tr{j}", [128, 128], BF16, ps) for j in range(4)])
        ps_r = k.pbuf("p1_psr", [128, 8], F32, ps)
        ps_sr = k.pbuf("p1_pssr", [16, 128], F32, ps)
        st = k.statbuf

        def stat(c):
            return st.ap[:, c:c + 1]

        stats = {c: k.vbuf(st.ap[:, c:c + 1], f"st{c}") for c in range(12)}

        with ExitStack() as ips:
            g32 = xb.bufs[0]
            rw32 = k.buf("p1_rw32", [128, 8 * NK], F32, ips)
            k.dma("sync", g32.ap, io["cgamma"], outs=(g32,))
            k.op("vector", lambda v: v.tensor_copy(gb.ap, g32.ap), ins=(g32,),
                 outs=(gb,))
            k.dma("sync", g32.ap, io["cbeta"], outs=(g32,))
            k.op("vector", lambda v: v.tensor_copy(bb.ap, g32.ap), ins=(g32,),
                 outs=(bb,))
            k.dma("sync", rw32.ap.rearrange("p (a c) -> p a c", a=NK),
                  io["L0_ffn_router_w"].rearrange("(a b) c -> b a c", b=128),
                  outs=(rw32,))
            k.op("vector", lambda v: v.tensor_copy(rwhi.ap, rw32.ap), ins=(rw32,),
                 outs=(rwhi,))
            k.op("vector", lambda v: v.tensor_sub(rwlo.ap, rw32.ap, rwhi.ap),
                 ins=(rw32, rwhi), outs=(rwlo,))
            k.barrier()

        def ln_stats(buf, c_nm, c_rstd, c_b2):
            k.op("vector", lambda v: v.reduce_sum(stat(c_nm), buf.ap, AX.X,
                                                  negate=True),
                 ins=(buf,), outs=(stats[c_nm],))
            k.op("vector", lambda v: v.tensor_scalar_mul(stat(c_nm), stat(c_nm),
                                                         1.0 / D),
                 ins=(stats[c_nm],), outs=(stats[c_nm],))
            k.op("scalar", lambda s: s.activation(sq.ap, buf.ap, AF.Square,
                                                  bias=stat(c_nm),
                                                  accum_out=stat(c_rstd)),
                 ins=(buf, stats[c_nm]), outs=(sq, stats[c_rstd]))
            k.op("scalar", lambda s: s.activation(stat(c_rstd), stat(c_rstd), AF.Ln,
                                                  scale=1.0 / D, bias=eps),
                 ins=(stats[c_rstd],), outs=(stats[c_rstd],))
            k.op("scalar", lambda s: s.activation(stat(c_rstd), stat(c_rstd), AF.Exp,
                                                  scale=nhalf),
                 ins=(stats[c_rstd],), outs=(stats[c_rstd],))
            k.op("vector", lambda v: v.tensor_tensor(stat(c_b2), stat(c_nm),
                                                     stat(c_rstd), ALU.mult),
                 ins=(stats[c_nm], stats[c_rstd]), outs=(stats[c_b2],))

        def transpose_to(src, stage):
            for kk in range(NK):
                pb = pstr.next()
                k.op("tensor", lambda t, pb=pb, kk=kk: t.transpose(
                    pb.ap, src[0:128, kk * 128:(kk + 1) * 128], k.r0["cident"].ap),
                    ins=(src, k.r0["cident"]), outs=(pb,))
                eng = "vector" if (kk % 2 == 0) else "scalar"
                if eng == "vector":
                    k.op(eng, lambda v, pb=pb, kk=kk: v.tensor_copy(
                        stage[0:128, kk * 128:(kk + 1) * 128], pb.ap),
                        ins=(pb,), outs=(stage,), partial=True)
                else:
                    k.op(eng, lambda s, pb=pb, kk=kk: s.copy(
                        stage[0:128, kk * 128:(kk + 1) * 128], pb.ap),
                        ins=(pb,), outs=(stage,), partial=True)

        for i in range(NT):
            x = xb.next()
            xn = xnb_.next()
            k.dma("sync", x.ap, io["x0"][i * 128:(i + 1) * 128, :], outs=(x,))
            ln_stats(x, 0, 1, 2)
            k.op("vector", lambda v, x=x, xn=xn: v.tensor_scalar(
                xn.ap, x.ap, stat(0), stat(1), ALU.add, ALU.mult),
                ins=(x, stats[0], stats[1]), outs=(xn,))
            k.dma("sync", io["xn"][i * 128:(i + 1) * 128, :], xn.ap, ins=(xn,))
            k.op("vector", lambda v, xn=xn: v.tensor_copy(hi16.ap, xn.ap),
                 ins=(xn,), outs=(hi16,))
            k.op("vector", lambda v, xn=xn: v.tensor_sub(lo16.ap, xn.ap, hi16.ap),
                 ins=(xn, hi16), outs=(lo16,))
            ln_stats(xn, 4, 5, 6)
            k.op("vector", lambda v, xn=xn: v.tensor_scalar(
                z16.ap, xn.ap, stat(4), stat(5), ALU.add, ALU.mult),
                ins=(xn, stats[4], stats[5]), outs=(z16,))
            k.op("vector", lambda v: v.tensor_mul(zg.ap, z16.ap, gb.ap),
                 ins=(z16, gb), outs=(zg,), partial=True)
            k.op("vector", lambda v: v.tensor_add(zg.ap, zg.ap, bb.ap),
                 ins=(zg, bb), outs=(zg,), partial=True)
            transpose_to(zg, zst)
            transpose_to(hi16, hist)
            transpose_to(lo16, lost)
            k.dma("sync",
                  io["zfm"][:, :, i * 128:(i + 1) * 128].rearrange("a p c -> p a c"),
                  zst.ap.rearrange("p (a c) -> p a c", a=NK), ins=(zst,))
            k.dma("sync",
                  io["xnfm"][:, :, i * 128:(i + 1) * 128].rearrange("a p c -> p a c"),
                  hist.ap.rearrange("p (a c) -> p a c", a=NK), ins=(hist,))
            for kk in range(NK):
                mm(k, ps_r, hist[0:128, kk * 128:(kk + 1) * 128],
                   rwhi[0:128, kk * 8:(kk + 1) * 8],
                   start=(kk == 0), stop=False, ins=(hist, rwhi))
            for kk in range(NK):
                mm(k, ps_r, hist[0:128, kk * 128:(kk + 1) * 128],
                   rwlo[0:128, kk * 8:(kk + 1) * 8], start=False, stop=False,
                   ins=(hist, rwlo))
            for kk in range(NK):
                mm(k, ps_r, lost[0:128, kk * 128:(kk + 1) * 128],
                   rwhi[0:128, kk * 8:(kk + 1) * 8], start=False,
                   stop=(kk == NK - 1), ins=(lost, rwhi))
            k.op("vector", lambda v: v.tensor_copy(logit.ap, ps_r.ap),
                 ins=(ps_r,), outs=(logit,))
            k.op("vector", lambda v: v.reduce_max(stat(8), logit.ap, AX.X),
                 ins=(logit,), outs=(stats[8],))
            for ee in range(8):
                k.op("vector", lambda v, ee=ee: v.tensor_scalar(
                    m8.ap[:, ee:ee + 1], logit.ap[:, ee:ee + 1], stat(8), None,
                    ALU.is_equal), ins=(logit, stats[8]), outs=(m8,), partial=(ee > 0))
            k.dma("sync", io["masks"][i], m8.ap, ins=(m8,))
            k.op("scalar", lambda s, i=i: s.activation(
                scol.ap[:, i:i + 1], stat(8), AF.Sigmoid),
                ins=(stats[8],), outs=(scol,), partial=(i > 0))
        k.op("tensor", lambda t: t.transpose(ps_sr.ap, scol.ap, k.r0["cident32"].ap),
             ins=(scol, k.r0["cident32"]), outs=(ps_sr,))
        k.op("vector", lambda v: v.tensor_copy(srow.ap, ps_sr.ap), ins=(ps_sr,),
             outs=(srow,))
        k.dma("sync", io["score"].rearrange("o (a c) -> (o a) c", a=16), srow.ap,
              ins=(srow,))


def phase2(k, io):
    with ExitStack() as ps:
        wb = Rot([k.buf(f"p2_w{j}", [128, NK * 128], BF16, ps) for j in range(3)])
        ost = Rot([k.buf(f"p2_o{j}", [128, S], BF16, ps) for j in range(2)])
        pso = Rot([k.pbuf(f"p2_ps{j}", [128, S], F32, ps) for j in range(2)])
        zA = k.Abuf
        for t in range(NK):
            k.dma("sync", k.A.ap[:, t * S:(t + 1) * S], io["zfm"][t],
                  outs=(zA,), partial=(t > 0))
        for name, src_ in (("qfm", "qb_"), ("kfm", "kb_"), ("vfm", "vb_")):
            for mc in range(NK):
                w = wb.next()
                for hh in range(2):
                    k.dma("scalar" if (mc % 2 == 0) else "sync",
                          w.ap[:, hh * 8 * 128:(hh + 1) * 8 * 128].rearrange(
                              "p (a c) -> p a c", a=8),
                          io[src_][hh * 1024:(hh + 1) * 1024,
                                   mc * 128:(mc + 1) * 128].rearrange(
                              "(a b) c -> b a c", b=128),
                          outs=(w,), partial=(hh > 0))
                po = pso.next()
                o = ost.next()
                for tb in range(NB):
                    for kk in range(NK):
                        mm(k, po, w[0:128, kk * 128:(kk + 1) * 128],
                           k.A.ap[:, kk * S + tb * 512: kk * S + (tb + 1) * 512],
                           start=(kk == 0), stop=(kk == NK - 1),
                           ins=(w, zA), partial_out=(tb > 0),
                           out_ap=po.ap[:, tb * 512:(tb + 1) * 512])
                for tb in range(NB):
                    k.op("vector", lambda v, o=o, po=po, tb=tb: v.tensor_copy(
                        o.ap[:, tb * 512:(tb + 1) * 512],
                        po.ap[:, tb * 512:(tb + 1) * 512]),
                        ins=(po,), outs=(o,), partial=(tb > 0))
                k.dma("sync", io[name][mc], o.ap, ins=(o,))


def phase3(k, io):
    with ExitStack() as ps:
        ob = k.buf("p3_ob", [128, NK * 512], BF16, ps)
        vb = Rot([k.buf(f"p3_v{j}", [128, NK * 128], BF16, ps) for j in range(2)])
        ost = Rot([k.buf(f"p3_o{j}", [128, 512], BF16, ps) for j in range(2)])
        pso = Rot([k.pbuf(f"p3_ps{j}", [128, 512], F32, ps) for j in range(2)])
        op_ = io["opb"]
        for db in range(NB):
            for kk in range(NK):
                k.dma("scalar" if (kk % 2 == 0) else "sync",
                      ob.ap[:, kk * 512:(kk + 1) * 512],
                      op_[kk * 128:(kk + 1) * 128, db * 512:(db + 1) * 512],
                      outs=(ob,), partial=(kk > 0))
            for tc in range(NT):
                v = vb.next()
                k.dma("sync", v.ap.rearrange("p (a c) -> p a c", a=NK),
                      io["vfm"][:, :, tc * 128:(tc + 1) * 128].rearrange(
                          "a b c -> b a c"), outs=(v,))
                po = pso.next()
                o = ost.next()
                for kk in range(NK):
                    mm(k, po, v[0:128, kk * 128:(kk + 1) * 128],
                       ob.ap[:, kk * 512:(kk + 1) * 512],
                       start=(kk == 0), stop=(kk == NK - 1), ins=(v, ob))
                k.op("vector", lambda vv, o=o, po=po: vv.tensor_copy(o.ap, po.ap),
                     ins=(po,), outs=(o,))
                k.dma("sync", io["vo"][tc, :, db * 512:(db + 1) * 512], o.ap,
                      ins=(o,))




def phase4(k, io):
    cc = k.r0["cconsts"]
    with ExitStack() as ps:
        qt = Rot([k.buf(f"p4_q{j}", [128, NK * 128], BF16, ps) for j in range(2)])
        P16 = Rot([k.buf(f"p4_p{j}", [128, S], BF16, ps) for j in range(2)])
        PT = Rot([k.buf(f"p4_pt{j}", [128, NK * 128], BF16, ps) for j in range(2)])
        xbl = Rot([k.buf(f"p4_xb{j}", [128, 512], F32, ps) for j in range(4)])
        xnbl = Rot([k.buf(f"p4_xnb{j}", [128, 512], F32, ps) for j in range(4)])
        obl = Rot([k.buf(f"p4_ob{j}", [128, 512], F32, ps) for j in range(4)])
        ps_sc = k.pbuf("p4_sc", [128, S], F32, ps)
        ps_av = k.pbuf("p4_av", [128, 1024], F32, ps)
        pstr = Rot([k.pbuf(f"p4_tr{j}", [128, 128], BF16, ps) for j in range(2)])
        st = k.statbuf

        def stat(c):
            return st.ap[:, c:c + 1]

        stats = {c: k.vbuf(st.ap[:, c:c + 1], f"st{c}") for c in range(12)}

        for t in range(NT):
            k.dma("sync", k.A.ap[:, t * S:(t + 1) * S], io["vo"][t],
                  outs=(k.A,), partial=(t > 0))
        for t in range(NK):
            k.dma("scalar", k.B.ap[:, t * S:(t + 1) * S], io["kfm"][t],
                  outs=(k.B,), partial=(t > 0))

        state = {}

        def softmax_part(i):
            nblk = i // 4 + 1
            W = 512 * nblk
            q = qt.next()
            k.dma("sync", q.ap.rearrange("p (a c) -> p a c", a=NK),
                  io["qfm"][:, :, i * 128:(i + 1) * 128].rearrange("a b c -> b a c"),
                  outs=(q,))
            for tb in range(nblk):
                for kk in range(NK):
                    mm(k, ps_sc, q[0:128, kk * 128:(kk + 1) * 128],
                       k.B.ap[:, kk * S + tb * 512: kk * S + (tb + 1) * 512],
                       start=(kk == 0), stop=(kk == NK - 1), ins=(q, k.B),
                       partial_out=(tb > 0),
                       out_ap=ps_sc.ap[:, tb * 512:(tb + 1) * 512])
            cn, cb, cs = 2 * (i % 2), 2 * (i % 2) + 1, 8 + (i % 2)
            k.op("vector", lambda v: v.reduce_max(stat(cn), ps_sc.ap[:, 0:W], AX.X,
                                                  negate=True),
                 ins=(ps_sc,), outs=(stats[cn],))
            k.op("vector", lambda v: v.tensor_scalar_mul(stat(cb), stat(cn), SC0),
                 ins=(stats[cn],), outs=(stats[cb],))
            p = P16.next()
            k.op("scalar", lambda s: s.activation(
                p.ap[:, 0:W], ps_sc.ap[:, 0:W], AF.Exp,
                scale=cc.ap[:, C_SC0:C_SC0 + 1], bias=stat(cb)),
                ins=(ps_sc, stats[cb], cc), outs=(p,))
            r = i % 4
            tbd = i // 4
            k.op("vector", lambda v: v.tensor_mul(
                p.ap[:, tbd * 512:(tbd + 1) * 512],
                p.ap[:, tbd * 512:(tbd + 1) * 512],
                k.r0["cmask"].ap[:, r * 512:(r + 1) * 512]),
                ins=(p, k.r0["cmask"]), outs=(p,), partial=True)
            k.op("vector", lambda v: v.reduce_sum(stat(cs), p.ap[:, 0:W], AX.X),
                 ins=(p,), outs=(stats[cs],))
            k.op("vector", lambda v: v.reciprocal(stat(cs), stat(cs)),
                 ins=(stats[cs],), outs=(stats[cs],))
            state[i] = (p, stats[cs])

        def av_part(i):
            p, rr = state.pop(i)
            pt = PT.next()
            for j in range(i + 1):
                pb = pstr.next()
                k.op("tensor", lambda t, pb=pb, j=j: t.transpose(
                    pb.ap, p.ap[:, j * 128:(j + 1) * 128], k.r0["cident"].ap),
                    ins=(p, k.r0["cident"]), outs=(pb,))
                eng = "vector" if (j % 2 == 0) else "scalar"
                if eng == "vector":
                    k.op(eng, lambda v, pb=pb, j=j: v.tensor_copy(
                        pt.ap[:, j * 128:(j + 1) * 128], pb.ap),
                        ins=(pb,), outs=(pt,), partial=True)
                else:
                    k.op(eng, lambda s, pb=pb, j=j: s.copy(
                        pt.ap[:, j * 128:(j + 1) * 128], pb.ap),
                        ins=(pb,), outs=(pt,), partial=True)
            for half in range(2):
                for b in range(2):
                    off = half * 1024 + b * 512
                    for j in range(i + 1):
                        mm(k, ps_av, pt[0:128, j * 128:(j + 1) * 128],
                           k.A.ap[:, j * S + off: j * S + off + 512],
                           start=(j == 0), stop=(j == i), ins=(pt, k.A),
                           partial_out=(b > 0),
                           out_ap=ps_av.ap[:, b * 512:(b + 1) * 512])
                for b in range(2):
                    off = half * 1024 + b * 512
                    xb_ = xbl.next()
                    xnb_ = xnbl.next()
                    ob = obl.next()
                    k.dma("sync", xb_.ap,
                          io["x0"][i * 128:(i + 1) * 128, off:off + 512],
                          outs=(xb_,))
                    k.dma("scalar", xnb_.ap,
                          io["xn"][i * 128:(i + 1) * 128, off:off + 512],
                          outs=(xnb_,))
                    k.op("vector", lambda v, ob=ob, b=b, rr=rr, xb_=xb_:
                         v.scalar_tensor_tensor(
                             ob.ap, ps_av.ap[:, b * 512:(b + 1) * 512], rr.ap,
                             xb_.ap, ALU.mult, ALU.add),
                         ins=(ps_av, rr, xb_), outs=(ob,))
                    k.op("vector", lambda v, ob=ob, xnb_=xnb_: v.tensor_add(
                        ob.ap, ob.ap, xnb_.ap), ins=(ob, xnb_), outs=(ob,))
                    k.dma("sync", io["x1"][i * 128:(i + 1) * 128, off:off + 512],
                          ob.ap, ins=(ob,), outs=(k.x1reg[i][off // 512],))

        for i in range(NT):
            softmax_part(i)
            if i >= 1:
                av_part(i - 1)
        av_part(NT - 1)


def phase5(k, io):
    nc = k.nc
    with ExitStack() as ps:
        wg = Rot([k.buf(f"p5_wg{j}", [128, NK * 128], BF16, ps) for j in range(2)])
        wu = Rot([k.buf(f"p5_wu{j}", [128, NK * 128], BF16, ps) for j in range(2)])
        dt_ = Rot([k.buf(f"p5_dt{j}", [128, NK * 512], BF16, ps) for j in range(2)])
        gs = Rot([k.buf(f"p5_gs{j}", [128, 512], BF16, ps) for j in range(2)])
        tmp = Rot([k.buf(f"p5_tmp{j}", [128, 512], F32, ps) for j in range(3)])
        scoreb = k.buf("p5_scb", [128, S], BF16, ps)
        srowb = k.buf("p5_srow", [1, S], F32, ps)
        msk = k.buf("p5_msk", [128, NT * 8], F32, ps)
        pb = [k.pbuf(f"p5_pb{j}", [128, 512], F32, ps) for j in range(8)]

        if ("x1_in" in io):
            for q4 in range(4):
                cp = k.dma("gpsimd", io["x1"][q4 * 512:(q4 + 1) * 512, :],
                           io["x1_in"][q4 * 512:(q4 + 1) * 512, :])
                for i in range(q4 * 4, q4 * 4 + 4):
                    for b in range(NB):
                        k.x1reg[i][b].wr = [cp]

        for t in range(NK):
            k.dma("sync", k.A.ap[:, t * S:(t + 1) * S], io["xnfm"][t],
                  outs=(k.A,), partial=(t > 0))
        k.dma("sync", msk.ap.rearrange("p (a c) -> p a c", a=NT),
              io["masks"].rearrange("a p c -> p a c"), outs=(msk,))
        k.dma("sync", srowb.ap, io["score"], outs=(srowb,))

        def expert(xbuf, gate_ap, up_ap, down_ap, mask_col):
            for fc in range(NK):
                g = wg.next()
                u = wu.next()
                for hh in range(2):
                    k.dma("scalar",
                          g.ap[:, hh * 8 * 128:(hh + 1) * 8 * 128].rearrange(
                              "p (a c) -> p a c", a=8),
                          gate_ap[hh * 1024:(hh + 1) * 1024,
                                  fc * 128:(fc + 1) * 128].rearrange(
                              "(a b) c -> b a c", b=128),
                          outs=(g,), partial=(hh > 0))
                for hh in range(2):
                    k.dma("sync",
                          u.ap[:, hh * 8 * 128:(hh + 1) * 8 * 128].rearrange(
                              "p (a c) -> p a c", a=8),
                          up_ap[hh * 1024:(hh + 1) * 1024,
                                fc * 128:(fc + 1) * 128].rearrange(
                              "(a b) c -> b a c", b=128),
                          outs=(u,), partial=(hh > 0))
                for tb in range(NB):
                    for kk in range(NK):
                        mm(k, pb[tb], g[0:128, kk * 128:(kk + 1) * 128],
                           xbuf.ap[:, kk * S + tb * 512: kk * S + (tb + 1) * 512],
                           start=(kk == 0), stop=(kk == NK - 1), ins=(g, xbuf))
                for tb in range(NB):
                    for kk in range(NK):
                        mm(k, pb[4 + tb], u[0:128, kk * 128:(kk + 1) * 128],
                           xbuf.ap[:, kk * S + tb * 512: kk * S + (tb + 1) * 512],
                           start=(kk == 0), stop=(kk == NK - 1), ins=(u, xbuf))
                for tb in range(NB):
                    gg = gs.next()
                    k.op("scalar", lambda s, gg=gg, tb=tb: s.activation(
                        gg.ap, pb[tb].ap, AF.Silu), ins=(pb[tb],), outs=(gg,))
                    k.op("vector", lambda v, gg=gg, tb=tb, fc=fc: v.tensor_mul(
                        k.B.ap[:, fc * S + tb * 512: fc * S + (tb + 1) * 512],
                        gg.ap, pb[4 + tb].ap),
                        ins=(gg, pb[4 + tb]), outs=(k.B,), partial=True)
            for db in range(NB):
                d = dt_.next()
                for hh in range(2):
                    k.dma("scalar" if (db % 2 == 0) else "sync",
                          d.ap[:, hh * 8 * 512:(hh + 1) * 8 * 512].rearrange(
                              "p (a c) -> p a c", a=8),
                          down_ap[hh * 1024:(hh + 1) * 1024,
                                  db * 512:(db + 1) * 512].rearrange(
                              "(a b) c -> b a c", b=128),
                          outs=(d,), partial=(hh > 0))
                for tc in range(NT):
                    pd = pb[tc % 2]
                    for fc in range(NK):
                        mm(k, pd, k.B.ap[:, fc * S + tc * 128: fc * S + tc * 128 + 128],
                           d.ap[:, fc * 512:(fc + 1) * 512],
                           start=(fc == 0), stop=(fc == NK - 1), ins=(k.B, d))
                    t_ = tmp.next()
                    if mask_col is None:
                        k.op("scalar", lambda s, t_=t_, pd=pd: s.copy(t_.ap, pd.ap),
                             ins=(pd,), outs=(t_,))
                    else:
                        mc = Buf(msk.ap[:, mask_col(tc):mask_col(tc) + 1], "m")
                        mc.wr = msk.wr
                        k.op("vector", lambda v, t_=t_, pd=pd, mc=mc: v.tensor_scalar(
                            t_.ap, pd.ap, mc.ap, None, ALU.mult),
                            ins=(pd, msk), outs=(t_,))
                    k.dma("gpsimd", io["x1"][tc * 128:(tc + 1) * 128,
                                             db * 512:(db + 1) * 512],
                          t_.ap, ins=(t_,), outs=(k.x1reg[tc][db],),
                          accum_op=ALU.add)

        expert(k.A, io["sgb"], io["sub"], io["sdb"], None)
        for nb in range(NB):
            mm(k, pb[nb], k.r0["cones32"][0:1, 0:128],
               srowb.ap[0:1, nb * 512:(nb + 1) * 512], start=True, stop=True,
               ins=(k.r0["cones32"], srowb))
            k.op("vector", lambda v, nb=nb: v.tensor_copy(
                scoreb.ap[:, nb * 512:(nb + 1) * 512], pb[nb].ap),
                ins=(pb[nb],), outs=(scoreb,), partial=(nb > 0))
        for kk in range(NK):
            k.op("vector", lambda v, kk=kk: v.tensor_mul(
                k.A.ap[:, kk * S:(kk + 1) * S], k.A.ap[:, kk * S:(kk + 1) * S],
                scoreb.ap), ins=(k.A, scoreb), outs=(k.A,), partial=True)
        for ee in range(8):
            expert(k.A, io["gwb"][ee], io["uwb"][ee], io["dwb"][ee],
                   mask_col=lambda tc, ee=ee: tc * 8 + ee)




def phase6(k, io):
    cc = k.r0["cconsts"]
    eps = cc.ap[:, C_EPS:C_EPS + 1]
    nhalf = cc.ap[:, C_NHALF:C_NHALF + 1]
    with ExitStack() as ps:
        xb = Rot([k.buf(f"p6_x{j}", [128, D], F32, ps) for j in range(2)])
        sq = k.buf("p6_sq", [128, D], BF16, ps)
        z16 = k.buf("p6_z16", [128, D], BF16, ps)
        zst = k.buf("p6_zst", [128, D], BF16, ps)
        pstr = Rot([k.pbuf(f"p6_tr{j}", [128, 128], BF16, ps) for j in range(4)])
        st = k.statbuf

        def stat(c):
            return st.ap[:, c:c + 1]

        stats = {c: k.vbuf(st.ap[:, c:c + 1], f"st{c}") for c in range(4)}
        for i in range(NT):
            x = xb.next()
            k.dma("sync", x.ap, io["x1"][i * 128:(i + 1) * 128, :], outs=(x,))
            k.op("vector", lambda v, x=x: v.reduce_sum(stat(0), x.ap, AX.X,
                                                       negate=True),
                 ins=(x,), outs=(stats[0],))
            k.op("vector", lambda v: v.tensor_scalar_mul(stat(0), stat(0), 1.0 / D),
                 ins=(stats[0],), outs=(stats[0],))
            k.op("scalar", lambda s, x=x: s.activation(
                sq.ap, x.ap, AF.Square, bias=stat(0), accum_out=stat(1)),
                ins=(x, stats[0]), outs=(sq, stats[1]))
            k.op("scalar", lambda s: s.activation(stat(1), stat(1), AF.Ln,
                                                  scale=1.0 / D, bias=eps),
                 ins=(stats[1],), outs=(stats[1],))
            k.op("scalar", lambda s: s.activation(stat(1), stat(1), AF.Exp,
                                                  scale=nhalf),
                 ins=(stats[1],), outs=(stats[1],))
            k.op("vector", lambda v, x=x: v.tensor_scalar(
                z16.ap, x.ap, stat(0), stat(1), ALU.add, ALU.mult),
                ins=(x, stats[0], stats[1]), outs=(z16,))
            for kk in range(NK):
                pb = pstr.next()
                k.op("tensor", lambda t, pb=pb, kk=kk: t.transpose(
                    pb.ap, z16.ap[:, kk * 128:(kk + 1) * 128], k.r0["cident"].ap),
                    ins=(z16, k.r0["cident"]), outs=(pb,))
                if kk % 2 == 0:
                    k.op("vector", lambda v, pb=pb, kk=kk: v.tensor_copy(
                        zst.ap[:, kk * 128:(kk + 1) * 128], pb.ap),
                        ins=(pb,), outs=(zst,), partial=True)
                else:
                    k.op("scalar", lambda s, pb=pb, kk=kk: s.copy(
                        zst.ap[:, kk * 128:(kk + 1) * 128], pb.ap),
                        ins=(pb,), outs=(zst,), partial=True)
            k.dma("sync",
                  io["xn2fm"][:, :, i * 128:(i + 1) * 128].rearrange("a p c -> p a c"),
                  zst.ap.rearrange("p (a c) -> p a c", a=NK), ins=(zst,))


def phase7(k, io):
    cc = k.r0["cconsts"]
    with ExitStack() as ps:
        for t in range(NK):
            k.dma("sync", k.A.ap[:, t * S:(t + 1) * S], io["xn2fm"][t],
                  outs=(k.A,), partial=(t > 0))
        brows = {}
        for nm, n in (("b_dkv", 512), ("b_dq", 512), ("b_uk", 1536), ("b_uv", D),
                      ("b_uq", 1536), ("b_qr", 512), ("b_kr", 512)):
            b = k.buf("p7_" + nm, [1, n], BF16, ps)
            k.dma("gpsimd", b.ap, io["L1_attn_" + nm], outs=(b,))
            brows[nm] = b
        wcol = Rot([k.buf(f"p7_wc{j}", [128, NK * 128], BF16, ps) for j in range(2)])
        wslab = k.buf("p7_ws", [128, 8192], BF16, ps)
        vstage = Rot([k.buf(f"p7_vs{j}", [128, D], BF16, ps) for j in range(2)])
        rtmp = Rot([k.buf(f"p7_rt{j}", [128, 512], F32, ps) for j in range(2)])
        rrot = k.buf("p7_rrot", [128, 512], F32, ps)
        rope16 = k.buf("p7_rope16", [128, 512], BF16, ps)
        cosb = Rot([k.buf(f"p7_cos{j}", [128, 512], F32, ps) for j in range(2)])
        sinb = Rot([k.buf(f"p7_sin{j}", [128, 512], F32, ps) for j in range(2)])
        khst = Rot([k.buf(f"p7_kh{j}", [96, D], BF16, ps) for j in range(2)])
        ps_big = Rot([k.pbuf("p7_pb0", [128, D], F32, ps)])
        ps_sm = Rot([k.pbuf(f"p7_sm{j}", [128, 512], F32, ps) for j in range(2)])
        pstr = Rot([k.pbuf(f"p7_tr{j}", [128, 128], BF16, ps) for j in range(2)])

        ones_row = k.r0["cones"]
        for nm, wname, bname, boff in (("ckv", "dkvb", "b_dkv", 0),
                                       ("cq", "dqb", "b_dq", 4 * S)):
            for mc in range(4):
                w = wcol.next()
                for hh in range(2):
                    k.dma("scalar",
                          w.ap[:, hh * 8 * 128:(hh + 1) * 8 * 128].rearrange(
                              "p (a c) -> p a c", a=8),
                          io[wname][hh * 1024:(hh + 1) * 1024,
                                    mc * 128:(mc + 1) * 128].rearrange(
                              "(a b) c -> b a c", b=128),
                          outs=(w,), partial=(hh > 0))
                po = ps_big.next()
                for tb in range(NB):
                    for kk in range(NK):
                        mm(k, po, w[0:128, kk * 128:(kk + 1) * 128],
                           k.A.ap[:, kk * S + tb * 512: kk * S + (tb + 1) * 512],
                           start=(kk == 0), stop=False, ins=(w, k.A),
                           partial_out=(tb > 0),
                           out_ap=po.ap[:, tb * 512:(tb + 1) * 512])
                    mm(k, po, brows[bname][0:1, mc * 128:(mc + 1) * 128],
                       ones_row[0:1, 0:512], start=False, stop=True,
                       ins=(brows[bname], ones_row), partial_out=True,
                       out_ap=po.ap[:, tb * 512:(tb + 1) * 512])
                for tb in range(NB):
                    k.op("vector", lambda v, po=po, tb=tb, boff=boff, mc=mc:
                         v.tensor_copy(
                             k.B.ap[:, boff + mc * S + tb * 512:
                                    boff + mc * S + (tb + 1) * 512],
                             po.ap[:, tb * 512:(tb + 1) * 512]),
                         ins=(po,), outs=(k.B,), partial=True)

        def rope_side(proj, dst_boff, bias_b, kr=False):
            for tc in range(NT):
                po = ps_sm.next()
                proj(po, tc)
                r = rtmp.next()
                k.op("vector", lambda v, r=r, po=po: v.tensor_copy(
                    r.ap, po.ap[:, 0:512]), ins=(po,), outs=(r,))
                cs = cosb.next()
                sn = sinb.next()
                k.dma("sync", cs.ap, io["ccos"][tc * 128:(tc + 1) * 128, :],
                      outs=(cs,))
                k.dma("sync", sn.ap, io["csin"][tc * 128:(tc + 1) * 128, :],
                      outs=(sn,))
                rr_ = r.ap.rearrange("p (g h) -> p g h", h=32)
                ro_ = rrot.ap.rearrange("p (g h) -> p g h", h=32)
                k.op("vector", lambda v, rr_=rr_, ro_=ro_: v.tensor_scalar_mul(
                    ro_[:, :, 0:16], rr_[:, :, 16:32], -1.0),
                    ins=(r,), outs=(rrot,))
                k.op("vector", lambda v, rr_=rr_, ro_=ro_: v.tensor_copy(
                    ro_[:, :, 16:32], rr_[:, :, 0:16]),
                    ins=(r,), outs=(rrot,), partial=True)
                k.op("vector", lambda v, r=r, cs=cs: v.tensor_mul(
                    r.ap, r.ap, cs.ap), ins=(r, cs), outs=(r,))
                k.op("vector", lambda v, sn=sn: v.tensor_mul(
                    rrot.ap, rrot.ap, sn.ap), ins=(rrot, sn), outs=(rrot,))
                k.op("vector", lambda v, r=r: v.tensor_add(
                    rope16.ap, r.ap, rrot.ap), ins=(r, rrot), outs=(rope16,))
                for g in range(4):
                    pb = pstr.next()
                    k.op("tensor", lambda t, pb=pb, g=g: t.transpose(
                        pb.ap, rope16.ap[:, g * 128:(g + 1) * 128],
                        k.r0["cident"].ap),
                        ins=(rope16, k.r0["cident"]), outs=(pb,))
                    k.op("vector", lambda v, pb=pb, g=g, tc=tc: v.tensor_copy(
                        k.B.ap[:, dst_boff + g * S + tc * 128:
                               dst_boff + g * S + (tc + 1) * 128], pb.ap),
                        ins=(pb,), outs=(k.B,), partial=True)

        for kk in range(NK):
            k.dma("scalar", wslab.ap[:, kk * 512:(kk + 1) * 512],
                  io["krb"][kk * 128:(kk + 1) * 128, :],
                  outs=(wslab,), partial=(kk > 0))

        def proj_kr(po, tc):
            for kk in range(NK):
                mm(k, po, k.A.ap[:, kk * S + tc * 128: kk * S + tc * 128 + 128],
                   wslab.ap[:, kk * 512:(kk + 1) * 512],
                   start=(kk == 0), stop=False, ins=(k.A, wslab),
                   out_ap=po.ap[:, 0:512])
            mm(k, po, k.r0["cones"][0:1, 0:128], brows["b_kr"][0:1, 0:512],
               start=False, stop=True, ins=(k.r0["cones"], brows["b_kr"]),
               partial_out=True, out_ap=po.ap[:, 0:512])

        rope_side(proj_kr, 8 * S, "b_kr")

        for ccc in range(4):
            k.dma("scalar", wslab.ap[:, ccc * 512:(ccc + 1) * 512],
                  io["qrb"][ccc * 128:(ccc + 1) * 128, :],
                  outs=(wslab,), partial=(ccc > 0))

        def proj_qr(po, tc):
            for ccc in range(4):
                mm(k, po, k.B.ap[:, 4 * S + ccc * S + tc * 128:
                                 4 * S + ccc * S + tc * 128 + 128],
                   wslab.ap[:, ccc * 512:(ccc + 1) * 512],
                   start=(ccc == 0), stop=False, ins=(k.B, wslab),
                   out_ap=po.ap[:, 0:512])
            mm(k, po, k.r0["cones"][0:1, 0:128], brows["b_qr"][0:1, 0:512],
               start=False, stop=True, ins=(k.r0["cones"], brows["b_qr"]),
               partial_out=True, out_ap=po.ap[:, 0:512])

        rope_side(proj_qr, 12 * S, "b_qr")

        for wname, bname, coff, dst in (("ukb", "b_uk", 0, "kfull"),
                                        ("uqb", "b_uq", 4 * S, "qfull")):
            for ccc in range(4):
                k.dma("scalar", wslab.ap[:, ccc * 1536:(ccc + 1) * 1536],
                      io[wname][ccc * 128:(ccc + 1) * 128, :],
                      outs=(wslab,), partial=(ccc > 0))
            for h in range(16):
                po = ps_big.next()
                kh = khst.next()
                for tb in range(NB):
                    for ccc in range(4):
                        mm(k, po, wslab.ap[:, ccc * 1536 + h * 96:
                                           ccc * 1536 + (h + 1) * 96],
                           k.B.ap[:, coff + ccc * S + tb * 512:
                                  coff + ccc * S + (tb + 1) * 512],
                           start=(ccc == 0), stop=False, ins=(wslab, k.B),
                           partial_out=(tb > 0),
                           out_ap=po.ap[0:96, tb * 512:(tb + 1) * 512])
                    mm(k, po, brows[bname][0:1, h * 96:(h + 1) * 96],
                       ones_row[0:1, 0:512], start=False, stop=True,
                       ins=(brows[bname], ones_row), partial_out=True,
                       out_ap=po.ap[0:96, tb * 512:(tb + 1) * 512])
                for tb in range(NB):
                    k.op("vector", lambda v, kh=kh, po=po, tb=tb: v.tensor_copy(
                        kh.ap[:, tb * 512:(tb + 1) * 512],
                        po.ap[0:96, tb * 512:(tb + 1) * 512]),
                        ins=(po,), outs=(kh,), partial=(tb > 0))
                k.dma("sync", io[dst][h, 0:96, :], kh.ap, ins=(kh,))

        for dst, boff in (("kfull", 8 * S), ("qfull", 12 * S)):
            for h in range(16):
                k.dma("sync", io[dst][h, 96:128, :],
                      k.B.ap[32 * (h % 4):32 * (h % 4) + 32,
                             boff + (h // 4) * S: boff + (h // 4 + 1) * S],
                      ins=(k.B,))

        for ccc in range(4):
            k.dma("scalar", wslab.ap[:, ccc * D:(ccc + 1) * D],
                  io["uvb"][ccc * 128:(ccc + 1) * 128, :],
                  outs=(wslab,), partial=(ccc > 0))
        for tc in range(NT):
            po = ps_big.next()
            vs = vstage.next()
            for db in range(NB):
                for ccc in range(4):
                    mm(k, po, k.B.ap[:, ccc * S + tc * 128: ccc * S + tc * 128 + 128],
                       wslab.ap[:, ccc * D + db * 512: ccc * D + (db + 1) * 512],
                       start=(ccc == 0), stop=False, ins=(k.B, wslab),
                       partial_out=(db > 0),
                       out_ap=po.ap[:, db * 512:(db + 1) * 512])
                mm(k, po, k.r0["cones"][0:1, 0:128],
                   brows["b_uv"][0:1, db * 512:(db + 1) * 512],
                   start=False, stop=True, ins=(k.r0["cones"], brows["b_uv"]),
                   partial_out=True, out_ap=po.ap[:, db * 512:(db + 1) * 512])
            for db in range(NB):
                k.op("vector", lambda v, vs=vs, po=po, db=db: v.tensor_copy(
                    vs.ap[:, db * 512:(db + 1) * 512],
                    po.ap[:, db * 512:(db + 1) * 512]),
                    ins=(po,), outs=(vs,), partial=(db > 0))
            k.dma("sync", io["v1tm"][tc], vs.ap, ins=(vs,))


def phase8(k, io):
    cc = k.r0["cconsts"]
    with ExitStack() as ps:
        qh = Rot([k.buf(f"p8_q{j}", [128, 512], BF16, ps) for j in range(2)])
        vh = Rot([k.buf(f"p8_v{j}", [128, NT * 128], BF16, ps) for j in range(2)])
        expst = Rot([k.buf(f"p8_e{j}", [128, NT * 512], BF16, ps) for j in range(2)])
        rbb = k.buf("p8_rbb", [128, 4 * 512], F32, ps)
        denr = k.buf("p8_den", [1, 4 * 512], F32, ps)
        ps_st = Rot([k.pbuf(f"p8_st{j}", [128, 512], F32, ps) for j in range(2)])
        ps_out = k.pbuf("p8_out", [128, D], F32, ps)
        ps_den = k.pbuf("p8_pden", [1, 512], F32, ps)
        ps_rb = k.pbuf("p8_prb", [128, 512], F32, ps)

        for t in range(NT):
            k.dma("sync", k.A.ap[:, t * S:(t + 1) * S], io["v1tm"][t],
                  outs=(k.A,), partial=(t > 0))
        for t in range(16):
            k.dma("scalar", k.B.ap[:, t * S:(t + 1) * S], io["kfull"][t],
                  outs=(k.B,), partial=(t > 0))

        state = {}

        def st_part(h, b):
            q = qh.next()
            k.dma("sync", q.ap, io["qfull"][h, :, b * 512:(b + 1) * 512], outs=(q,))
            es = expst.next()
            for j in range(NT):
                pst = ps_st.next()
                mm(k, pst, k.B.ap[:, h * S + j * 128: h * S + (j + 1) * 128],
                   q.ap, start=True, stop=True, ins=(k.B, q))
                k.op("scalar", lambda s, es=es, j=j, pst=pst: s.activation(
                    es.ap[:, j * 512:(j + 1) * 512], pst.ap, AF.Exp,
                    scale=cc.ap[:, C_SC1:C_SC1 + 1],
                    bias=cc.ap[:, C_L1SHIFT:C_L1SHIFT + 1]),
                    ins=(pst, cc), outs=(es,), partial=(j > 0))
            state[(h, b)] = es

        def av_part(h, b):
            es = state.pop((h, b))
            v = vh.bufs[h % 2]
            for j in range(NT):
                mm(k, ps_out, v[0:128, j * 128:(j + 1) * 128],
                   es.ap[:, j * 512:(j + 1) * 512],
                   start=(j == 0), stop=(j == NT - 1), ins=(v, es),
                   partial_out=(b > 0), out_ap=ps_out.ap[:, b * 512:(b + 1) * 512])
            for j in range(NT):
                mm(k, ps_den, k.r0["conescol"][0:128, 0:1],
                   es.ap[:, j * 512:(j + 1) * 512],
                   start=(j == 0), stop=(j == NT - 1), ins=(k.r0["conescol"], es))
            k.op("vector", lambda v_, b=b: v_.tensor_copy(
                denr.ap[0:1, b * 512:(b + 1) * 512], ps_den.ap),
                ins=(ps_den,), outs=(denr,), partial=(b > 0))
            k.op("vector", lambda v_, b=b: v_.reciprocal(
                denr.ap[0:1, b * 512:(b + 1) * 512],
                denr.ap[0:1, b * 512:(b + 1) * 512]),
                ins=(denr,), outs=(denr,), partial=True)
            mm(k, ps_rb, k.r0["cones32"][0:1, 0:128],
               denr.ap[0:1, b * 512:(b + 1) * 512], start=True, stop=True,
               ins=(k.r0["cones32"], denr))
            k.op("vector", lambda v_, b=b: v_.tensor_copy(
                rbb.ap[:, b * 512:(b + 1) * 512], ps_rb.ap),
                ins=(ps_rb,), outs=(rbb,), partial=(b > 0))

        for h in range(16):
            v = vh.bufs[h % 2]
            vbuf = v
            k.dma("sync", vbuf.ap.rearrange("p (a c) -> p a c", a=NT),
                  io["v1tm"][:, :, h * 128:(h + 1) * 128].rearrange("a b c -> b a c"),
                  outs=(vbuf,))
            for b in range(NB):
                st_part(h, b)
                if b >= 1:
                    av_part(h, b - 1)
            av_part(h, NB - 1)
            for b in range(NB):
                k.op("vector", lambda v_, h=h, b=b: v_.tensor_mul(
                    k.B.ap[:, h * S + b * 512: h * S + (b + 1) * 512],
                    ps_out.ap[:, b * 512:(b + 1) * 512],
                    rbb.ap[:, b * 512:(b + 1) * 512]),
                    ins=(ps_out, rbb), outs=(k.B,), partial=True)
            k.dma("sync", io["outfm"][h], k.B.ap[:, h * S:(h + 1) * S], ins=(k.B,))


def phase9(k, io):
    with ExitStack() as ps:
        oc = Rot([k.buf(f"p9_oc{j}", [128, 16 * 128], BF16, ps) for j in range(2)])
        bo = k.buf("p9_bo", [1, D], BF16, ps)
        x1b = Rot([k.buf(f"p9_x{j}", [128, 512], F32, ps) for j in range(4)])
        ost = Rot([k.buf(f"p9_o{j}", [128, 512], F32, ps) for j in range(4)])
        pso = Rot([k.pbuf(f"p9_ps{j}", [128, D], F32, ps) for j in range(2)])
        k.dma("gpsimd", bo.ap, io["L1_attn_b_o"], outs=(bo,))
        for t in range(16):
            k.dma("scalar" if (t % 2 == 0) else "sync",
                  k.B.ap[:, t * S:(t + 1) * S],
                  io["wob"][t * 128:(t + 1) * 128, :],
                  outs=(k.B,), partial=(t > 0))
        for qc in range(NT):
            o = oc.next()
            k.dma("sync", o.ap.rearrange("p (a c) -> p a c", a=16),
                  io["outfm"][:, :, qc * 128:(qc + 1) * 128].rearrange(
                      "a b c -> b a c"), outs=(o,))
            po = pso.next()
            for db in range(NB):
                for h in range(16):
                    mm(k, po, o[0:128, h * 128:(h + 1) * 128],
                       k.B.ap[:, h * S + db * 512: h * S + (db + 1) * 512],
                       start=(h == 0), stop=False, ins=(o, k.B),
                       partial_out=(db > 0),
                       out_ap=po.ap[:, db * 512:(db + 1) * 512])
                mm(k, po, k.r0["cones"][0:1, 0:128],
                   bo.ap[0:1, db * 512:(db + 1) * 512], start=False, stop=True,
                   ins=(k.r0["cones"], bo), partial_out=True,
                   out_ap=po.ap[:, db * 512:(db + 1) * 512])
            for db in range(NB):
                xb = x1b.next()
                os_ = ost.next()
                k.dma("scalar", xb.ap,
                      io["x1"][qc * 128:(qc + 1) * 128, db * 512:(db + 1) * 512],
                      outs=(xb,))
                k.op("vector", lambda v, os_=os_, po=po, db=db, xb=xb: v.tensor_add(
                    os_.ap, po.ap[:, db * 512:(db + 1) * 512], xb.ap),
                    ins=(po, xb), outs=(os_,))
                k.dma("sync", io["p2t"][qc * 128:(qc + 1) * 128,
                                        db * 512:(db + 1) * 512], os_.ap,
                      ins=(os_,))


def phase10(k, io):
    with ExitStack() as ps:
        w0t = Rot([k.vbuf(k.B.ap[:, 0:8192], "w0t0"), k.vbuf(k.B.ap[:, 8192:16384], "w0t1")])
        b0 = k.buf("p10_b0", [1, 8192], BF16, ps)
        b1 = k.buf("p10_b1", [1, D], BF16, ps)
        dst_ = Rot([k.buf(f"p10_d{j}", [128, D], BF16, ps) for j in range(2)])
        dbuf = Rot([k.vbuf(k.B.ap[:, 16384:24576], "dbuf0"), k.vbuf(k.B.ap[:, 24576:32768], "dbuf1")])
        p2b = Rot([k.buf(f"p10_p2{j}", [128, 512], F32, ps) for j in range(4)])
        ost = Rot([k.buf(f"p10_o{j}", [128, 512], F32, ps) for j in range(4)])
        ps_big = Rot([k.pbuf(f"p10_pb{j}", [128, D], F32, ps) for j in range(2)])
        k.dma("gpsimd", b0.ap, io["L1_ffn_bias0"], outs=(b0,))
        k.dma("gpsimd", b1.ap, io["L1_ffn_bias1"], outs=(b1,))
        for t in range(NK):
            k.dma("sync", k.A.ap[:, t * S:(t + 1) * S], io["xn2fm"][t],
                  outs=(k.A,), partial=(t > 0))
        for fb in range(16):
            w = w0t.next()
            for kc in range(NK):
                k.dmat("scalar", w.ap[:, kc * 512:(kc + 1) * 512],
                       io["w0b"][fb * 512:(fb + 1) * 512, kc * 128:(kc + 1) * 128],
                       outs=(w,), partial=(kc > 0))
            for f4 in range(4):
                fcg = fb * 4 + f4
                po = ps_big.next()
                d = dst_.next()
                for tb in range(NB):
                    for kc in range(NK):
                        mm(k, po, w[0:128, kc * 512 + f4 * 128: kc * 512 + f4 * 128 + 128],
                           k.A.ap[:, kc * S + tb * 512: kc * S + (tb + 1) * 512],
                           start=(kc == 0), stop=False, ins=(w, k.A),
                           partial_out=(tb > 0),
                           out_ap=po.ap[:, tb * 512:(tb + 1) * 512])
                    mm(k, po, b0.ap[0:1, fcg * 128:(fcg + 1) * 128],
                       k.r0["cones"][0:1, 0:512], start=False, stop=True,
                       ins=(b0, k.r0["cones"]), partial_out=True,
                       out_ap=po.ap[:, tb * 512:(tb + 1) * 512])
                for tb in range(NB):
                    k.op("scalar", lambda s, d=d, po=po, tb=tb: s.activation(
                        d.ap[:, tb * 512:(tb + 1) * 512],
                        po.ap[:, tb * 512:(tb + 1) * 512], AF.Gelu),
                        ins=(po,), outs=(d,), partial=(tb > 0))
                k.dma("sync", io["dfm"][fcg], d.ap, ins=(d,))
        for db in range(NB):
            for fc in range(64):
                k.dmat("scalar", k.A.ap[:, fc * 512:(fc + 1) * 512],
                       io["w1b"][db * 512:(db + 1) * 512, fc * 128:(fc + 1) * 128],
                       outs=(k.A,), partial=(fc > 0))
            for tc in range(NT):
                db_ = dbuf.next()
                k.dma("sync", db_.ap.rearrange("p (a c) -> p a c", a=64),
                      io["dfm"][:, :, tc * 128:(tc + 1) * 128].rearrange(
                          "a b c -> b a c"), outs=(db_,))
                po = ps_big.next()
                for fc in range(64):
                    mm(k, po, db_[0:128, fc * 128:(fc + 1) * 128],
                       k.A.ap[:, fc * 512:(fc + 1) * 512],
                       start=(fc == 0), stop=False, ins=(db_, k.A),
                       out_ap=po.ap[:, 0:512])
                mm(k, po, k.r0["cones"][0:1, 0:128],
                   b1.ap[0:1, db * 512:(db + 1) * 512], start=False, stop=True,
                   ins=(k.r0["cones"], b1), partial_out=True,
                   out_ap=po.ap[:, 0:512])
                p2_ = p2b.next()
                os_ = ost.next()
                k.dma("scalar", p2_.ap,
                      io["p2t"][tc * 128:(tc + 1) * 128, db * 512:(db + 1) * 512],
                      outs=(p2_,))
                k.op("vector", lambda v, os_=os_, po=po, p2_=p2_: v.tensor_add(
                    os_.ap, po.ap[:, 0:512], p2_.ap), ins=(po, p2_), outs=(os_,))
                k.dma("sync", io["out"][tc * 128:(tc + 1) * 128,
                                        db * 512:(db + 1) * 512], os_.ap,
                      ins=(os_,))


def emit_copies(k, io, phases):
    def cp(dst, src_, nsplit):
        rows = SCRATCH[dst][0][0] if len(SCRATCH[dst][0]) == 2 else None
        n = io[dst].shape[0]
        step = n // nsplit
        for j in range(nsplit):
            k.dma("gpsimd", io[dst][j * step:(j + 1) * step], src_[j * step:(j + 1) * step])
    if "p2" in phases:
        qkv = io["L0_attn_qkv"]
        for name, col in (("qb_", 0), ("kb_", 1), ("vb_", 2)):
            for j in range(4):
                k.dma("gpsimd", io[name][j * 512:(j + 1) * 512, :],
                      qkv[j * 512:(j + 1) * 512, col * D:(col + 1) * D])
    if "p3" in phases:
        cp("opb", io["L0_attn_o_proj"], 4)
    if "p5" in phases:
        cp("sgb", io["L0_ffn_shared_gate_w"], 4)
        cp("sub", io["L0_ffn_shared_up_w"], 4)
        cp("sdb", io["L0_ffn_shared_down_w"], 4)
        for ee in range(8):
            for nm, srcnm in (("gwb", "L0_ffn_gate_w"), ("uwb", "L0_ffn_up_w"),
                              ("dwb", "L0_ffn_down_w")):
                for j in range(4):
                    k.dma("gpsimd", io[nm][ee, j * 512:(j + 1) * 512, :],
                          io[srcnm][ee, j * 512:(j + 1) * 512, :])
    if "p7" in phases:
        cp("dkvb", io["L1_attn_W_dkv"], 2)
        cp("dqb", io["L1_attn_W_dq"], 2)
        cp("krb", io["L1_attn_W_kr"], 2)
        cp("ukb", io["L1_attn_W_uk"], 1)
        cp("uvb", io["L1_attn_W_uv"], 1)
        cp("uqb", io["L1_attn_W_uq"], 1)
        cp("qrb", io["L1_attn_W_qr"], 1)
        for j in range(8):
            k.dma("gpsimd", io["w0b"][j * 1024:(j + 1) * 1024, :],
                  io["L1_ffn_W0"][j * 1024:(j + 1) * 1024, :])
        for j in range(8):
            k.dma("gpsimd", io["w1b"][j * 256:(j + 1) * 256, :],
                  io["L1_ffn_W1"][j * 256:(j + 1) * 256, :])
    if "p9" in phases:
        cp("wob", io["L1_attn_W_o"], 4)


PHASES = {"p1": phase1, "p2": phase2, "p3": phase3, "p4": phase4,
          "p5": phase5, "p6": phase6, "p7": phase7, "p8": phase8,
          "p9": phase9, "p10": phase10}


def build_group(phases):
    nc = bass.Bass("TRN2", target_bir_lowering=False)
    stack = ExitStack()
    k = Ctx(nc, stack)
    io = {}

    group_inputs = set(CONST_INPUTS)
    for p in phases:
        group_inputs |= set(PHASE_INPUTS[p])
    produced = set()
    consumed = set()
    for p in phases:
        produced |= set(PHASE_SCRATCH_WRITES[p])
        consumed |= set(PHASE_SCRATCH_READS[p])
        for m_ in MUTATES.get(p, []):
            consumed.add(m_)

    in_names, out_names = [], []
    for nm in sorted(group_inputs):
        shape, dt = INPUTS[nm]
        io[nm] = nc.dram_tensor(nm, shape, dt, kind="ExternalInput").ap()
        in_names.append(nm)
    last_idx = ALL_PHASES.index(phases[-1])
    for nm, (shape, dt, prod) in SCRATCH.items():
        if prod in phases:
            later = any(nm in PHASE_SCRATCH_READS[q] or nm in MUTATES.get(q, [])
                        for q in ALL_PHASES[last_idx + 1:])
            kind = "ExternalOutput" if later else "Internal"
            io[nm] = nc.dram_tensor(nm, shape, dt, kind=kind).ap()
            if kind == "ExternalOutput":
                out_names.append(nm)
        elif nm in consumed:
            if any(nm in MUTATES.get(p, []) for p in phases):
                io[nm + "_in"] = nc.dram_tensor(nm + "_in", shape, dt,
                                                kind="ExternalInput").ap()
                in_names.append(nm + "_in")
                later = any(nm in PHASE_SCRATCH_READS[q] or nm in MUTATES.get(q, [])
                            for q in ALL_PHASES[last_idx + 1:])
                kind = "ExternalOutput" if later else "Internal"
                io[nm] = nc.dram_tensor(nm, shape, dt, kind=kind).ap()
                if kind == "ExternalOutput":
                    out_names.append(nm)
                k.mut_init = nm
            else:
                io[nm] = nc.dram_tensor(nm, shape, dt, kind="ExternalInput").ap()
                in_names.append(nm)
    if "p10" in phases:
        io["out"] = nc.dram_tensor("out", [S, D], F32, kind="ExternalOutput").ap()
        out_names.append("out")

    for nm in list(io.keys()):
        base = nm[:-3] if nm.endswith("_in") else nm
        if base in SCRATCH or nm == "out":
            k.hbufs[nm] = Buf(None, "h_" + nm)
    with stack:
        need_arena = {a for p in phases for a in PHASE_ARENAS.get(p, ())}
        if "A" in need_arena:
            k.A = Buf(nc.alloc_sbuf_tensor("arenaA", [128, NK * S], BF16)[:], "A")
            k.Abuf = k.A
        if "B" in need_arena:
            k.B = Buf(nc.alloc_sbuf_tensor("arenaB", [128, NK * S], BF16)[:], "B")
            k.Bbuf = k.B
        k.r0 = {}
        for nm, shape, dt in (("cident", [128, 128], BF16),
                              ("cident32", [128, 128], F32),
                              ("cones", [1, 512], BF16),
                              ("cones32", [1, 512], F32),
                              ("conescol", [128, 1], BF16),
                              ("cconsts", [128, 8], F32),
                              ("cmask", [128, 4 * 512], BF16)):
            k.r0[nm] = Buf(nc.alloc_sbuf_tensor("sb_" + nm, shape, dt)[:], nm)
        k.statbuf = Buf(nc.alloc_sbuf_tensor("stat", [128, 32], F32)[:], "stat")
        k.x1reg = [[Buf(None, f"x1_{i}_{b}") for b in range(NB)] for i in range(NT)]

        k.esem = {"vector": k.sem("s_dve"), "scalar": k.sem("s_act"),
                  "tensor": k.sem("s_pe"), "gpsimd": k.sem("s_gpc"),
                  "sync": k.sem("s_sy")}
        k.qsem = {"sync": k.sem("q_sync"), "scalar": k.sem("q_scalar"),
                  "gpsimd": k.sem("q_gp")}

        for nm in CONST_INPUTS:
            k.dma("sync", k.r0[nm].ap, io[nm], outs=(k.r0[nm],))
        k.op("vector", lambda v: v.tensor_copy(k.r0["cident32"].ap,
                                               k.r0["cident"].ap),
             ins=(k.r0["cident"],), outs=(k.r0["cident32"],))

        emit_copies(k, io, phases)
        for p in phases:
            k.phase_guard = [(s, s.n) for s in k.semlist if s.n > 0]
            PHASES[p](k, io)

        k.waitall("sync", [(s, s.n) for s in k.semlist])

        with nc.Block() as block:
            @block.sync
            def _(eng):
                for f in k.ops["sync"]:
                    f(eng)

            @block.vector
            def _(eng):
                for f in k.ops["vector"]:
                    f(eng)

            @block.scalar
            def _(eng):
                for f in k.ops["scalar"]:
                    f(eng)

            @block.tensor
            def _(eng):
                for f in k.ops["tensor"]:
                    f(eng)

            @block.gpsimd
            def _(eng):
                for f in k.ops["gpsimd"]:
                    f(eng)

    return nc, in_names, out_names


GROUPS = [list(ALL_PHASES)]


def make_constants():
    c = {}
    c["cident"] = np.eye(128, dtype=BF)
    c["cones"] = np.ones((1, 512), dtype=BF)
    c["cones32"] = np.ones((1, 512), dtype=np.float32)
    c["conescol"] = np.ones((128, 1), dtype=BF)
    cc = np.zeros((128, 8), np.float32)
    cc[:, C_SC0] = SC0
    cc[:, C_SC1] = SC1
    cc[:, C_L1SHIFT] = L1SHIFT
    cc[:, C_EPS] = 1e-5
    cc[:, C_NHALF] = -0.5
    cc[:, C_ONE] = 1.0
    c["cconsts"] = cc
    m = np.ones((128, 4, 512), np.float32)
    for r in range(4):
        for q in range(128):
            m[q, r, 128 * r + q + 1:] = 0.0
    c["cmask"] = np.ascontiguousarray(m.reshape(128, 4 * 512)).astype(BF)
    inv = 1.0 / (10000.0 ** (np.arange(0, 32, 2, dtype=np.float32) / 32.0))
    fr = np.arange(S, dtype=np.float32)[:, None] * inv[None, :]
    emb = np.concatenate([fr, fr], axis=1)
    c["ccos"] = np.ascontiguousarray(np.tile(np.cos(emb), (1, 16))).astype(np.float32)
    c["csin"] = np.ascontiguousarray(np.tile(np.sin(emb), (1, 16))).astype(np.float32)
    return c


def prep_inputs(core, inputs):
    d = {}
    d["x0"] = np.ascontiguousarray(np.asarray(inputs["L0_attn_x"])[core])
    for nm in ("L0_attn_qkv", "L0_attn_o_proj", "L0_ffn_router_w",
               "L0_ffn_gate_w", "L0_ffn_up_w", "L0_ffn_down_w",
               "L0_ffn_shared_gate_w", "L0_ffn_shared_up_w",
               "L0_ffn_shared_down_w", "L1_attn_W_dkv", "L1_attn_W_dq",
               "L1_attn_W_uk", "L1_attn_W_uv", "L1_attn_W_uq", "L1_attn_W_qr",
               "L1_attn_W_kr", "L1_attn_W_o", "L1_ffn_W0", "L1_ffn_W1"):
        d[nm] = np.ascontiguousarray(np.asarray(inputs[nm]))
    for nm in ("L1_attn_b_dkv", "L1_attn_b_dq", "L1_attn_b_uk", "L1_attn_b_uv",
               "L1_attn_b_uq", "L1_attn_b_qr", "L1_attn_b_kr", "L1_attn_b_o",
               "L1_ffn_bias0", "L1_ffn_bias1"):
        d[nm] = np.ascontiguousarray(np.asarray(inputs[nm])).reshape(1, -1)
    d["cgamma"] = np.ascontiguousarray(
        np.broadcast_to(np.asarray(inputs["L0_attn_gamma"])[None, :], (128, D)))
    d["cbeta"] = np.ascontiguousarray(
        np.broadcast_to(np.asarray(inputs["L0_attn_beta"])[None, :], (128, D)))
    d.update(make_constants())
    return d


def run_groups(groups, per_core, n_cores=8):
    for phases in groups:
        nc, in_names, out_names = build_group(phases)
        in_maps = []
        for c in range(n_cores):
            m_ = {}
            for nm in in_names:
                if nm.endswith("_in") and nm[:-3] in SCRATCH:
                    m_[nm] = per_core[c][nm[:-3]]
                else:
                    m_[nm] = per_core[c][nm]
            in_maps.append(m_)
        res = run_bass_kernel_spmd(nc, in_maps, core_ids=list(range(n_cores)))
        for c in range(n_cores):
            for nm in out_names:
                per_core[c][nm] = np.asarray(res.results[c][nm])
    return per_core


def kernel(**inputs):
    per_core = [prep_inputs(c, inputs) for c in range(8)]
    per_core = run_groups(GROUPS, per_core)
    return np.stack([per_core[c]["out"] for c in range(8)], axis=0)


# version 13
# speedup vs baseline: 1.1988x; 1.1889x over previous
import numpy as np
import ml_dtypes
from contextlib import ExitStack

import concourse.bass as bass
import concourse.mybir as mybir
from concourse.bass_utils import run_bass_kernel_spmd

F32 = mybir.dt.float32
BF16 = mybir.dt.bfloat16
AF = mybir.ActivationFunctionType
ALU = mybir.AluOpType
AX = mybir.AxisListType
BF = ml_dtypes.bfloat16

S = 2048
D = 2048
NT = S // 128
NK = D // 128
NB = S // 512
SC0 = 1.0 / float(np.sqrt(2048.0))
SC1 = 1.0 / float(np.sqrt(128.0))
L1SHIFT = -4.0
NEG = -1.0e30

ENGINES = ("sync", "vector", "scalar", "tensor", "gpsimd")
DMA_ENGS = ("sync", "scalar", "gpsimd")

SCRATCH = {
    "xn":     ([S, D], F32, "p1"),
    "zfm":    ([NK, 128, S], BF16, "p1"),
    "xnfm":   ([NK, 128, S], BF16, "p1"),
    "masks":  ([NT, 128, 8], F32, "p1"),
    "score":  ([1, S], F32, "p1"),
    "qfm":    ([NK, 128, S], BF16, "p2"),
    "kfm":    ([NK, 128, S], BF16, "p2"),
    "vfm":    ([NK, 128, S], BF16, "p2"),
    "vo":     ([NT, 128, D], BF16, "p3"),
    "x1":     ([S, D], F32, "p4"),
    "xn2fm":  ([NK, 128, S], BF16, "p6"),
    "kfull":  ([16, 128, S], BF16, "p7"),
    "qfull":  ([16, 128, S], BF16, "p7"),
    "v1tm":   ([NT, 128, D], BF16, "p7"),
    "outfm":  ([16, 128, S], BF16, "p8"),
    "p2t":    ([S, D], F32, "p9"),
    "w0b":    ([8192, D], BF16, "p7"),
    "w1b":    ([D, 8192], BF16, "p7"),
    "dfm":    ([16, 128, 64, 128], F32, "p10"),
    "qb_":    ([D, D], BF16, "p2"),
    "kb_":    ([D, D], BF16, "p2"),
    "vb_":    ([D, D], BF16, "p2"),
    "gwb":    ([8, D, D], BF16, "p5"),
    "uwb":    ([8, D, D], BF16, "p5"),
    "dwb":    ([8, D, D], BF16, "p5"),
    "sgb":    ([D, D], BF16, "p5"),
    "sub":    ([D, D], BF16, "p5"),
    "sdb":    ([D, D], BF16, "p5"),
    "dkvb":   ([D, 512], BF16, "p7"),
    "dqb":    ([D, 512], BF16, "p7"),
    "opb":    ([D, D], BF16, "p3"),
    "krb":    ([D, 512], BF16, "p7"),
    "ukb":    ([512, 1536], BF16, "p7"),
    "uvb":    ([512, D], BF16, "p7"),
    "uqb":    ([512, 1536], BF16, "p7"),
    "qrb":    ([512, 512], BF16, "p7"),
    "wob":    ([D, D], BF16, "p9"),
}
MUTATES = {"p5": ["x1"]}

INPUTS = {
    "x0": ([S, D], F32),
    "L0_attn_qkv": ([D, 3 * D], F32),
    "L0_attn_o_proj": ([D, D], F32),
    "cgamma": ([128, D], F32),
    "cbeta": ([128, D], F32),
    "L0_ffn_router_w": ([D, 8], F32),
    "L0_ffn_gate_w": ([8, D, D], F32),
    "L0_ffn_up_w": ([8, D, D], F32),
    "L0_ffn_down_w": ([8, D, D], F32),
    "L0_ffn_shared_gate_w": ([D, D], F32),
    "L0_ffn_shared_up_w": ([D, D], F32),
    "L0_ffn_shared_down_w": ([D, D], F32),
    "L1_attn_W_dkv": ([D, 512], F32),
    "L1_attn_b_dkv": ([1, 512], F32),
    "L1_attn_W_dq": ([D, 512], F32),
    "L1_attn_b_dq": ([1, 512], F32),
    "L1_attn_W_uk": ([512, 1536], F32),
    "L1_attn_b_uk": ([1, 1536], F32),
    "L1_attn_W_uv": ([512, D], F32),
    "L1_attn_b_uv": ([1, D], F32),
    "L1_attn_W_uq": ([512, 1536], F32),
    "L1_attn_b_uq": ([1, 1536], F32),
    "L1_attn_W_qr": ([512, 512], F32),
    "L1_attn_b_qr": ([1, 512], F32),
    "L1_attn_W_kr": ([D, 512], F32),
    "L1_attn_b_kr": ([1, 512], F32),
    "L1_attn_W_o": ([D, D], F32),
    "L1_attn_b_o": ([1, D], F32),
    "L1_ffn_W0": ([8192, D], F32),
    "L1_ffn_bias0": ([1, 8192], F32),
    "L1_ffn_W1": ([D, 8192], F32),
    "L1_ffn_bias1": ([1, D], F32),
    "cident": ([128, 128], BF16),
    "cones": ([1, 512], BF16),
    "cones32": ([1, 512], F32),
    "conescol": ([128, 1], BF16),
    "cconsts": ([128, 8], F32),
    "cmask": ([128, 4 * 512], BF16),
    "ccos": ([S, 512], F32),
    "csin": ([S, 512], F32),
}
C_SC0, C_SC1, C_L1SHIFT, C_EPS, C_NHALF, C_ONE = 0, 1, 2, 3, 4, 5

PHASE_INPUTS = {
    "p1": ["x0", "L0_ffn_router_w", "cgamma", "cbeta"],
    "p2": ["L0_attn_qkv"],
    "p3": ["L0_attn_o_proj"],
    "p4": ["x0"],
    "p5": ["L0_ffn_gate_w", "L0_ffn_up_w", "L0_ffn_down_w",
           "L0_ffn_shared_gate_w", "L0_ffn_shared_up_w", "L0_ffn_shared_down_w"],
    "p6": [],
    "p7": ["L1_attn_W_dkv", "L1_attn_b_dkv", "L1_attn_W_dq", "L1_attn_b_dq",
           "L1_attn_W_uk", "L1_attn_b_uk", "L1_attn_W_uv", "L1_attn_b_uv",
           "L1_attn_W_uq", "L1_attn_b_uq", "L1_attn_W_qr", "L1_attn_b_qr",
           "L1_attn_W_kr", "L1_attn_b_kr", "ccos", "csin",
           "L1_ffn_W0", "L1_ffn_W1"],
    "p8": [],
    "p9": ["L1_attn_W_o", "L1_attn_b_o"],
    "p10": ["L1_ffn_bias0", "L1_ffn_bias1"],
}
CONST_INPUTS = ["cident", "cones", "cones32", "conescol", "cconsts", "cmask"]
PHASE_SCRATCH_READS = {
    "p1": [],
    "p2": ["zfm"],
    "p3": ["vfm"],
    "p4": ["qfm", "kfm", "vo", "xn"],
    "p5": ["xnfm", "masks", "score"],
    "p6": ["x1"],
    "p7": ["xn2fm"],
    "p8": ["kfull", "qfull", "v1tm"],
    "p9": ["outfm", "x1"],
    "p10": ["xn2fm", "p2t", "w0b", "w1b"],
}
ALL_PHASES = ["p1", "p2", "p3", "p4", "p5", "p6", "p7", "p8", "p9", "p10"]
QUIET_PHASES = set(ALL_PHASES)
PHASE_ARENAS = {"p2": ["A"], "p4": ["A", "B"], "p5": ["A", "B"],
                "p7": ["A", "B"], "p8": ["A", "B"], "p9": ["B"], "p10": ["A", "B"]}
PHASE_SCRATCH_WRITES = {p: [n for n, v in SCRATCH.items() if v[2] == p]
                        for p in ALL_PHASES}


class Sem:
    __slots__ = ("h", "n", "name")


class Buf:
    __slots__ = ("ap", "wr", "rd", "name", "dsem")

    def __init__(self, ap, name=""):
        self.ap = ap
        self.wr = []
        self.rd = []
        self.name = name
        self.dsem = None

    def __getitem__(self, sl):
        return self.ap[sl]


class Ctx:
    def __init__(self, nc, stack):
        self.nc = nc
        self.stack = stack
        self.ops = {e: [] for e in ENGINES}
        self.waited = {e: {} for e in ENGINES}
        self.semlist = []
        self.nops = 0
        self.phase_guard = []
        self.hbufs = {}
        self.pending = {}
        self.dmapool = []
        self.dmapool_idx = 0
        self.dedicated = {}

    def sem(self, name):
        s = Sem()
        s.h = self.stack.enter_context(self.nc.semaphore(name))
        s.n = 0
        s.name = name
        self.semlist.append(s)
        return s

    def _resolve(self, eng, waits):
        ws = []
        for (s, v) in waits:
            if v <= 0:
                continue
            prev = self.waited[eng].get(id(s), 0)
            if v > prev:
                ws.append((s.h, v))
                self.waited[eng][id(s)] = v
        return ws

    def op(self, eng, f, ins=(), outs=(), partial=False, extra=(), sem=None,
           inc_by=None, quiet=False):
        if sem is None:
            sem = self.esem[eng]
            inc_by = 1 if inc_by is None else inc_by
        waits = list(extra)
        for b in ins:
            waits += b.wr
        for b in outs:
            waits += b.rd
            if not partial:
                waits += b.wr
        ws = self._resolve(eng, waits)
        self.nops += 1
        if quiet:
            self.pending.setdefault(eng, []).append(
                (tuple(ins), tuple((b, partial) for b in outs)))

            def runq(e, f=f, ws=tuple(ws)):
                for (hh, v) in ws:
                    e.wait_ge(hh, v)
                f(e)
            self.ops[eng].append(runq)
            return None
        sem.n += inc_by
        tgt = (sem, sem.n)
        for (pins, pouts) in self.pending.pop(eng, []):
            for b in pins:
                b.rd.append(tgt)
            for (b, par) in pouts:
                if par:
                    b.wr.append(tgt)
                else:
                    b.wr = [tgt]
                    b.rd = []
        for b in ins:
            b.rd.append(tgt)
            if len(b.rd) > 400:
                del b.rd[0:len(b.rd) - 400]
        for b in outs:
            if partial:
                b.wr.append(tgt)
                if len(b.wr) > 400:
                    del b.wr[0:len(b.wr) - 400]
            else:
                b.wr = [tgt]
                b.rd = []

        def run(e, f=f, ws=tuple(ws), h=sem.h, inc_by=inc_by):
            for (hh, v) in ws:
                e.wait_ge(hh, v)
            f(e).then_inc(h, inc_by)
        self.ops[eng].append(run)
        return tgt

    def _hb(self, ap):
        t = getattr(ap, "tensor", None)
        nm = getattr(t, "name", None)
        return self.hbufs.get(nm)

    def _dsem_for(self, buf, eng):
        if buf is None:
            return self.qsem[eng]
        if buf.dsem is not None:
            return buf.dsem
        nm = "d_" + (buf.name or "anon")
        if nm in self.dedicated:
            buf.dsem = self.dedicated[nm]
        elif buf.name.startswith("h_") or buf.name in ("A", "B"):
            buf.dsem = self.dedicated[nm] = self.sem(nm)
        else:
            if self.dmapool_idx >= len(self.dmapool):
                self.dmapool.append(self.sem(f"dp{len(self.dmapool)}"))
            buf.dsem = self.dmapool[self.dmapool_idx]
            self.dmapool_idx += 1
        return buf.dsem

    def dma(self, eng, out, in_, ins=(), outs=(), partial=False, extra=(), **kw):
        hi = self._hb(in_)
        ho = self._hb(out)
        if hi is not None:
            ins = tuple(ins) + (hi,)
        if ho is not None:
            outs = tuple(outs) + (ho,)
            partial = True
        tb = ho if ho is not None else (outs[0] if outs else None)
        sem = self._dsem_for(tb, eng)
        return self.op(eng, lambda e: e.dma_start(out=out, in_=in_, **kw),
                       ins=ins, outs=outs, partial=partial, extra=extra,
                       sem=sem, inc_by=16)

    def dmat(self, eng, out, in_, ins=(), outs=(), partial=False, extra=()):
        hi = self._hb(in_)
        if hi is not None:
            ins = tuple(ins) + (hi,)
        tb = outs[0] if outs else None
        sem = self._dsem_for(tb, eng)
        return self.op(eng, lambda e: e.dma_start_transpose(out, in_),
                       ins=ins, outs=outs, partial=partial, extra=extra,
                       sem=sem, inc_by=16)

    def waitall(self, eng, targets):
        ws = self._resolve(eng, targets)
        if not ws:
            return
        self.nops += 1

        def run(e, ws=tuple(ws)):
            for (hh, v) in ws:
                e.wait_ge(hh, v)
        self.ops[eng].append(run)

    def barrier(self):
        targets = [(s, s.n) for s in self.semlist if s.n > 0]
        for e in ENGINES:
            self.waitall(e, targets)

    def vbuf(self, ap, name=""):
        b = Buf(ap, name)
        b.wr = list(self.phase_guard)
        return b

    def buf(self, name, shape, dtype, stack=None):
        st = stack if stack is not None else self.stack
        h = st.enter_context(self.nc.sbuf_tensor(name, shape, dtype))
        b = Buf(h[:], name)
        b.wr = list(self.phase_guard)
        return b

    def pbuf(self, name, shape, dtype=F32, stack=None):
        st = stack if stack is not None else self.stack
        h = st.enter_context(self.nc.psum_tensor(name, shape, dtype))
        b = Buf(h[:], name)
        b.wr = list(self.phase_guard)
        return b


def mm(k, out_psum, lhsT, rhs, start, stop, ins, partial_out=False, out_ap=None):
    oap = out_psum.ap if out_ap is None else out_ap
    return k.op("tensor", lambda t: t.matmul(oap, lhsT, rhs, start=start, stop=stop),
                ins=ins, outs=(out_psum,),
                partial=(not start) or partial_out,
                quiet=(not stop) and (k.cur_phase in QUIET_PHASES))


class Rot:
    def __init__(self, bufs):
        self.bufs = bufs
        self.i = 0

    def next(self):
        b = self.bufs[self.i % len(self.bufs)]
        self.i += 1
        return b


def phase1(k, io):
    nc = k.nc
    cc = k.r0["cconsts"]
    eps = cc.ap[:, C_EPS:C_EPS + 1]
    nhalf = cc.ap[:, C_NHALF:C_NHALF + 1]
    with ExitStack() as ps:
        xb = Rot([k.buf(f"p1_x{j}", [128, D], F32, ps) for j in range(2)])
        xnb_ = Rot([k.buf(f"p1_xn{j}", [128, D], F32, ps) for j in range(2)])
        z16 = k.buf("p1_z16", [128, D], BF16, ps)
        sq = z16
        zg = z16
        hi16 = k.buf("p1_hi16", [128, D], BF16, ps)
        lo16 = k.buf("p1_lo16", [128, D], BF16, ps)
        zst = k.buf("p1_zst", [128, D], BF16, ps)
        hist = k.buf("p1_hist", [128, D], BF16, ps)
        lost = k.buf("p1_lost", [128, D], BF16, ps)
        gb = k.buf("p1_gb", [128, D], BF16, ps)
        bb = k.buf("p1_bb", [128, D], BF16, ps)
        rwhi = k.buf("p1_rwhi", [128, 8 * NK], BF16, ps)
        rwlo = k.buf("p1_rwlo", [128, 8 * NK], BF16, ps)
        logit = k.buf("p1_logit", [128, 8], F32, ps)
        m8 = k.buf("p1_m8", [128, 8], F32, ps)
        scol = k.buf("p1_scol", [128, 16], F32, ps)
        srow = k.buf("p1_srow", [16, 128], F32, ps)
        pstr = Rot([k.pbuf(f"p1_tr{j}", [128, 128], BF16, ps) for j in range(4)])
        ps_r = k.pbuf("p1_psr", [128, 8], F32, ps)
        ps_sr = k.pbuf("p1_pssr", [16, 128], F32, ps)
        st = k.statbuf

        def stat(c):
            return st.ap[:, c:c + 1]

        stats = {c: k.vbuf(st.ap[:, c:c + 1], f"st{c}") for c in range(12)}

        with ExitStack() as ips:
            g32 = xb.bufs[0]
            rw32 = k.buf("p1_rw32", [128, 8 * NK], F32, ips)
            k.dma("sync", g32.ap, io["cgamma"], outs=(g32,))
            k.op("vector", lambda v: v.tensor_copy(gb.ap, g32.ap), ins=(g32,),
                 outs=(gb,))
            k.dma("sync", g32.ap, io["cbeta"], outs=(g32,))
            k.op("vector", lambda v: v.tensor_copy(bb.ap, g32.ap), ins=(g32,),
                 outs=(bb,))
            k.dma("sync", rw32.ap.rearrange("p (a c) -> p a c", a=NK),
                  io["L0_ffn_router_w"].rearrange("(a b) c -> b a c", b=128),
                  outs=(rw32,))
            k.op("vector", lambda v: v.tensor_copy(rwhi.ap, rw32.ap), ins=(rw32,),
                 outs=(rwhi,))
            k.op("vector", lambda v: v.tensor_sub(rwlo.ap, rw32.ap, rwhi.ap),
                 ins=(rw32, rwhi), outs=(rwlo,))
            k.barrier()

        def ln_stats(buf, c_nm, c_rstd, c_b2):
            k.op("vector", lambda v: v.reduce_sum(stat(c_nm), buf.ap, AX.X,
                                                  negate=True),
                 ins=(buf,), outs=(stats[c_nm],))
            k.op("vector", lambda v: v.tensor_scalar_mul(stat(c_nm), stat(c_nm),
                                                         1.0 / D),
                 ins=(stats[c_nm],), outs=(stats[c_nm],))
            k.op("scalar", lambda s: s.activation(sq.ap, buf.ap, AF.Square,
                                                  bias=stat(c_nm),
                                                  accum_out=stat(c_rstd)),
                 ins=(buf, stats[c_nm]), outs=(sq, stats[c_rstd]))
            k.op("scalar", lambda s: s.activation(stat(c_rstd), stat(c_rstd), AF.Ln,
                                                  scale=1.0 / D, bias=eps),
                 ins=(stats[c_rstd],), outs=(stats[c_rstd],))
            k.op("scalar", lambda s: s.activation(stat(c_rstd), stat(c_rstd), AF.Exp,
                                                  scale=nhalf),
                 ins=(stats[c_rstd],), outs=(stats[c_rstd],))
            k.op("vector", lambda v: v.tensor_tensor(stat(c_b2), stat(c_nm),
                                                     stat(c_rstd), ALU.mult),
                 ins=(stats[c_nm], stats[c_rstd]), outs=(stats[c_b2],))

        def transpose_to(src, stage):
            for kk in range(NK):
                pb = pstr.next()
                k.op("tensor", lambda t, pb=pb, kk=kk: t.transpose(
                    pb.ap, src[0:128, kk * 128:(kk + 1) * 128], k.r0["cident"].ap),
                    ins=(src, k.r0["cident"]), outs=(pb,))
                eng = "vector" if (kk % 2 == 0) else "scalar"
                if eng == "vector":
                    k.op(eng, lambda v, pb=pb, kk=kk: v.tensor_copy(
                        stage[0:128, kk * 128:(kk + 1) * 128], pb.ap),
                        ins=(pb,), outs=(stage,), partial=True)
                else:
                    k.op(eng, lambda s, pb=pb, kk=kk: s.copy(
                        stage[0:128, kk * 128:(kk + 1) * 128], pb.ap),
                        ins=(pb,), outs=(stage,), partial=True)

        for i in range(NT):
            x = xb.next()
            xn = xnb_.next()
            k.dma("sync", x.ap, io["x0"][i * 128:(i + 1) * 128, :], outs=(x,))
            ln_stats(x, 0, 1, 2)
            k.op("vector", lambda v, x=x, xn=xn: v.tensor_scalar(
                xn.ap, x.ap, stat(0), stat(1), ALU.add, ALU.mult),
                ins=(x, stats[0], stats[1]), outs=(xn,))
            k.dma("sync", io["xn"][i * 128:(i + 1) * 128, :], xn.ap, ins=(xn,))
            k.op("vector", lambda v, xn=xn: v.tensor_copy(hi16.ap, xn.ap),
                 ins=(xn,), outs=(hi16,))
            k.op("vector", lambda v, xn=xn: v.tensor_sub(lo16.ap, xn.ap, hi16.ap),
                 ins=(xn, hi16), outs=(lo16,))
            ln_stats(xn, 4, 5, 6)
            k.op("vector", lambda v, xn=xn: v.tensor_scalar(
                z16.ap, xn.ap, stat(4), stat(5), ALU.add, ALU.mult),
                ins=(xn, stats[4], stats[5]), outs=(z16,))
            k.op("vector", lambda v: v.tensor_mul(zg.ap, z16.ap, gb.ap),
                 ins=(z16, gb), outs=(zg,), partial=True)
            k.op("vector", lambda v: v.tensor_add(zg.ap, zg.ap, bb.ap),
                 ins=(zg, bb), outs=(zg,), partial=True)
            transpose_to(zg, zst)
            transpose_to(hi16, hist)
            transpose_to(lo16, lost)
            k.dma("sync",
                  io["zfm"][:, :, i * 128:(i + 1) * 128].rearrange("a p c -> p a c"),
                  zst.ap.rearrange("p (a c) -> p a c", a=NK), ins=(zst,))
            k.dma("sync",
                  io["xnfm"][:, :, i * 128:(i + 1) * 128].rearrange("a p c -> p a c"),
                  hist.ap.rearrange("p (a c) -> p a c", a=NK), ins=(hist,))
            for kk in range(NK):
                mm(k, ps_r, hist[0:128, kk * 128:(kk + 1) * 128],
                   rwhi[0:128, kk * 8:(kk + 1) * 8],
                   start=(kk == 0), stop=False, ins=(hist, rwhi))
            for kk in range(NK):
                mm(k, ps_r, hist[0:128, kk * 128:(kk + 1) * 128],
                   rwlo[0:128, kk * 8:(kk + 1) * 8], start=False, stop=False,
                   ins=(hist, rwlo))
            for kk in range(NK):
                mm(k, ps_r, lost[0:128, kk * 128:(kk + 1) * 128],
                   rwhi[0:128, kk * 8:(kk + 1) * 8], start=False,
                   stop=(kk == NK - 1), ins=(lost, rwhi))
            k.op("vector", lambda v: v.tensor_copy(logit.ap, ps_r.ap),
                 ins=(ps_r,), outs=(logit,))
            k.op("vector", lambda v: v.reduce_max(stat(8), logit.ap, AX.X),
                 ins=(logit,), outs=(stats[8],))
            for ee in range(8):
                k.op("vector", lambda v, ee=ee: v.tensor_scalar(
                    m8.ap[:, ee:ee + 1], logit.ap[:, ee:ee + 1], stat(8), None,
                    ALU.is_equal), ins=(logit, stats[8]), outs=(m8,), partial=(ee > 0))
            k.dma("sync", io["masks"][i], m8.ap, ins=(m8,))
            k.op("scalar", lambda s, i=i: s.activation(
                scol.ap[:, i:i + 1], stat(8), AF.Sigmoid),
                ins=(stats[8],), outs=(scol,), partial=(i > 0))
        k.op("tensor", lambda t: t.transpose(ps_sr.ap, scol.ap, k.r0["cident32"].ap),
             ins=(scol, k.r0["cident32"]), outs=(ps_sr,))
        k.op("vector", lambda v: v.tensor_copy(srow.ap, ps_sr.ap), ins=(ps_sr,),
             outs=(srow,))
        k.dma("sync", io["score"].rearrange("o (a c) -> (o a) c", a=16), srow.ap,
              ins=(srow,))


def phase2(k, io):
    with ExitStack() as ps:
        wb = Rot([k.buf(f"p2_w{j}", [128, NK * 128], BF16, ps) for j in range(3)])
        ost = Rot([k.buf(f"p2_o{j}", [128, S], BF16, ps) for j in range(2)])
        pso = Rot([k.pbuf(f"p2_ps{j}", [128, S], F32, ps) for j in range(2)])
        zA = k.Abuf
        for t in range(NK):
            k.dma("sync", k.A.ap[:, t * S:(t + 1) * S], io["zfm"][t],
                  outs=(zA,), partial=(t > 0))
        for name, src_ in (("qfm", "qb_"), ("kfm", "kb_"), ("vfm", "vb_")):
            for mc in range(NK):
                w = wb.next()
                for hh in range(2):
                    k.dma("scalar" if (mc % 2 == 0) else "sync",
                          w.ap[:, hh * 8 * 128:(hh + 1) * 8 * 128].rearrange(
                              "p (a c) -> p a c", a=8),
                          io[src_][hh * 1024:(hh + 1) * 1024,
                                   mc * 128:(mc + 1) * 128].rearrange(
                              "(a b) c -> b a c", b=128),
                          outs=(w,), partial=(hh > 0))
                po = pso.next()
                o = ost.next()
                for tb in range(NB):
                    for kk in range(NK):
                        mm(k, po, w[0:128, kk * 128:(kk + 1) * 128],
                           k.A.ap[:, kk * S + tb * 512: kk * S + (tb + 1) * 512],
                           start=(kk == 0), stop=(kk == NK - 1),
                           ins=(w, zA), partial_out=(tb > 0),
                           out_ap=po.ap[:, tb * 512:(tb + 1) * 512])
                for tb in range(NB):
                    k.op("vector", lambda v, o=o, po=po, tb=tb: v.tensor_copy(
                        o.ap[:, tb * 512:(tb + 1) * 512],
                        po.ap[:, tb * 512:(tb + 1) * 512]),
                        ins=(po,), outs=(o,), partial=(tb > 0))
                k.dma("sync", io[name][mc], o.ap, ins=(o,))


def phase3(k, io):
    with ExitStack() as ps:
        ob = k.buf("p3_ob", [128, NK * 512], BF16, ps)
        vb = Rot([k.buf(f"p3_v{j}", [128, NK * 128], BF16, ps) for j in range(2)])
        ost = Rot([k.buf(f"p3_o{j}", [128, 512], BF16, ps) for j in range(2)])
        pso = Rot([k.pbuf(f"p3_ps{j}", [128, 512], F32, ps) for j in range(2)])
        op_ = io["opb"]
        for db in range(NB):
            for kk in range(NK):
                k.dma("scalar" if (kk % 2 == 0) else "sync",
                      ob.ap[:, kk * 512:(kk + 1) * 512],
                      op_[kk * 128:(kk + 1) * 128, db * 512:(db + 1) * 512],
                      outs=(ob,), partial=(kk > 0))
            for tc in range(NT):
                v = vb.next()
                k.dma("sync", v.ap.rearrange("p (a c) -> p a c", a=NK),
                      io["vfm"][:, :, tc * 128:(tc + 1) * 128].rearrange(
                          "a b c -> b a c"), outs=(v,))
                po = pso.next()
                o = ost.next()
                for kk in range(NK):
                    mm(k, po, v[0:128, kk * 128:(kk + 1) * 128],
                       ob.ap[:, kk * 512:(kk + 1) * 512],
                       start=(kk == 0), stop=(kk == NK - 1), ins=(v, ob))
                k.op("vector", lambda vv, o=o, po=po: vv.tensor_copy(o.ap, po.ap),
                     ins=(po,), outs=(o,))
                k.dma("sync", io["vo"][tc, :, db * 512:(db + 1) * 512], o.ap,
                      ins=(o,))




def phase4(k, io):
    cc = k.r0["cconsts"]
    with ExitStack() as ps:
        qt = Rot([k.buf(f"p4_q{j}", [128, NK * 128], BF16, ps) for j in range(2)])
        P16 = Rot([k.buf(f"p4_p{j}", [128, S], BF16, ps) for j in range(2)])
        PT = Rot([k.buf(f"p4_pt{j}", [128, NK * 128], BF16, ps) for j in range(2)])
        xbl = Rot([k.buf(f"p4_xb{j}", [128, 512], F32, ps) for j in range(4)])
        xnbl = Rot([k.buf(f"p4_xnb{j}", [128, 512], F32, ps) for j in range(4)])
        obl = Rot([k.buf(f"p4_ob{j}", [128, 512], F32, ps) for j in range(4)])
        ps_sc = k.pbuf("p4_sc", [128, S], F32, ps)
        ps_av = k.pbuf("p4_av", [128, 1024], F32, ps)
        pstr = Rot([k.pbuf(f"p4_tr{j}", [128, 128], BF16, ps) for j in range(2)])
        st = k.statbuf

        def stat(c):
            return st.ap[:, c:c + 1]

        stats = {c: k.vbuf(st.ap[:, c:c + 1], f"st{c}") for c in range(12)}

        for t in range(NT):
            k.dma("sync", k.A.ap[:, t * S:(t + 1) * S], io["vo"][t],
                  outs=(k.A,), partial=(t > 0))
        for t in range(NK):
            k.dma("scalar", k.B.ap[:, t * S:(t + 1) * S], io["kfm"][t],
                  outs=(k.B,), partial=(t > 0))

        state = {}

        def softmax_part(i):
            nblk = i // 4 + 1
            W = 512 * nblk
            q = qt.next()
            k.dma("sync", q.ap.rearrange("p (a c) -> p a c", a=NK),
                  io["qfm"][:, :, i * 128:(i + 1) * 128].rearrange("a b c -> b a c"),
                  outs=(q,))
            for tb in range(nblk):
                for kk in range(NK):
                    mm(k, ps_sc, q[0:128, kk * 128:(kk + 1) * 128],
                       k.B.ap[:, kk * S + tb * 512: kk * S + (tb + 1) * 512],
                       start=(kk == 0), stop=(kk == NK - 1), ins=(q, k.B),
                       partial_out=(tb > 0),
                       out_ap=ps_sc.ap[:, tb * 512:(tb + 1) * 512])
            cn, cb, cs = 2 * (i % 2), 2 * (i % 2) + 1, 8 + (i % 2)
            k.op("vector", lambda v: v.reduce_max(stat(cn), ps_sc.ap[:, 0:W], AX.X,
                                                  negate=True),
                 ins=(ps_sc,), outs=(stats[cn],))
            k.op("vector", lambda v: v.tensor_scalar_mul(stat(cb), stat(cn), SC0),
                 ins=(stats[cn],), outs=(stats[cb],))
            p = P16.next()
            k.op("scalar", lambda s: s.activation(
                p.ap[:, 0:W], ps_sc.ap[:, 0:W], AF.Exp,
                scale=cc.ap[:, C_SC0:C_SC0 + 1], bias=stat(cb)),
                ins=(ps_sc, stats[cb], cc), outs=(p,))
            r = i % 4
            tbd = i // 4
            k.op("vector", lambda v: v.tensor_mul(
                p.ap[:, tbd * 512:(tbd + 1) * 512],
                p.ap[:, tbd * 512:(tbd + 1) * 512],
                k.r0["cmask"].ap[:, r * 512:(r + 1) * 512]),
                ins=(p, k.r0["cmask"]), outs=(p,), partial=True)
            k.op("vector", lambda v: v.reduce_sum(stat(cs), p.ap[:, 0:W], AX.X),
                 ins=(p,), outs=(stats[cs],))
            k.op("vector", lambda v: v.reciprocal(stat(cs), stat(cs)),
                 ins=(stats[cs],), outs=(stats[cs],))
            state[i] = (p, stats[cs])

        def av_part(i):
            p, rr = state.pop(i)
            pt = PT.next()
            for j in range(i + 1):
                pb = pstr.next()
                k.op("tensor", lambda t, pb=pb, j=j: t.transpose(
                    pb.ap, p.ap[:, j * 128:(j + 1) * 128], k.r0["cident"].ap),
                    ins=(p, k.r0["cident"]), outs=(pb,))
                eng = "vector" if (j % 2 == 0) else "scalar"
                if eng == "vector":
                    k.op(eng, lambda v, pb=pb, j=j: v.tensor_copy(
                        pt.ap[:, j * 128:(j + 1) * 128], pb.ap),
                        ins=(pb,), outs=(pt,), partial=True)
                else:
                    k.op(eng, lambda s, pb=pb, j=j: s.copy(
                        pt.ap[:, j * 128:(j + 1) * 128], pb.ap),
                        ins=(pb,), outs=(pt,), partial=True)
            for half in range(2):
                for b in range(2):
                    off = half * 1024 + b * 512
                    for j in range(i + 1):
                        mm(k, ps_av, pt[0:128, j * 128:(j + 1) * 128],
                           k.A.ap[:, j * S + off: j * S + off + 512],
                           start=(j == 0), stop=(j == i), ins=(pt, k.A),
                           partial_out=(b > 0),
                           out_ap=ps_av.ap[:, b * 512:(b + 1) * 512])
                for b in range(2):
                    off = half * 1024 + b * 512
                    xb_ = xbl.next()
                    xnb_ = xnbl.next()
                    ob = obl.next()
                    k.dma("sync", xb_.ap,
                          io["x0"][i * 128:(i + 1) * 128, off:off + 512],
                          outs=(xb_,))
                    k.dma("scalar", xnb_.ap,
                          io["xn"][i * 128:(i + 1) * 128, off:off + 512],
                          outs=(xnb_,))
                    k.op("vector", lambda v, ob=ob, b=b, rr=rr, xb_=xb_:
                         v.scalar_tensor_tensor(
                             ob.ap, ps_av.ap[:, b * 512:(b + 1) * 512], rr.ap,
                             xb_.ap, ALU.mult, ALU.add),
                         ins=(ps_av, rr, xb_), outs=(ob,))
                    k.op("vector", lambda v, ob=ob, xnb_=xnb_: v.tensor_add(
                        ob.ap, ob.ap, xnb_.ap), ins=(ob, xnb_), outs=(ob,))
                    k.dma("sync", io["x1"][i * 128:(i + 1) * 128, off:off + 512],
                          ob.ap, ins=(ob,), outs=(k.x1reg[i][off // 512],))

        for i in range(NT):
            softmax_part(i)
            if i >= 1:
                av_part(i - 1)
        av_part(NT - 1)


def phase5(k, io):
    nc = k.nc
    with ExitStack() as ps:
        wg = Rot([k.buf(f"p5_wg{j}", [128, NK * 128], BF16, ps) for j in range(2)])
        wu = Rot([k.buf(f"p5_wu{j}", [128, NK * 128], BF16, ps) for j in range(2)])
        dt_ = Rot([k.buf(f"p5_dt{j}", [128, NK * 512], BF16, ps) for j in range(2)])
        gs = Rot([k.buf(f"p5_gs{j}", [128, 512], BF16, ps) for j in range(2)])
        tmp = Rot([k.buf(f"p5_tmp{j}", [128, 512], F32, ps) for j in range(3)])
        scoreb = k.buf("p5_scb", [128, S], BF16, ps)
        srowb = k.buf("p5_srow", [1, S], F32, ps)
        msk = k.buf("p5_msk", [128, NT * 8], F32, ps)
        pb = [k.pbuf(f"p5_pb{j}", [128, 512], F32, ps) for j in range(8)]

        if ("x1_in" in io):
            for q4 in range(4):
                cp = k.dma("gpsimd", io["x1"][q4 * 512:(q4 + 1) * 512, :],
                           io["x1_in"][q4 * 512:(q4 + 1) * 512, :])
                for i in range(q4 * 4, q4 * 4 + 4):
                    for b in range(NB):
                        k.x1reg[i][b].wr = [cp]

        for t in range(NK):
            k.dma("sync", k.A.ap[:, t * S:(t + 1) * S], io["xnfm"][t],
                  outs=(k.A,), partial=(t > 0))
        k.dma("sync", msk.ap.rearrange("p (a c) -> p a c", a=NT),
              io["masks"].rearrange("a p c -> p a c"), outs=(msk,))
        k.dma("sync", srowb.ap, io["score"], outs=(srowb,))

        def expert(xbuf, gate_ap, up_ap, down_ap, mask_col):
            for fc in range(NK):
                g = wg.next()
                u = wu.next()
                for hh in range(2):
                    k.dma("scalar",
                          g.ap[:, hh * 8 * 128:(hh + 1) * 8 * 128].rearrange(
                              "p (a c) -> p a c", a=8),
                          gate_ap[hh * 1024:(hh + 1) * 1024,
                                  fc * 128:(fc + 1) * 128].rearrange(
                              "(a b) c -> b a c", b=128),
                          outs=(g,), partial=(hh > 0))
                for hh in range(2):
                    k.dma("sync",
                          u.ap[:, hh * 8 * 128:(hh + 1) * 8 * 128].rearrange(
                              "p (a c) -> p a c", a=8),
                          up_ap[hh * 1024:(hh + 1) * 1024,
                                fc * 128:(fc + 1) * 128].rearrange(
                              "(a b) c -> b a c", b=128),
                          outs=(u,), partial=(hh > 0))
                for tb in range(NB):
                    for kk in range(NK):
                        mm(k, pb[tb], g[0:128, kk * 128:(kk + 1) * 128],
                           xbuf.ap[:, kk * S + tb * 512: kk * S + (tb + 1) * 512],
                           start=(kk == 0), stop=(kk == NK - 1), ins=(g, xbuf))
                for tb in range(NB):
                    for kk in range(NK):
                        mm(k, pb[4 + tb], u[0:128, kk * 128:(kk + 1) * 128],
                           xbuf.ap[:, kk * S + tb * 512: kk * S + (tb + 1) * 512],
                           start=(kk == 0), stop=(kk == NK - 1), ins=(u, xbuf))
                for tb in range(NB):
                    gg = gs.next()
                    k.op("scalar", lambda s, gg=gg, tb=tb: s.activation(
                        gg.ap, pb[tb].ap, AF.Silu), ins=(pb[tb],), outs=(gg,))
                    k.op("vector", lambda v, gg=gg, tb=tb, fc=fc: v.tensor_mul(
                        k.B.ap[:, fc * S + tb * 512: fc * S + (tb + 1) * 512],
                        gg.ap, pb[4 + tb].ap),
                        ins=(gg, pb[4 + tb]), outs=(k.B,), partial=True)
            for db in range(NB):
                d = dt_.next()
                for hh in range(2):
                    k.dma("scalar" if (db % 2 == 0) else "sync",
                          d.ap[:, hh * 8 * 512:(hh + 1) * 8 * 512].rearrange(
                              "p (a c) -> p a c", a=8),
                          down_ap[hh * 1024:(hh + 1) * 1024,
                                  db * 512:(db + 1) * 512].rearrange(
                              "(a b) c -> b a c", b=128),
                          outs=(d,), partial=(hh > 0))
                for tc in range(NT):
                    pd = pb[tc % 2]
                    for fc in range(NK):
                        mm(k, pd, k.B.ap[:, fc * S + tc * 128: fc * S + tc * 128 + 128],
                           d.ap[:, fc * 512:(fc + 1) * 512],
                           start=(fc == 0), stop=(fc == NK - 1), ins=(k.B, d))
                    t_ = tmp.next()
                    if mask_col is None:
                        k.op("scalar", lambda s, t_=t_, pd=pd: s.copy(t_.ap, pd.ap),
                             ins=(pd,), outs=(t_,))
                    else:
                        mc = Buf(msk.ap[:, mask_col(tc):mask_col(tc) + 1], "m")
                        mc.wr = msk.wr
                        k.op("vector", lambda v, t_=t_, pd=pd, mc=mc: v.tensor_scalar(
                            t_.ap, pd.ap, mc.ap, None, ALU.mult),
                            ins=(pd, msk), outs=(t_,))
                    k.dma("gpsimd", io["x1"][tc * 128:(tc + 1) * 128,
                                             db * 512:(db + 1) * 512],
                          t_.ap, ins=(t_,), outs=(k.x1reg[tc][db],),
                          accum_op=ALU.add)

        expert(k.A, io["sgb"], io["sub"], io["sdb"], None)
        for nb in range(NB):
            mm(k, pb[nb], k.r0["cones32"][0:1, 0:128],
               srowb.ap[0:1, nb * 512:(nb + 1) * 512], start=True, stop=True,
               ins=(k.r0["cones32"], srowb))
            k.op("vector", lambda v, nb=nb: v.tensor_copy(
                scoreb.ap[:, nb * 512:(nb + 1) * 512], pb[nb].ap),
                ins=(pb[nb],), outs=(scoreb,), partial=(nb > 0))
        for kk in range(NK):
            k.op("vector", lambda v, kk=kk: v.tensor_mul(
                k.A.ap[:, kk * S:(kk + 1) * S], k.A.ap[:, kk * S:(kk + 1) * S],
                scoreb.ap), ins=(k.A, scoreb), outs=(k.A,), partial=True)
        for ee in range(8):
            expert(k.A, io["gwb"][ee], io["uwb"][ee], io["dwb"][ee],
                   mask_col=lambda tc, ee=ee: tc * 8 + ee)




def phase6(k, io):
    cc = k.r0["cconsts"]
    eps = cc.ap[:, C_EPS:C_EPS + 1]
    nhalf = cc.ap[:, C_NHALF:C_NHALF + 1]
    with ExitStack() as ps:
        xb = Rot([k.buf(f"p6_x{j}", [128, D], F32, ps) for j in range(2)])
        sq = k.buf("p6_sq", [128, D], BF16, ps)
        z16 = k.buf("p6_z16", [128, D], BF16, ps)
        zst = k.buf("p6_zst", [128, D], BF16, ps)
        pstr = Rot([k.pbuf(f"p6_tr{j}", [128, 128], BF16, ps) for j in range(4)])
        st = k.statbuf

        def stat(c):
            return st.ap[:, c:c + 1]

        stats = {c: k.vbuf(st.ap[:, c:c + 1], f"st{c}") for c in range(4)}
        for i in range(NT):
            x = xb.next()
            k.dma("sync", x.ap, io["x1"][i * 128:(i + 1) * 128, :], outs=(x,))
            k.op("vector", lambda v, x=x: v.reduce_sum(stat(0), x.ap, AX.X,
                                                       negate=True),
                 ins=(x,), outs=(stats[0],))
            k.op("vector", lambda v: v.tensor_scalar_mul(stat(0), stat(0), 1.0 / D),
                 ins=(stats[0],), outs=(stats[0],))
            k.op("scalar", lambda s, x=x: s.activation(
                sq.ap, x.ap, AF.Square, bias=stat(0), accum_out=stat(1)),
                ins=(x, stats[0]), outs=(sq, stats[1]))
            k.op("scalar", lambda s: s.activation(stat(1), stat(1), AF.Ln,
                                                  scale=1.0 / D, bias=eps),
                 ins=(stats[1],), outs=(stats[1],))
            k.op("scalar", lambda s: s.activation(stat(1), stat(1), AF.Exp,
                                                  scale=nhalf),
                 ins=(stats[1],), outs=(stats[1],))
            k.op("vector", lambda v, x=x: v.tensor_scalar(
                z16.ap, x.ap, stat(0), stat(1), ALU.add, ALU.mult),
                ins=(x, stats[0], stats[1]), outs=(z16,))
            for kk in range(NK):
                pb = pstr.next()
                k.op("tensor", lambda t, pb=pb, kk=kk: t.transpose(
                    pb.ap, z16.ap[:, kk * 128:(kk + 1) * 128], k.r0["cident"].ap),
                    ins=(z16, k.r0["cident"]), outs=(pb,))
                if kk % 2 == 0:
                    k.op("vector", lambda v, pb=pb, kk=kk: v.tensor_copy(
                        zst.ap[:, kk * 128:(kk + 1) * 128], pb.ap),
                        ins=(pb,), outs=(zst,), partial=True)
                else:
                    k.op("scalar", lambda s, pb=pb, kk=kk: s.copy(
                        zst.ap[:, kk * 128:(kk + 1) * 128], pb.ap),
                        ins=(pb,), outs=(zst,), partial=True)
            k.dma("sync",
                  io["xn2fm"][:, :, i * 128:(i + 1) * 128].rearrange("a p c -> p a c"),
                  zst.ap.rearrange("p (a c) -> p a c", a=NK), ins=(zst,))


def phase7(k, io):
    cc = k.r0["cconsts"]
    with ExitStack() as ps:
        for t in range(NK):
            k.dma("sync", k.A.ap[:, t * S:(t + 1) * S], io["xn2fm"][t],
                  outs=(k.A,), partial=(t > 0))
        brows = {}
        for nm, n in (("b_dkv", 512), ("b_dq", 512), ("b_uk", 1536), ("b_uv", D),
                      ("b_uq", 1536), ("b_qr", 512), ("b_kr", 512)):
            b = k.buf("p7_" + nm, [1, n], BF16, ps)
            k.dma("gpsimd", b.ap, io["L1_attn_" + nm], outs=(b,))
            brows[nm] = b
        wcol = Rot([k.buf(f"p7_wc{j}", [128, NK * 128], BF16, ps) for j in range(2)])
        wslab = k.buf("p7_ws", [128, 8192], BF16, ps)
        vstage = Rot([k.buf(f"p7_vs{j}", [128, D], BF16, ps) for j in range(2)])
        rtmp = Rot([k.buf(f"p7_rt{j}", [128, 512], F32, ps) for j in range(2)])
        rrot = k.buf("p7_rrot", [128, 512], F32, ps)
        rope16 = k.buf("p7_rope16", [128, 512], BF16, ps)
        cosb = Rot([k.buf(f"p7_cos{j}", [128, 512], F32, ps) for j in range(2)])
        sinb = Rot([k.buf(f"p7_sin{j}", [128, 512], F32, ps) for j in range(2)])
        khst = Rot([k.buf(f"p7_kh{j}", [96, D], BF16, ps) for j in range(2)])
        ps_big = Rot([k.pbuf("p7_pb0", [128, D], F32, ps)])
        ps_sm = Rot([k.pbuf(f"p7_sm{j}", [128, 512], F32, ps) for j in range(2)])
        pstr = Rot([k.pbuf(f"p7_tr{j}", [128, 128], BF16, ps) for j in range(2)])

        ones_row = k.r0["cones"]
        for nm, wname, bname, boff in (("ckv", "dkvb", "b_dkv", 0),
                                       ("cq", "dqb", "b_dq", 4 * S)):
            for mc in range(4):
                w = wcol.next()
                for hh in range(2):
                    k.dma("scalar",
                          w.ap[:, hh * 8 * 128:(hh + 1) * 8 * 128].rearrange(
                              "p (a c) -> p a c", a=8),
                          io[wname][hh * 1024:(hh + 1) * 1024,
                                    mc * 128:(mc + 1) * 128].rearrange(
                              "(a b) c -> b a c", b=128),
                          outs=(w,), partial=(hh > 0))
                po = ps_big.next()
                for tb in range(NB):
                    for kk in range(NK):
                        mm(k, po, w[0:128, kk * 128:(kk + 1) * 128],
                           k.A.ap[:, kk * S + tb * 512: kk * S + (tb + 1) * 512],
                           start=(kk == 0), stop=False, ins=(w, k.A),
                           partial_out=(tb > 0),
                           out_ap=po.ap[:, tb * 512:(tb + 1) * 512])
                    mm(k, po, brows[bname][0:1, mc * 128:(mc + 1) * 128],
                       ones_row[0:1, 0:512], start=False, stop=True,
                       ins=(brows[bname], ones_row), partial_out=True,
                       out_ap=po.ap[:, tb * 512:(tb + 1) * 512])
                for tb in range(NB):
                    k.op("vector", lambda v, po=po, tb=tb, boff=boff, mc=mc:
                         v.tensor_copy(
                             k.B.ap[:, boff + mc * S + tb * 512:
                                    boff + mc * S + (tb + 1) * 512],
                             po.ap[:, tb * 512:(tb + 1) * 512]),
                         ins=(po,), outs=(k.B,), partial=True)

        def rope_side(proj, dst_boff, bias_b, kr=False):
            for tc in range(NT):
                po = ps_sm.next()
                proj(po, tc)
                r = rtmp.next()
                k.op("vector", lambda v, r=r, po=po: v.tensor_copy(
                    r.ap, po.ap[:, 0:512]), ins=(po,), outs=(r,))
                cs = cosb.next()
                sn = sinb.next()
                k.dma("sync", cs.ap, io["ccos"][tc * 128:(tc + 1) * 128, :],
                      outs=(cs,))
                k.dma("sync", sn.ap, io["csin"][tc * 128:(tc + 1) * 128, :],
                      outs=(sn,))
                rr_ = r.ap.rearrange("p (g h) -> p g h", h=32)
                ro_ = rrot.ap.rearrange("p (g h) -> p g h", h=32)
                k.op("vector", lambda v, rr_=rr_, ro_=ro_: v.tensor_scalar_mul(
                    ro_[:, :, 0:16], rr_[:, :, 16:32], -1.0),
                    ins=(r,), outs=(rrot,))
                k.op("vector", lambda v, rr_=rr_, ro_=ro_: v.tensor_copy(
                    ro_[:, :, 16:32], rr_[:, :, 0:16]),
                    ins=(r,), outs=(rrot,), partial=True)
                k.op("vector", lambda v, r=r, cs=cs: v.tensor_mul(
                    r.ap, r.ap, cs.ap), ins=(r, cs), outs=(r,))
                k.op("vector", lambda v, sn=sn: v.tensor_mul(
                    rrot.ap, rrot.ap, sn.ap), ins=(rrot, sn), outs=(rrot,))
                k.op("vector", lambda v, r=r: v.tensor_add(
                    rope16.ap, r.ap, rrot.ap), ins=(r, rrot), outs=(rope16,))
                for g in range(4):
                    pb = pstr.next()
                    k.op("tensor", lambda t, pb=pb, g=g: t.transpose(
                        pb.ap, rope16.ap[:, g * 128:(g + 1) * 128],
                        k.r0["cident"].ap),
                        ins=(rope16, k.r0["cident"]), outs=(pb,))
                    k.op("vector", lambda v, pb=pb, g=g, tc=tc: v.tensor_copy(
                        k.B.ap[:, dst_boff + g * S + tc * 128:
                               dst_boff + g * S + (tc + 1) * 128], pb.ap),
                        ins=(pb,), outs=(k.B,), partial=True)

        for kk in range(NK):
            k.dma("scalar", wslab.ap[:, kk * 512:(kk + 1) * 512],
                  io["krb"][kk * 128:(kk + 1) * 128, :],
                  outs=(wslab,), partial=(kk > 0))

        def proj_kr(po, tc):
            for kk in range(NK):
                mm(k, po, k.A.ap[:, kk * S + tc * 128: kk * S + tc * 128 + 128],
                   wslab.ap[:, kk * 512:(kk + 1) * 512],
                   start=(kk == 0), stop=False, ins=(k.A, wslab),
                   out_ap=po.ap[:, 0:512])
            mm(k, po, k.r0["cones"][0:1, 0:128], brows["b_kr"][0:1, 0:512],
               start=False, stop=True, ins=(k.r0["cones"], brows["b_kr"]),
               partial_out=True, out_ap=po.ap[:, 0:512])

        rope_side(proj_kr, 8 * S, "b_kr")

        for ccc in range(4):
            k.dma("scalar", wslab.ap[:, ccc * 512:(ccc + 1) * 512],
                  io["qrb"][ccc * 128:(ccc + 1) * 128, :],
                  outs=(wslab,), partial=(ccc > 0))

        def proj_qr(po, tc):
            for ccc in range(4):
                mm(k, po, k.B.ap[:, 4 * S + ccc * S + tc * 128:
                                 4 * S + ccc * S + tc * 128 + 128],
                   wslab.ap[:, ccc * 512:(ccc + 1) * 512],
                   start=(ccc == 0), stop=False, ins=(k.B, wslab),
                   out_ap=po.ap[:, 0:512])
            mm(k, po, k.r0["cones"][0:1, 0:128], brows["b_qr"][0:1, 0:512],
               start=False, stop=True, ins=(k.r0["cones"], brows["b_qr"]),
               partial_out=True, out_ap=po.ap[:, 0:512])

        rope_side(proj_qr, 12 * S, "b_qr")

        for wname, bname, coff, dst in (("ukb", "b_uk", 0, "kfull"),
                                        ("uqb", "b_uq", 4 * S, "qfull")):
            for ccc in range(4):
                k.dma("scalar", wslab.ap[:, ccc * 1536:(ccc + 1) * 1536],
                      io[wname][ccc * 128:(ccc + 1) * 128, :],
                      outs=(wslab,), partial=(ccc > 0))
            for h in range(16):
                po = ps_big.next()
                kh = khst.next()
                for tb in range(NB):
                    for ccc in range(4):
                        mm(k, po, wslab.ap[:, ccc * 1536 + h * 96:
                                           ccc * 1536 + (h + 1) * 96],
                           k.B.ap[:, coff + ccc * S + tb * 512:
                                  coff + ccc * S + (tb + 1) * 512],
                           start=(ccc == 0), stop=False, ins=(wslab, k.B),
                           partial_out=(tb > 0),
                           out_ap=po.ap[0:96, tb * 512:(tb + 1) * 512])
                    mm(k, po, brows[bname][0:1, h * 96:(h + 1) * 96],
                       ones_row[0:1, 0:512], start=False, stop=True,
                       ins=(brows[bname], ones_row), partial_out=True,
                       out_ap=po.ap[0:96, tb * 512:(tb + 1) * 512])
                for tb in range(NB):
                    k.op("vector", lambda v, kh=kh, po=po, tb=tb: v.tensor_copy(
                        kh.ap[:, tb * 512:(tb + 1) * 512],
                        po.ap[0:96, tb * 512:(tb + 1) * 512]),
                        ins=(po,), outs=(kh,), partial=(tb > 0))
                k.dma("sync", io[dst][h, 0:96, :], kh.ap, ins=(kh,))

        for dst, boff in (("kfull", 8 * S), ("qfull", 12 * S)):
            for h in range(16):
                k.dma("sync", io[dst][h, 96:128, :],
                      k.B.ap[32 * (h % 4):32 * (h % 4) + 32,
                             boff + (h // 4) * S: boff + (h // 4 + 1) * S],
                      ins=(k.B,))

        for ccc in range(4):
            k.dma("scalar", wslab.ap[:, ccc * D:(ccc + 1) * D],
                  io["uvb"][ccc * 128:(ccc + 1) * 128, :],
                  outs=(wslab,), partial=(ccc > 0))
        for tc in range(NT):
            po = ps_big.next()
            vs = vstage.next()
            for db in range(NB):
                for ccc in range(4):
                    mm(k, po, k.B.ap[:, ccc * S + tc * 128: ccc * S + tc * 128 + 128],
                       wslab.ap[:, ccc * D + db * 512: ccc * D + (db + 1) * 512],
                       start=(ccc == 0), stop=False, ins=(k.B, wslab),
                       partial_out=(db > 0),
                       out_ap=po.ap[:, db * 512:(db + 1) * 512])
                mm(k, po, k.r0["cones"][0:1, 0:128],
                   brows["b_uv"][0:1, db * 512:(db + 1) * 512],
                   start=False, stop=True, ins=(k.r0["cones"], brows["b_uv"]),
                   partial_out=True, out_ap=po.ap[:, db * 512:(db + 1) * 512])
            for db in range(NB):
                k.op("vector", lambda v, vs=vs, po=po, db=db: v.tensor_copy(
                    vs.ap[:, db * 512:(db + 1) * 512],
                    po.ap[:, db * 512:(db + 1) * 512]),
                    ins=(po,), outs=(vs,), partial=(db > 0))
            k.dma("sync", io["v1tm"][tc], vs.ap, ins=(vs,))


def phase8(k, io):
    cc = k.r0["cconsts"]
    with ExitStack() as ps:
        qh = Rot([k.buf(f"p8_q{j}", [128, 512], BF16, ps) for j in range(2)])
        vh = Rot([k.buf(f"p8_v{j}", [128, NT * 128], BF16, ps) for j in range(2)])
        expst = Rot([k.buf(f"p8_e{j}", [128, NT * 512], BF16, ps) for j in range(2)])
        rbb = k.buf("p8_rbb", [128, 4 * 512], F32, ps)
        denr = k.buf("p8_den", [1, 4 * 512], F32, ps)
        ps_st = Rot([k.pbuf(f"p8_st{j}", [128, 512], F32, ps) for j in range(2)])
        ps_out = k.pbuf("p8_out", [128, D], F32, ps)
        ps_den = k.pbuf("p8_pden", [1, 512], F32, ps)
        ps_rb = k.pbuf("p8_prb", [128, 512], F32, ps)

        for t in range(NT):
            k.dma("sync", k.A.ap[:, t * S:(t + 1) * S], io["v1tm"][t],
                  outs=(k.A,), partial=(t > 0))
        for t in range(16):
            k.dma("scalar", k.B.ap[:, t * S:(t + 1) * S], io["kfull"][t],
                  outs=(k.B,), partial=(t > 0))

        state = {}

        def st_part(h, b):
            q = qh.next()
            k.dma("sync", q.ap, io["qfull"][h, :, b * 512:(b + 1) * 512], outs=(q,))
            es = expst.next()
            for j in range(NT):
                pst = ps_st.next()
                mm(k, pst, k.B.ap[:, h * S + j * 128: h * S + (j + 1) * 128],
                   q.ap, start=True, stop=True, ins=(k.B, q))
                k.op("scalar", lambda s, es=es, j=j, pst=pst: s.activation(
                    es.ap[:, j * 512:(j + 1) * 512], pst.ap, AF.Exp,
                    scale=cc.ap[:, C_SC1:C_SC1 + 1],
                    bias=cc.ap[:, C_L1SHIFT:C_L1SHIFT + 1]),
                    ins=(pst, cc), outs=(es,), partial=(j > 0))
            state[(h, b)] = es

        def av_part(h, b):
            es = state.pop((h, b))
            v = vh.bufs[h % 2]
            for j in range(NT):
                mm(k, ps_out, v[0:128, j * 128:(j + 1) * 128],
                   es.ap[:, j * 512:(j + 1) * 512],
                   start=(j == 0), stop=(j == NT - 1), ins=(v, es),
                   partial_out=(b > 0), out_ap=ps_out.ap[:, b * 512:(b + 1) * 512])
            for j in range(NT):
                mm(k, ps_den, k.r0["conescol"][0:128, 0:1],
                   es.ap[:, j * 512:(j + 1) * 512],
                   start=(j == 0), stop=(j == NT - 1), ins=(k.r0["conescol"], es))
            k.op("vector", lambda v_, b=b: v_.tensor_copy(
                denr.ap[0:1, b * 512:(b + 1) * 512], ps_den.ap),
                ins=(ps_den,), outs=(denr,), partial=(b > 0))
            k.op("vector", lambda v_, b=b: v_.reciprocal(
                denr.ap[0:1, b * 512:(b + 1) * 512],
                denr.ap[0:1, b * 512:(b + 1) * 512]),
                ins=(denr,), outs=(denr,), partial=True)
            mm(k, ps_rb, k.r0["cones32"][0:1, 0:128],
               denr.ap[0:1, b * 512:(b + 1) * 512], start=True, stop=True,
               ins=(k.r0["cones32"], denr))
            k.op("vector", lambda v_, b=b: v_.tensor_copy(
                rbb.ap[:, b * 512:(b + 1) * 512], ps_rb.ap),
                ins=(ps_rb,), outs=(rbb,), partial=(b > 0))

        for h in range(16):
            v = vh.bufs[h % 2]
            vbuf = v
            k.dma("sync", vbuf.ap.rearrange("p (a c) -> p a c", a=NT),
                  io["v1tm"][:, :, h * 128:(h + 1) * 128].rearrange("a b c -> b a c"),
                  outs=(vbuf,))
            for b in range(NB):
                st_part(h, b)
                if b >= 1:
                    av_part(h, b - 1)
            av_part(h, NB - 1)
            for b in range(NB):
                k.op("vector", lambda v_, h=h, b=b: v_.tensor_mul(
                    k.B.ap[:, h * S + b * 512: h * S + (b + 1) * 512],
                    ps_out.ap[:, b * 512:(b + 1) * 512],
                    rbb.ap[:, b * 512:(b + 1) * 512]),
                    ins=(ps_out, rbb), outs=(k.B,), partial=True)
            k.dma("sync", io["outfm"][h], k.B.ap[:, h * S:(h + 1) * S], ins=(k.B,))


def phase9(k, io):
    with ExitStack() as ps:
        oc = Rot([k.buf(f"p9_oc{j}", [128, 16 * 128], BF16, ps) for j in range(2)])
        bo = k.buf("p9_bo", [1, D], BF16, ps)
        x1b = Rot([k.buf(f"p9_x{j}", [128, 512], F32, ps) for j in range(4)])
        ost = Rot([k.buf(f"p9_o{j}", [128, 512], F32, ps) for j in range(4)])
        pso = Rot([k.pbuf(f"p9_ps{j}", [128, D], F32, ps) for j in range(2)])
        k.dma("gpsimd", bo.ap, io["L1_attn_b_o"], outs=(bo,))
        for t in range(16):
            k.dma("scalar" if (t % 2 == 0) else "sync",
                  k.B.ap[:, t * S:(t + 1) * S],
                  io["wob"][t * 128:(t + 1) * 128, :],
                  outs=(k.B,), partial=(t > 0))
        for qc in range(NT):
            o = oc.next()
            k.dma("sync", o.ap.rearrange("p (a c) -> p a c", a=16),
                  io["outfm"][:, :, qc * 128:(qc + 1) * 128].rearrange(
                      "a b c -> b a c"), outs=(o,))
            po = pso.next()
            for db in range(NB):
                for h in range(16):
                    mm(k, po, o[0:128, h * 128:(h + 1) * 128],
                       k.B.ap[:, h * S + db * 512: h * S + (db + 1) * 512],
                       start=(h == 0), stop=False, ins=(o, k.B),
                       partial_out=(db > 0),
                       out_ap=po.ap[:, db * 512:(db + 1) * 512])
                mm(k, po, k.r0["cones"][0:1, 0:128],
                   bo.ap[0:1, db * 512:(db + 1) * 512], start=False, stop=True,
                   ins=(k.r0["cones"], bo), partial_out=True,
                   out_ap=po.ap[:, db * 512:(db + 1) * 512])
            for db in range(NB):
                xb = x1b.next()
                os_ = ost.next()
                k.dma("scalar", xb.ap,
                      io["x1"][qc * 128:(qc + 1) * 128, db * 512:(db + 1) * 512],
                      outs=(xb,))
                k.op("vector", lambda v, os_=os_, po=po, db=db, xb=xb: v.tensor_add(
                    os_.ap, po.ap[:, db * 512:(db + 1) * 512], xb.ap),
                    ins=(po, xb), outs=(os_,))
                k.dma("sync", io["p2t"][qc * 128:(qc + 1) * 128,
                                        db * 512:(db + 1) * 512], os_.ap,
                      ins=(os_,))


def phase10(k, io):
    with ExitStack() as ps:
        w0t = Rot([k.vbuf(k.B.ap[:, 0:8192], "w0t0"), k.vbuf(k.B.ap[:, 8192:16384], "w0t1")])
        b0 = k.buf("p10_b0", [1, 8192], BF16, ps)
        b1 = k.buf("p10_b1", [1, D], BF16, ps)
        dst_ = Rot([k.buf(f"p10_d{j}", [128, D], F32, ps) for j in range(2)])
        dbuf = Rot([k.vbuf(k.B.ap[:, 16384:24576], "dbuf0"), k.vbuf(k.B.ap[:, 24576:32768], "dbuf1")])
        p2b = Rot([k.buf(f"p10_p2{j}", [128, 512], F32, ps) for j in range(4)])
        ost = Rot([k.buf(f"p10_o{j}", [128, 512], F32, ps) for j in range(4)])
        ps_big = Rot([k.pbuf(f"p10_pb{j}", [128, D], F32, ps) for j in range(2)])
        k.dma("gpsimd", b0.ap, io["L1_ffn_bias0"], outs=(b0,))
        k.dma("gpsimd", b1.ap, io["L1_ffn_bias1"], outs=(b1,))
        for t in range(NK):
            k.dma("sync", k.A.ap[:, t * S:(t + 1) * S], io["xn2fm"][t],
                  outs=(k.A,), partial=(t > 0))
        for fb in range(16):
            w = w0t.next()
            for kc in range(NK):
                k.dmat("scalar", w.ap[:, kc * 512:(kc + 1) * 512],
                       io["w0b"][fb * 512:(fb + 1) * 512, kc * 128:(kc + 1) * 128],
                       outs=(w,), partial=(kc > 0))
            for f4 in range(4):
                fcg = fb * 4 + f4
                po = ps_big.next()
                d = dst_.next()
                for tb in range(NB):
                    for kc in range(NK):
                        mm(k, po, w[0:128, kc * 512 + f4 * 128: kc * 512 + f4 * 128 + 128],
                           k.A.ap[:, kc * S + tb * 512: kc * S + (tb + 1) * 512],
                           start=(kc == 0), stop=False, ins=(w, k.A),
                           partial_out=(tb > 0),
                           out_ap=po.ap[:, tb * 512:(tb + 1) * 512])
                    mm(k, po, b0.ap[0:1, fcg * 128:(fcg + 1) * 128],
                       k.r0["cones"][0:1, 0:512], start=False, stop=True,
                       ins=(b0, k.r0["cones"]), partial_out=True,
                       out_ap=po.ap[:, tb * 512:(tb + 1) * 512])
                for tb in range(NB):
                    k.op("scalar", lambda s, d=d, po=po, tb=tb: s.activation(
                        d.ap[:, tb * 512:(tb + 1) * 512],
                        po.ap[:, tb * 512:(tb + 1) * 512], AF.Gelu),
                        ins=(po,), outs=(d,), partial=(tb > 0))
                for q4 in range(4):
                    k.dma("sync" if q4 % 2 == 0 else "scalar",
                          io["dfm"][q4 * 4:(q4 + 1) * 4, :, fcg, :].rearrange(
                              "a p c -> p a c"),
                          d.ap[:, q4 * 512:(q4 + 1) * 512].rearrange(
                              "p (a c) -> p a c", a=4), ins=(d,))
        for db in range(NB):
            for fc in range(64):
                k.dmat("scalar", k.A.ap[:, fc * 512:(fc + 1) * 512],
                       io["w1b"][db * 512:(db + 1) * 512, fc * 128:(fc + 1) * 128],
                       outs=(k.A,), partial=(fc > 0))
            for tc in range(NT):
                db_ = dbuf.next()
                k.dma("gpsimd", db_.ap,
                      io["dfm"][tc].rearrange("p a c -> p (a c)"), outs=(db_,))
                po = ps_big.next()
                for fc in range(64):
                    mm(k, po, db_[0:128, fc * 128:(fc + 1) * 128],
                       k.A.ap[:, fc * 512:(fc + 1) * 512],
                       start=(fc == 0), stop=False, ins=(db_, k.A),
                       out_ap=po.ap[:, 0:512])
                mm(k, po, k.r0["cones"][0:1, 0:128],
                   b1.ap[0:1, db * 512:(db + 1) * 512], start=False, stop=True,
                   ins=(k.r0["cones"], b1), partial_out=True,
                   out_ap=po.ap[:, 0:512])
                p2_ = p2b.next()
                os_ = ost.next()
                k.dma("scalar", p2_.ap,
                      io["p2t"][tc * 128:(tc + 1) * 128, db * 512:(db + 1) * 512],
                      outs=(p2_,))
                k.op("vector", lambda v, os_=os_, po=po, p2_=p2_: v.tensor_add(
                    os_.ap, po.ap[:, 0:512], p2_.ap), ins=(po, p2_), outs=(os_,))
                k.dma("sync", io["out"][tc * 128:(tc + 1) * 128,
                                        db * 512:(db + 1) * 512], os_.ap,
                      ins=(os_,))


def emit_copies(k, io, phases):
    def cp(dst, src_, nsplit):
        rows = SCRATCH[dst][0][0] if len(SCRATCH[dst][0]) == 2 else None
        n = io[dst].shape[0]
        step = n // nsplit
        for j in range(nsplit):
            k.dma("gpsimd", io[dst][j * step:(j + 1) * step], src_[j * step:(j + 1) * step])
    if "p2" in phases:
        qkv = io["L0_attn_qkv"]
        for name, col in (("qb_", 0), ("kb_", 1), ("vb_", 2)):
            for j in range(4):
                k.dma("gpsimd", io[name][j * 512:(j + 1) * 512, :],
                      qkv[j * 512:(j + 1) * 512, col * D:(col + 1) * D])
    if "p3" in phases:
        cp("opb", io["L0_attn_o_proj"], 4)
    if "p5" in phases:
        cp("sgb", io["L0_ffn_shared_gate_w"], 4)
        cp("sub", io["L0_ffn_shared_up_w"], 4)
        cp("sdb", io["L0_ffn_shared_down_w"], 4)
        for ee in range(8):
            for nm, srcnm in (("gwb", "L0_ffn_gate_w"), ("uwb", "L0_ffn_up_w"),
                              ("dwb", "L0_ffn_down_w")):
                for j in range(4):
                    k.dma("gpsimd", io[nm][ee, j * 512:(j + 1) * 512, :],
                          io[srcnm][ee, j * 512:(j + 1) * 512, :])
    if "p7" in phases:
        cp("dkvb", io["L1_attn_W_dkv"], 2)
        cp("dqb", io["L1_attn_W_dq"], 2)
        cp("krb", io["L1_attn_W_kr"], 2)
        cp("ukb", io["L1_attn_W_uk"], 1)
        cp("uvb", io["L1_attn_W_uv"], 1)
        cp("uqb", io["L1_attn_W_uq"], 1)
        cp("qrb", io["L1_attn_W_qr"], 1)
        for j in range(8):
            k.dma("gpsimd", io["w0b"][j * 1024:(j + 1) * 1024, :],
                  io["L1_ffn_W0"][j * 1024:(j + 1) * 1024, :])
        for j in range(8):
            k.dma("gpsimd", io["w1b"][j * 256:(j + 1) * 256, :],
                  io["L1_ffn_W1"][j * 256:(j + 1) * 256, :])
    if "p9" in phases:
        cp("wob", io["L1_attn_W_o"], 4)


PHASES = {"p1": phase1, "p2": phase2, "p3": phase3, "p4": phase4,
          "p5": phase5, "p6": phase6, "p7": phase7, "p8": phase8,
          "p9": phase9, "p10": phase10}


def build_group(phases):
    nc = bass.Bass("TRN2", target_bir_lowering=False)
    stack = ExitStack()
    k = Ctx(nc, stack)
    io = {}

    group_inputs = set(CONST_INPUTS)
    for p in phases:
        group_inputs |= set(PHASE_INPUTS[p])
    produced = set()
    consumed = set()
    for p in phases:
        produced |= set(PHASE_SCRATCH_WRITES[p])
        consumed |= set(PHASE_SCRATCH_READS[p])
        for m_ in MUTATES.get(p, []):
            consumed.add(m_)

    in_names, out_names = [], []
    for nm in sorted(group_inputs):
        shape, dt = INPUTS[nm]
        io[nm] = nc.dram_tensor(nm, shape, dt, kind="ExternalInput").ap()
        in_names.append(nm)
    last_idx = ALL_PHASES.index(phases[-1])
    for nm, (shape, dt, prod) in SCRATCH.items():
        if prod in phases:
            later = any(nm in PHASE_SCRATCH_READS[q] or nm in MUTATES.get(q, [])
                        for q in ALL_PHASES[last_idx + 1:])
            kind = "ExternalOutput" if later else "Internal"
            io[nm] = nc.dram_tensor(nm, shape, dt, kind=kind).ap()
            if kind == "ExternalOutput":
                out_names.append(nm)
        elif nm in consumed:
            if any(nm in MUTATES.get(p, []) for p in phases):
                io[nm + "_in"] = nc.dram_tensor(nm + "_in", shape, dt,
                                                kind="ExternalInput").ap()
                in_names.append(nm + "_in")
                later = any(nm in PHASE_SCRATCH_READS[q] or nm in MUTATES.get(q, [])
                            for q in ALL_PHASES[last_idx + 1:])
                kind = "ExternalOutput" if later else "Internal"
                io[nm] = nc.dram_tensor(nm, shape, dt, kind=kind).ap()
                if kind == "ExternalOutput":
                    out_names.append(nm)
                k.mut_init = nm
            else:
                io[nm] = nc.dram_tensor(nm, shape, dt, kind="ExternalInput").ap()
                in_names.append(nm)
    if "p10" in phases:
        io["out"] = nc.dram_tensor("out", [S, D], F32, kind="ExternalOutput").ap()
        out_names.append("out")

    for nm in list(io.keys()):
        base = nm[:-3] if nm.endswith("_in") else nm
        if base in SCRATCH or nm == "out":
            k.hbufs[nm] = Buf(None, "h_" + nm)
    with stack:
        need_arena = {a for p in phases for a in PHASE_ARENAS.get(p, ())}
        if "A" in need_arena:
            k.A = Buf(nc.alloc_sbuf_tensor("arenaA", [128, NK * S], BF16)[:], "A")
            k.Abuf = k.A
        if "B" in need_arena:
            k.B = Buf(nc.alloc_sbuf_tensor("arenaB", [128, NK * S], BF16)[:], "B")
            k.Bbuf = k.B
        k.r0 = {}
        for nm, shape, dt in (("cident", [128, 128], BF16),
                              ("cident32", [128, 128], F32),
                              ("cones", [1, 512], BF16),
                              ("cones32", [1, 512], F32),
                              ("conescol", [128, 1], BF16),
                              ("cconsts", [128, 8], F32),
                              ("cmask", [128, 4 * 512], BF16)):
            k.r0[nm] = Buf(nc.alloc_sbuf_tensor("sb_" + nm, shape, dt)[:], nm)
        k.statbuf = Buf(nc.alloc_sbuf_tensor("stat", [128, 32], F32)[:], "stat")
        k.x1reg = [[Buf(None, f"x1_{i}_{b}") for b in range(NB)] for i in range(NT)]

        k.esem = {"vector": k.sem("s_dve"), "scalar": k.sem("s_act"),
                  "tensor": k.sem("s_pe"), "gpsimd": k.sem("s_gpc"),
                  "sync": k.sem("s_sy")}
        k.qsem = {"sync": k.sem("q_sync"), "scalar": k.sem("q_scalar"),
                  "gpsimd": k.sem("q_gp")}

        for nm in CONST_INPUTS:
            k.dma("sync", k.r0[nm].ap, io[nm], outs=(k.r0[nm],))
        k.op("vector", lambda v: v.tensor_copy(k.r0["cident32"].ap,
                                               k.r0["cident"].ap),
             ins=(k.r0["cident"],), outs=(k.r0["cident32"],))

        k.cur_phase = "p0"
        emit_copies(k, io, phases)
        for p in phases:
            k.cur_phase = p
            k.phase_guard = [(s, s.n) for s in k.semlist if s.n > 0]
            k.dmapool_idx = 0
            PHASES[p](k, io)

        k.waitall("sync", [(s, s.n) for s in k.semlist])

        with nc.Block() as block:
            @block.sync
            def _(eng):
                for f in k.ops["sync"]:
                    f(eng)

            @block.vector
            def _(eng):
                for f in k.ops["vector"]:
                    f(eng)

            @block.scalar
            def _(eng):
                for f in k.ops["scalar"]:
                    f(eng)

            @block.tensor
            def _(eng):
                for f in k.ops["tensor"]:
                    f(eng)

            @block.gpsimd
            def _(eng):
                for f in k.ops["gpsimd"]:
                    f(eng)

    return nc, in_names, out_names


GROUPS = [list(ALL_PHASES)]


def make_constants():
    c = {}
    c["cident"] = np.eye(128, dtype=BF)
    c["cones"] = np.ones((1, 512), dtype=BF)
    c["cones32"] = np.ones((1, 512), dtype=np.float32)
    c["conescol"] = np.ones((128, 1), dtype=BF)
    cc = np.zeros((128, 8), np.float32)
    cc[:, C_SC0] = SC0
    cc[:, C_SC1] = SC1
    cc[:, C_L1SHIFT] = L1SHIFT
    cc[:, C_EPS] = 1e-5
    cc[:, C_NHALF] = -0.5
    cc[:, C_ONE] = 1.0
    c["cconsts"] = cc
    m = np.ones((128, 4, 512), np.float32)
    for r in range(4):
        for q in range(128):
            m[q, r, 128 * r + q + 1:] = 0.0
    c["cmask"] = np.ascontiguousarray(m.reshape(128, 4 * 512)).astype(BF)
    inv = 1.0 / (10000.0 ** (np.arange(0, 32, 2, dtype=np.float32) / 32.0))
    fr = np.arange(S, dtype=np.float32)[:, None] * inv[None, :]
    emb = np.concatenate([fr, fr], axis=1)
    c["ccos"] = np.ascontiguousarray(np.tile(np.cos(emb), (1, 16))).astype(np.float32)
    c["csin"] = np.ascontiguousarray(np.tile(np.sin(emb), (1, 16))).astype(np.float32)
    return c


def prep_inputs(core, inputs):
    d = {}
    d["x0"] = np.ascontiguousarray(np.asarray(inputs["L0_attn_x"])[core])
    for nm in ("L0_attn_qkv", "L0_attn_o_proj", "L0_ffn_router_w",
               "L0_ffn_gate_w", "L0_ffn_up_w", "L0_ffn_down_w",
               "L0_ffn_shared_gate_w", "L0_ffn_shared_up_w",
               "L0_ffn_shared_down_w", "L1_attn_W_dkv", "L1_attn_W_dq",
               "L1_attn_W_uk", "L1_attn_W_uv", "L1_attn_W_uq", "L1_attn_W_qr",
               "L1_attn_W_kr", "L1_attn_W_o", "L1_ffn_W0", "L1_ffn_W1"):
        d[nm] = np.ascontiguousarray(np.asarray(inputs[nm]))
    for nm in ("L1_attn_b_dkv", "L1_attn_b_dq", "L1_attn_b_uk", "L1_attn_b_uv",
               "L1_attn_b_uq", "L1_attn_b_qr", "L1_attn_b_kr", "L1_attn_b_o",
               "L1_ffn_bias0", "L1_ffn_bias1"):
        d[nm] = np.ascontiguousarray(np.asarray(inputs[nm])).reshape(1, -1)
    d["cgamma"] = np.ascontiguousarray(
        np.broadcast_to(np.asarray(inputs["L0_attn_gamma"])[None, :], (128, D)))
    d["cbeta"] = np.ascontiguousarray(
        np.broadcast_to(np.asarray(inputs["L0_attn_beta"])[None, :], (128, D)))
    d.update(make_constants())
    return d


def run_groups(groups, per_core, n_cores=8):
    for phases in groups:
        nc, in_names, out_names = build_group(phases)
        in_maps = []
        for c in range(n_cores):
            m_ = {}
            for nm in in_names:
                if nm.endswith("_in") and nm[:-3] in SCRATCH:
                    m_[nm] = per_core[c][nm[:-3]]
                else:
                    m_[nm] = per_core[c][nm]
            in_maps.append(m_)
        res = run_bass_kernel_spmd(nc, in_maps, core_ids=list(range(n_cores)))
        for c in range(n_cores):
            for nm in out_names:
                per_core[c][nm] = np.asarray(res.results[c][nm])
    return per_core


def kernel(**inputs):
    per_core = [prep_inputs(c, inputs) for c in range(8)]
    per_core = run_groups(GROUPS, per_core)
    return np.stack([per_core[c]["out"] for c in range(8)], axis=0)
